# version 12
# baseline (speedup 1.0000x reference)
import math
from contextlib import ExitStack

import numpy as np
import ml_dtypes

import concourse.bass as bass
import concourse.mybir as mybir
from concourse.bass_utils import run_bass_kernel_spmd

F32 = mybir.dt.float32
BF16 = mybir.dt.bfloat16
AX = mybir.AxisListType.X
ALU = mybir.AluOpType
ACT = mybir.ActivationFunctionType

MAGIC = float(1.5 * 2**23)
NEG = -1e30

B, T, D = 4, 2048, 2048
NH, HD = 16, 128
I0 = 5632
H1 = 8192
ISQ = 1.0 / math.sqrt(HD)
INV2C = 0.5 * ISQ
KB = 1024

DEBUG_TAP = None
ENGINES = ("sync", "vector", "scalar", "tensor", "gpsimd")


CHAINED = ("vector", "scalar", "gpsimd")


class Plan:

    def __init__(self):
        self.q = {e: [] for e in ENGINES}
        self.cnt = {}
        self.last = {}
        self.phase = 0
        self.span = {}
        self.nbar = 0
        self.dmasems = set()
        self.swdge = set()

    def _touch(self, sem):
        sp = self.span.setdefault(sem, [self.phase, self.phase])
        sp[1] = self.phase

    def op(self, eng, emit, waits=(), inc=None, dma=False, chain=True):
        post = None
        waits = list(waits)
        if eng in CHAINED and chain:
            if eng in self.last:
                waits.append(self.last[eng])
            if inc is None:
                inc = ("_ch_" + eng, 1)
        if inc is not None:
            s, a = inc
            if dma:
                self.dmasems.add(s)
                prev = self.cnt.get(s, 0)
                if prev:
                    waits.append((s, prev))
            if eng == "gpsimd":
                self.swdge.add(s)
            self.cnt[s] = self.cnt.get(s, 0) + a
            post = self.cnt[s]
            self._touch(s)
            if eng in CHAINED and chain:
                self.last[eng] = (s, post)
        for ss, vv in waits:
            self._touch(ss)
        self.q[eng].append((tuple(waits), emit, inc, self.phase))
        return post

    def n(self, sem):
        return self.cnt.get(sem, 0)

    def barrier(self):
        self.nbar += 1
        tgt = 5 * self.nbar
        dw = [(s, self.n(s)) for s in sorted(self.dmasems) if self.n(s) > 0]
        self.op("sync", lambda en: en.nop(), dw, None)
        for e in ENGINES:
            self.op(e, lambda en: en.nop(), [], ("BAR", 1))
        self.phase += 1
        for e in ENGINES:
            self.op(e, lambda en: en.nop(), [("BAR", tgt)], None)

    def emit_all(self, nc, block, stack):
        persistent = [s for s, sp in self.span.items()
                      if sp[0] != sp[1] or s == "BAR" or s in self.swdge]
        by_phase = {}
        for s, sp in self.span.items():
            if s not in persistent:
                by_phase.setdefault(sp[0], []).append(s)
        nrecycle = max((len(v) for v in by_phase.values()), default=0)
        npool = len(persistent) + nrecycle
        avail = 104
        assert npool <= avail, (len(persistent), nrecycle)
        handles = {}
        base = {}
        phys = [stack.enter_context(nc.semaphore(f"S{i}")) for i in range(npool)]
        physcnt = [0] * npool
        for i, s in enumerate(persistent):
            handles[s] = phys[i]
            base[s] = 0
        for ph, names in by_phase.items():
            for j, s in enumerate(sorted(names)):
                k = len(persistent) + j
                handles[s] = phys[k]
                base[s] = physcnt[k]
                physcnt[k] += self.cnt[s]
        for eng in ENGINES:
            plan = self.q[eng]
            if not plan:
                continue

            def body(e, plan=plan):
                seen = {}
                for waits, emit, inc, ph in plan:
                    for ss, vv in waits:
                        v = vv + base[ss]
                        if vv > 0 and seen.get(ss, -1) < v:
                            e.wait_ge(handles[ss], v)
                            seen[ss] = v
                    inst = emit(e)
                    if inc is not None:
                        ss, a = inc
                        if inst is None:
                            inst = e.nop()
                        inst.then_inc(handles[ss], a)

            getattr(block, eng)(body)


class Arena:
    def __init__(self, ap):
        self.ap = ap

    def f32(self, off, *shape):
        n = int(np.prod(shape))
        return _shape(self.ap[:, off:off + 4 * n].bitcast(F32), shape)

    def bf(self, off, *shape):
        n = int(np.prod(shape))
        return _shape(self.ap[:, off:off + 2 * n].bitcast(BF16), shape)


def _shape(v, shape):
    if len(shape) <= 1:
        return v
    pat_in = "p (" + " ".join(f"d{i}" for i in range(len(shape))) + ")"
    pat_out = "p " + " ".join(f"d{i}" for i in range(len(shape)))
    kw = {f"d{i}": int(s) for i, s in enumerate(shape)}
    return v.rearrange(pat_in + " -> " + pat_out, **kw)


TAP_SHAPES = {
    "x1": [8, 128, D], "x2": [8, 128, D], "x3": [8, 128, D],
    "q0": [128, 16, 1024], "k0": [128, 4, T], "v0": [128, 16, 512],
    "enc0": [128, 16, 1024], "attn0": [8, 128, D],
}


def build(tap=None, upto=None):
    nc = bass.Bass("TRN2", target_bir_lowering=False)
    P = Plan()

    def din(name, shape):
        return nc.dram_tensor(name, shape, F32, kind="ExternalInput").ap()

    A = {}
    A["xs0"] = din("xs0", [8, 128, D])
    A["xfm"] = din("xfm", [16, 128, T])
    A["xfmo"] = din("xfmo", [16, 128, 1024])
    A["l0q"] = din("l0q", [16, 128, 16, 128])
    A["l0k"] = din("l0k", [4, 128, 16, 128])
    A["l0v"] = din("l0v", [16, 128, 512])
    A["l0o"] = din("l0o", [16, 128, 16, 128])
    A["l0w1"] = din("l0w1", [88, 128, D])
    A["l0w2"] = din("l0w2", [16, 128, I0])
    A["l1q"] = din("l1q", [16, 128, 16, 128])
    A["l1k"] = din("l1k", [16, 128, 16, 128])
    A["l1v"] = din("l1v", [16, 128, D])
    A["l1o"] = din("l1o", [16, 128, 16, 128])
    A["l1w1"] = din("l1w1", [64, 128, 16, 128])
    A["l1w3"] = din("l1w3", [64, 128, 16, 128])
    A["l1w2"] = din("l1w2", [4, 64, 128, 512])
    A["ident"] = din("ident", [128, 128])
    A["h128"] = din("h128", [128, 128])
    A["h64b"] = din("h64b", [128, 128])
    A["gqa"] = din("gqa", [4, 16])
    A["ropec_o"] = din("ropec_o", [128, 1024])
    A["ropes_o"] = din("ropes_o", [128, 1024])
    A["ropec_f"] = din("ropec_f", [128, T])
    A["ropes_f"] = din("ropes_f", [128, T])
    A["maskd"] = din("maskd", [128, 8, 2, 128])
    A["swp"] = din("swp", [128, 128])
    A["epsc"] = din("epsc", [128, 2])
    A["ssel"] = din("ssel", [128, 24, 32])

    A["out"] = nc.dram_tensor("out", [8, 128, D], F32, kind="ExternalOutput").ap()

    A["cc_k"] = nc.dram_tensor("cc_k", [16, 128, 1024], BF16)
    A["cc_v"] = nc.dram_tensor("cc_v", [8, 128, D], BF16)
    A["cc_s"] = nc.dram_tensor("cc_s", [16, 2], F32)
    A["gg_k"] = nc.dram_tensor("gg_k", [2, 16, 128, 1024], BF16)
    A["gg_v"] = nc.dram_tensor("gg_v", [2, 8, 128, D], BF16)
    A["gg_s"] = nc.dram_tensor("gg_s", [2, 16, 2], F32)
    A["xsp"] = nc.dram_tensor("xsp", [8, 128, D], F32)
    A["hsp"] = nc.dram_tensor("hsp", [44, 128, 1024], F32)

    A["tap"] = None
    A["tapname"] = tap
    if tap:
        A["tap"] = nc.dram_tensor("tap_" + tap, TAP_SHAPES[tap], F32,
                                  kind="ExternalOutput").ap()

    with ExitStack() as stack:
        AR = Arena(stack.enter_context(
            nc.sbuf_tensor("arena", [128, 207 * KB], mybir.dt.uint8)).ap())
        A["PA"] = stack.enter_context(nc.psum_tensor("PA", [128, 2048], F32)).ap()
        A["PB"] = stack.enter_context(nc.psum_tensor("PB", [128, 1024], F32)).ap()
        A["PC"] = stack.enter_context(nc.psum_tensor("PC", [128, 512], F32)).ap()
        A["PT"] = stack.enter_context(nc.psum_tensor("PT", [128, 1024], BF16)).ap()
        A["PAh"] = [A["PA"][:, 0:1024], A["PA"][:, 1024:2048]]

        block = stack.enter_context(nc.Block())
        emit_program(nc, P, AR, A, upto)
        P.emit_all(nc, block, stack)

    return nc


def emit_program(nc, P, AR, A, upto):
    PA, PB, PC, PT, PAh = A["PA"], A["PB"], A["PC"], A["PT"], A["PAh"]
    tap = A["tapname"]

    C = 184 * KB
    ident_f = AR.f32(C, 128); C += 512
    ident_b = AR.bf(C, 128); C += 256
    h128_b = AR.bf(C, 128); C += 256
    h64b_b = AR.bf(C, 128); C += 256
    onesmat_b = AR.bf(C, 128); C += 256
    onesmat_f = AR.f32(C, 128); C += 512
    gqa_f = AR.f32(C, 16); C += 64
    swp_b = AR.bf(C, 128); C += 256
    ssel_b = AR.bf(C, 24, 32); C += 1536
    sml = AR.f32(C, 128); C += 512
    scr2s = AR.f32(C, 128); C += 512
    mrow = AR.f32(C, 1024); C += 4 * KB
    scr = AR.f32(C, 1024); C += 4 * KB
    rowb = AR.f32(C, 2048); C += 8 * KB
    assert C <= 207 * KB, C
    ones_b = onesmat_b[:, 0:1]
    scrb = scr.bitcast(BF16)
    M_Q2H, M_QTOT, M_KT0, M_KT1 = 0, 16, 17, 18
    M_K2H0, M_K2H1 = 20, 24

    def load(dst, src, waits=(), q="sync", sem="ld"):
        return P.op(q, lambda e, d=dst, s=src: e.dma_start(out=d, in_=s), waits,
                    (sem, 16), dma=True)

    def vop(emit, waits=(), sem="vs", eng="vector"):
        return P.op(eng, emit, waits, (sem, 1) if sem else None)

    def mm(out, lhsT, rhs, start, stop, waits=(), sem=None):
        return P.op("tensor",
                    lambda e, o=out, l=lhsT, r=rhs, st=start, sp=stop:
                    e.matmul(o, l, r, start=st, stop=sp),
                    waits, (sem, 1) if sem else None)

    def transpose(out, in_, idn, waits=(), sem=None):
        return P.op("tensor", lambda e, o=out, i=in_, d=idn: e.transpose(o, i, d),
                    waits, (sem, 1) if sem else None)

    t1 = load(ident_f[:, :], A["ident"], sem="dc")
    t2 = load(mrow[:, 0:128], A["h128"], sem="dc")
    t3 = load(mrow[:, 128:256], A["h64b"], sem="dc")
    t4 = load(sml[0:4, 16:32], A["gqa"], sem="dc")
    t5 = load(sml[:, 48:50], A["epsc"], sem="dc")
    vop(lambda e: e.memset(onesmat_b[:, :], 1.0), [], sem=None)
    vop(lambda e: e.memset(onesmat_f[:, :], 1.0), [], sem=None)
    vop(lambda e: e.tensor_copy(ident_b[:, :], ident_f[:, :]), [("dc", t1)], sem=None)
    vop(lambda e: e.tensor_copy(h128_b[:, :], mrow[:, 0:128]), [("dc", t2)], sem=None)
    vop(lambda e: e.tensor_copy(gqa_f[0:4, :], sml[0:4, 16:32]), [("dc", t4)], sem=None)
    t6 = load(scr[:, 0:768], A["ssel"].rearrange("p v c -> p (v c)"), sem="dc")
    vop(lambda e: e.tensor_copy(ssel_b[:, :, :], _shape(scr[:, 0:768], (24, 32))),
        [("dc", t6)], sem=None)
    c_done = vop(lambda e: e.tensor_copy(h64b_b[:, :], mrow[:, 128:256]),
                 [("dc", t3), ("dc", t5)])

    xfm_b = AR.bf(0, 16, T)
    q_b = AR.bf(64 * KB, 16, 1024)
    k_b = AR.bf(96 * KB, 4, T)
    v_b = AR.bf(112 * KB, 16, 512)
    xo_b = AR.bf(128 * KB, 16, 1024)
    stg = [AR.f32(160 * KB + i * 8 * KB, 2048) for i in range(2)]
    wcol = [AR.bf(176 * KB + i * 4 * KB, 16, 128) for i in range(2)]

    for dk in range(16):
        w = [("vs", c_done)] if dk < 2 else [("xfmc", dk - 1)]
        t = load(stg[dk % 2][:, :], A["xfm"][dk], w, sem=f"ds{dk % 2}")
        vop(lambda e, dk=dk, s=stg[dk % 2]: e.tensor_copy(xfm_b[:, dk, :], s[:, :]),
            [(f"ds{dk % 2}", t)], sem="xfmc")
    for dk in range(16):
        w = [("xfmc", 16)] if dk < 2 else [("xoc", dk - 1)]
        t = load(stg[dk % 2][:, 0:1024], A["xfmo"][dk], w, sem=f"ds{dk % 2}")
        vop(lambda e, dk=dk, s=stg[dk % 2]: e.tensor_copy(xo_b[:, dk, :], s[:, 0:1024]),
            [(f"ds{dk % 2}", t)], sem="xoc")

    for ci in range(16):
        w = [("xoc", 16)] if ci < 2 else [("qw", ci - 1), ("qmm", ci - 1)]
        t = load(stg[ci % 2][:, :], A["l0q"][ci].rearrange("p a b -> p (a b)"), w,
                 sem=f"ds{ci % 2}")
        vop(lambda e, ci=ci, s=stg[ci % 2]:
            e.tensor_copy(wcol[ci % 2][:, :, :], _shape(s[:, :], (16, 128))),
            [(f"ds{ci % 2}", t)], sem="qw", eng="gpsimd")
    for ci in range(16):
        wv = wcol[ci % 2]
        half = PAh[ci % 2]
        for dk in range(16):
            for j in range(2):
                waits = []
                if dk == 0 and j == 0:
                    waits = [("qw", ci + 1)] + ([("qev", ci - 1)] if ci >= 2 else [])
                mm(half[:, j * 512:(j + 1) * 512], wv[:, dk, :],
                   xo_b[:, dk, j * 512:(j + 1) * 512], dk == 0, dk == 15,
                   waits, "qmm" if (dk == 15 and j == 1) else None)
        mmn = P.n("qmm")
        sq = vop(lambda e, half=half: e.activation(scrb[:, 0:1024], half, ACT.Square),
                 [("qmm", mmn)], sem="ss", eng="scalar")
        for j in range(2):
            mm(PB[0:30, j * 512:(j + 1) * 512], ssel_b[:, ci, 0:30],
               scrb[:, j * 512:(j + 1) * 512], ci == 0, False, [("ss", sq)],
               "qst" if (ci == 15 and j == 1) else None)
        vop(lambda e, ci=ci, half=half: e.tensor_copy(q_b[:, ci, :], half),
            [("ss", sq)], sem="qev")

    for ci in range(4):
        w = [("qev", 16), ("qmm", 16)] if ci < 2 else [("kw", ci - 1), ("kmm", ci - 1)]
        t = load(stg[ci % 2][:, :], A["l0k"][ci].rearrange("p a b -> p (a b)"), w,
                 sem=f"ds{ci % 2}")
        vop(lambda e, ci=ci, s=stg[ci % 2]:
            e.tensor_copy(wcol[ci % 2][:, :, :], _shape(s[:, :], (16, 128))),
            [(f"ds{ci % 2}", t)], sem="kw", eng="gpsimd")
    for ci in range(4):
        wv = wcol[ci % 2]
        for dk in range(16):
            for j in range(4):
                waits = []
                if dk == 0 and j == 0:
                    waits = [("kw", ci + 1)] + ([("kev", ci)] if ci >= 1 else [])
                mm(PA[:, j * 512:(j + 1) * 512], wv[:, dk, :],
                   xfm_b[:, dk, j * 512:(j + 1) * 512], dk == 0, dk == 15,
                   waits, "kmm" if (dk == 15 and j == 3) else None)
        mmn = P.n("kmm")
        sq = vop(lambda e: e.activation(scrb[:, 0:2048], PA[:, :], ACT.Square),
                 [("kmm", mmn)], sem="ss", eng="scalar")
        for j in range(4):
            v = (16 + ci) if j < 2 else (20 + ci)
            colj = j % 2
            mm(PB[0:30, colj * 512:(colj + 1) * 512], ssel_b[:, v, 0:30],
               scrb[:, j * 512:(j + 1) * 512], False, ci == 3 and j >= 2,
               [("ss", sq)], "kst" if (ci == 3 and j == 3) else None)
        vop(lambda e, ci=ci: e.tensor_copy(k_b[:, ci, :], PA[:, :]),
            [("ss", sq)], sem="kev")

    l0v_b = AR.bf(128 * KB, 16, 512)
    for dk in range(16):
        w = [("kev", 4)] if dk < 2 else [("vw", dk - 1)]
        t = load(stg[dk % 2][:, 0:512], A["l0v"][dk], w, sem=f"ds{dk % 2}")
        vop(lambda e, dk=dk, s=stg[dk % 2]: e.tensor_copy(l0v_b[:, dk, :], s[:, 0:512]),
            [(f"ds{dk % 2}", t)], sem="vw", eng="gpsimd")
    for st in range(16):
        slot = PA[:, (st % 4) * 512:(st % 4 + 1) * 512]
        for dk in range(16):
            waits = []
            if dk == 0:
                waits = [("vw", 16)] + ([("vev", st - 3)] if st >= 4 else [])
            mm(slot, xfm_b[:, dk, st * 128:(st + 1) * 128], l0v_b[:, dk, :],
               dk == 0, dk == 15, waits, "vmm" if dk == 15 else None)
        mmn = P.n("vmm")
        vop(lambda e, st=st, slot=slot: e.tensor_copy(v_b[:, st, :], slot),
            [("vmm", mmn)], sem="vev")

    stev = vop(lambda e: e.tensor_copy(mrow[0:28, 0:1024], PB[0:28, :]),
               [("qst", 1), ("kst", 1)], sem="vs")

    def rowmove(dst, src, waits, sem="dm"):
        return P.op("gpsimd", lambda e, d=dst, s=src: e.dma_start(out=d, in_=s),
                    waits, (sem, 16), dma=True)

    mv1 = rowmove(rowb[0:1, 0:1024], mrow[M_QTOT:M_QTOT + 1, 0:1024], [("vs", stev)])
    s1 = vop(lambda e: e.activation(rowb[0:1, 0:1024], rowb[0:1, 0:1024],
             ACT.Sqrt, bias=sml[0:1, 48:49], scale=1.0 / D), [("dm", mv1)],
             sem="ss", eng="scalar")
    r1 = vop(lambda e: e.reciprocal(rowb[0:1, 0:1024], rowb[0:1, 0:1024]),
             [("ss", s1)], sem="vs")

    tq1 = load(stg[0][:, 0:1024], A["ropec_o"], [("vev", 16)], sem="ds0")
    tq2 = load(stg[0][:, 1024:2048], A["ropes_o"], [("vev", 16)], sem="ds1")
    mm(PB[:, 0:512], onesmat_f[0:1, :], rowb[0:1, 0:512], True, True, [("vs", r1)])
    bq = mm(PB[:, 512:1024], onesmat_f[0:1, :], rowb[0:1, 512:1024], True, True,
            [], "mmx")
    bqn = P.n("mmx")
    rc1 = vop(lambda e: e.tensor_mul(stg[0][:, 0:1024], stg[0][:, 0:1024], PB[:, 0:1024]),
              [("ds0", tq1), ("mmx", bqn)], sem="vs")
    rc2 = vop(lambda e: e.tensor_mul(stg[0][:, 1024:2048], stg[0][:, 1024:2048], PB[:, 0:1024]),
              [("ds1", tq2), ("mmx", bqn)], sem="vs")
    rcq, rsq_t = stg[0][:, 0:1024], stg[0][:, 1024:2048]

    rsq = vop(lambda e: e.tensor_mul(rowb[0:1, 0:1024], rowb[0:1, 0:1024],
              rowb[0:1, 0:1024]), [("mmx", bqn)], sem="vs")
    b1a = mm(PB[0:16, 0:512], onesmat_f[0:1, 0:16], rowb[0:1, 0:512], True, True,
             [("vs", rsq), ("vs", stev), ("vs", rc2)])
    b1b = mm(PB[0:16, 512:1024], onesmat_f[0:1, 0:16], rowb[0:1, 512:1024], True, True,
             [], "mmx")
    mq = vop(lambda e: e.tensor_mul(scr[0:16, 0:1024], mrow[M_Q2H:M_Q2H + 16, 0:1024],
             PB[0:16, 0:1024]), [("mmx", P.n("mmx"))], sem="vs")
    mq2 = vop(lambda e: e.tensor_copy(mrow[M_Q2H:M_Q2H + 16, 0:1024], scr[0:16, 0:1024]),
              [("vs", mq)], sem="vs")
    q2m = vop(lambda e: e.nop(), [("vs", mq2)], sem="vs")

    tsw = load(scr2s[:, 0:64], A["swp"][:, 0:64], [], sem="dc")
    tsw2 = load(scr2s[:, 64:128], A["swp"][:, 64:128], [], sem="dc")
    csw = vop(lambda e: e.tensor_copy(swp_b[:, :], scr2s[:, 0:128]),
              [("dc", tsw2)], sem="vs")

    for ci in range(16):
        half = PAh[ci % 2]
        w = [("vs", rc1), ("vs", rc2), ("vs", csw), ("vs", q2m)] if ci < 2 \
            else [("qrope", ci - 1)]
        for j in range(2):
            mm(half[:, j * 512:(j + 1) * 512], swp_b[:, :],
               q_b[:, ci, j * 512:(j + 1) * 512], True, True,
               w if j == 0 else (), "mmq" if j == 1 else None)
        mmn = P.n("mmq")
        vop(lambda e, ci=ci: e.tensor_mul(scr[:, 0:1024], q_b[:, ci, :], rcq),
            [("mmq", mmn)], sem=None)
        vop(lambda e, ci=ci, half=half: e.tensor_mul(q_b[:, ci, :], half, rsq_t),
            [], sem=None)
        vop(lambda e, ci=ci: e.tensor_add(q_b[:, ci, :], scr[:, 0:1024], q_b[:, ci, :]),
            [], sem="qrope")

    mv2 = rowmove(rowb[0:1, 0:1024], mrow[M_KT0:M_KT0 + 1, 0:1024],
                  [("vs", q2m), ("vs", rc2)])
    mv3 = rowmove(rowb[0:1, 1024:2048], mrow[M_KT1:M_KT1 + 1, 0:1024], [])
    s2 = vop(lambda e: e.activation(rowb[0:1, 0:1024], rowb[0:1, 0:1024],
             ACT.Sqrt, bias=sml[0:1, 48:49], scale=1.0 / 512), [("dm", mv2)],
             sem="ss", eng="scalar")
    s3 = vop(lambda e: e.activation(rowb[0:1, 1024:2048], rowb[0:1, 1024:2048],
             ACT.Sqrt, bias=sml[0:1, 48:49], scale=1.0 / 512), [("dm", mv3)],
             sem="ss", eng="scalar")
    r2 = vop(lambda e: e.reciprocal(rowb[0:1, 0:1024], rowb[0:1, 0:1024]),
             [("ss", s2)], sem="vs")
    r3 = vop(lambda e: e.reciprocal(rowb[0:1, 1024:2048], rowb[0:1, 1024:2048]),
             [("ss", s3)], sem="vs")

    tk1 = load(stg[0][:, :], A["ropec_f"], [("qrope", 16)], sem="ds0")
    tk2 = load(stg[1][:, :], A["ropes_f"], [("qrope", 16)], sem="ds1")
    for j in range(2):
        rkr = rowb[0:1, j * 1024:(j + 1) * 1024]
        mm(PB[:, 0:512], onesmat_f[0:1, :], rkr[:, 0:512], True, True,
           [("vs", r2), ("vs", r3)] if j == 0 else [("krope_t", 2 * j)])
        mm(PB[:, 512:1024], onesmat_f[0:1, :], rkr[:, 512:1024], True, True, [], "mmx")
        bn = P.n("mmx")
        vop(lambda e, j=j: e.tensor_mul(stg[0][:, j * 1024:(j + 1) * 1024],
            stg[0][:, j * 1024:(j + 1) * 1024], PB[:, 0:1024]),
            [("ds0", tk1), ("ds1", tk2), ("mmx", bn)], sem="krope_t")
        vop(lambda e, j=j: e.tensor_mul(stg[1][:, j * 1024:(j + 1) * 1024],
            stg[1][:, j * 1024:(j + 1) * 1024], PB[:, 0:1024]),
            [("mmx", bn)], sem="krope_t")
    rck, rsk = stg[0], stg[1]

    rks = vop(lambda e: e.tensor_mul(rowb[0:1, 0:2048], rowb[0:1, 0:2048],
              rowb[0:1, 0:2048]), [("krope_t", 4)], sem="vs")
    mv4 = rowmove(mrow[64:68, 0:1024], mrow[M_K2H0:M_K2H0 + 4, 0:1024],
                  [("vs", q2m)])
    mv5 = rowmove(mrow[32:36, 0:1024], mrow[M_K2H1:M_K2H1 + 4, 0:1024],
                  [("vs", q2m)])
    mm(PB[64:68, 0:512], onesmat_f[0:1, 0:4], rowb[0:1, 0:512], True, True,
       [("vs", rks), ("krope_t", 4)])
    b2 = mm(PB[64:68, 512:1024], onesmat_f[0:1, 0:4], rowb[0:1, 512:1024], True,
            True, [], "mmx")
    mk0 = vop(lambda e: e.tensor_mul(scr[64:68, 0:1024], mrow[64:68, 0:1024],
              PB[64:68, 0:1024]), [("mmx", P.n("mmx")), ("dm", mv4)], sem="vs")
    k2a = vop(lambda e: e.reduce_max(sml[64:68, 1:2], scr[64:68, 0:1024], AX),
              [("vs", mk0)], sem="vs")
    b3 = mm(PB[32:36, 0:512], onesmat_f[0:1, 0:4], rowb[0:1, 1024:1536], True, True,
            [("vs", k2a)])
    b3b = mm(PB[32:36, 512:1024], onesmat_f[0:1, 0:4], rowb[0:1, 1536:2048], True,
             True, [], "mmx")
    mk1 = vop(lambda e: e.tensor_mul(scr[32:36, 0:1024], mrow[32:36, 0:1024],
              PB[32:36, 0:1024]), [("mmx", P.n("mmx")), ("dm", mv5)], sem="vs")
    k2b = vop(lambda e: e.reduce_max(sml[32:36, 2:3], scr[32:36, 0:1024], AX),
              [("vs", mk1)], sem="vs")
    mv6 = rowmove(sml[64:68, 2:3], sml[32:36, 2:3], [("vs", k2b)])
    k2m0 = vop(lambda e: e.tensor_max(sml[64:68, 1:2], sml[64:68, 1:2],
               sml[64:68, 2:3]), [("dm", mv6)], sem="vs")
    mv7 = rowmove(sml[0:4, 1:2], sml[64:68, 1:2], [("vs", k2m0)])
    k2m = vop(lambda e: e.nop(), [("dm", mv7)], sem="vs")
    g1 = mm(PC[0:16, 0:1], gqa_f[0:4, 0:16], sml[0:4, 1:2], True, True,
            [("vs", k2m)], "mmx")
    e1 = vop(lambda e: e.tensor_copy(sml[0:16, 3:4], PC[0:16, 0:1]),
             [("mmx", P.n("mmx"))], sem="vs")
    e3 = vop(lambda e: e.tensor_scalar(sml[0:16, 3:4], sml[0:16, 3:4], -INV2C, None,
             ALU.mult), [("vs", e1)], sem="vs")
    tr1 = transpose(PC[0:1, 0:16], sml[0:16, 3:4], ident_f[0:16, 0:16],
                    [("vs", e3)], "mmx")
    e4 = vop(lambda e: e.tensor_copy(sml[0:1, 4:20], PC[0:1, 0:16]),
             [("mmx", P.n("mmx"))], sem="vs")
    g2 = mm(PC[0:128, 0:16], onesmat_f[0:1, :], sml[0:1, 4:20], True, True,
            [("vs", e4)], "mmx")
    Mb = sml[:, 32:48]
    e5 = vop(lambda e: e.tensor_copy(sml[:, 32:48], PC[:, 0:16]),
             [("mmx", P.n("mmx"))], sem="vs")

    for ci in range(4):
        w = [("krope_t", 4)] if ci == 0 else [("krope", 2 * ci)]
        for j in range(4):
            mm(PA[:, j * 512:(j + 1) * 512], swp_b[:, :],
               k_b[:, ci, j * 512:(j + 1) * 512], True, True,
               w if j == 0 else (), "mmq" if j == 3 else None)
        mmn = P.n("mmq")
        for j in range(2):
            vop(lambda e, ci=ci, j=j: e.tensor_mul(scr[:, 0:1024],
                k_b[:, ci, j * 1024:(j + 1) * 1024], rck[:, j * 1024:(j + 1) * 1024]),
                [("mmq", mmn)] if j == 0 else [], sem=None)
            vop(lambda e, ci=ci, j=j: e.tensor_mul(k_b[:, ci, j * 1024:(j + 1) * 1024],
                PA[:, j * 1024:(j + 1) * 1024], rsk[:, j * 1024:(j + 1) * 1024]),
                [], sem=None)
            vop(lambda e, ci=ci, j=j: e.tensor_add(k_b[:, ci, j * 1024:(j + 1) * 1024],
                scr[:, 0:1024], k_b[:, ci, j * 1024:(j + 1) * 1024]), [], sem="krope")

    if tap in ("q0", "k0", "v0"):
        src = {"q0": q_b, "k0": k_b, "v0": v_b}[tap]
        w = {"q0": ("qrope", 16), "k0": ("krope", 8), "v0": ("vev", 16)}[tap]
        nch, width = TAP_SHAPES[tap][1], TAP_SHAPES[tap][2]
        for ci in range(nch):
            for j0 in range(0, width, 1024):
                jw = min(1024, width - j0)
                vop(lambda e, ci=ci, j0=j0, jw=jw, src=src:
                    e.tensor_copy(mrow[:, 0:jw], src[:, ci, j0:j0 + jw]), [w], sem="tapv")
                P.op("scalar", lambda e, ci=ci, j0=j0, jw=jw:
                     e.dma_start(out=A["tap"][:, ci, j0:j0 + jw], in_=mrow[:, 0:jw]),
                     [("tapv", P.n("tapv"))], ("st", 16), dma=True)
        P.op("sync", lambda e: e.nop(), [("st", P.n("st"))], ("fin", 1))
        return

    pt_b = AR.bf(128 * KB, 16, 1024)
    p_sb = AR.bf(120 * KB, 2048)
    enc_b = AR.bf(0, 16, 1024)
    negm_t = AR.f32(160 * KB, 8, 16)
    p_sb = AR.bf(162 * KB, 2048)
    tr = transpose
    for qt in range(8):
        tr(PC[:, 0:16], mrow[M_Q2H:M_Q2H + 16, qt * 128:(qt + 1) * 128],
           ident_f[0:16, 0:16], [("vs", e5)] if qt == 0 else [("nmt", qt)], "mmx")
        vop(lambda e, qt=qt: e.scalar_tensor_tensor(negm_t[:, qt, :],
            PC[:, 0:16], -INV2C, Mb[:, 0:16], ALU.mult, ALU.add),
            [("mmx", P.n("mmx"))], sem="nmt")
    nmtn = P.n("nmt")

    for qh in range(16):
        kv = qh // 4
        base = P.n("exp0")
        for qt in range(8):
            for j in range(4):
                w = []
                if j == 0:
                    if qh == 0 and qt == 0:
                        w = [("krope", 8), ("qrope", 16), ("nmt", nmtn)]
                    else:
                        w = [("ptr", P.n("ptr"))]
                    if qt == 0 and qh:
                        w.append(("bcd", qh))
                mm(PA[:, j * 512:(j + 1) * 512], q_b[:, qh, qt * 128:(qt + 1) * 128],
                   k_b[:, kv, j * 512:(j + 1) * 512], True, True, w,
                   "smm" if j == 3 else None)
            smm = P.n("smm")
            ew = [("smm", smm)] + ([("avmm", qh)] if (qh and qt == 0) else [])
            vop(lambda e, qh=qh, qt=qt: e.activation(p_sb[:, :], PA[:, :], ACT.Exp,
                bias=negm_t[:, qt, qh:qh + 1], scale=ISQ,
                accum_out=sml[:, 50 + qt:51 + qt]),
                ew, sem="exp0", eng="scalar")
            expn = P.n("exp0")
            for w8 in range(2):
                for i in range(8):
                    st = w8 * 8 + i
                    tr(PT[:, i * 128:(i + 1) * 128], p_sb[:, st * 128:(st + 1) * 128],
                       ident_b[:, :],
                       [("exp0", expn), ("ptw", P.n("ptw"))] if i == 0 else [],
                       "ptr" if i == 7 else None)
                trn = P.n("ptr")
                pw = [("ptr", trn)] + ([("avmm", qh)] if (qt == 0 and w8 == 0 and qh)
                                       else [])
                vop(lambda e, qt=qt, w8=w8: e.tensor_copy(
                    pt_b[:, w8 * 8:(w8 + 1) * 8, qt * 128:(qt + 1) * 128],
                    _shape(PT[:, 0:1024], (8, 128))), pw, sem="ptw")
        for qt in range(8):
            tr(PC[0:1, 0:128], sml[:, 50 + qt:51 + qt], ident_f[:, :],
               [("ptw", P.n("ptw"))] if qt == 0 else [("rcd", P.n("rcd"))], "mmx")
            vop(lambda e, qt=qt: e.tensor_copy(rowb[0:1, qt * 128:(qt + 1) * 128],
                PC[0:1, 0:128]), [("mmx", P.n("mmx"))], sem="rcd")
        rc2_ = vop(lambda e: e.reciprocal(rowb[0:1, 0:1024], rowb[0:1, 0:1024]),
                   [], sem="rcd")
        mm(PAh[0][:, 0:512], onesmat_f[0:1, :], rowb[0:1, 0:512], True, True,
           [("rcd", P.n("rcd"))])
        bc = mm(PAh[0][:, 512:1024], onesmat_f[0:1, :], rowb[0:1, 512:1024],
                True, True, [], "mmx")
        bcn = P.n("mmx")
        ev = vop(lambda e: e.tensor_copy(scr[:, 0:1024], PAh[0]), [("mmx", bcn)],
                 sem="bcd")
        for st in range(16):
            for j in range(2):
                aw = []
                if st == 0 and j == 0:
                    aw = [("encd", qh)] if qh else []
                mm(PB[:, j * 512:(j + 1) * 512], v_b[:, st, kv * 128:(kv + 1) * 128],
                   pt_b[:, st, j * 512:(j + 1) * 512], st == 0, st == 15,
                   aw, "avmm" if (st == 15 and j == 1) else None)
        avn = P.n("avmm")
        vop(lambda e, qh=qh: e.tensor_mul(enc_b[:, qh, :], PB[:, 0:1024], scr[:, 0:1024]),
            [("bcd", P.n("bcd")), ("avmm", avn)], sem="encd")

    if tap == "enc0":
        for ci in range(16):
            vop(lambda e, ci=ci: e.tensor_copy(mrow[:, 0:1024], enc_b[:, ci, :]),
                [("encd", 16)], sem="tapv")
            P.op("scalar", lambda e, ci=ci: e.dma_start(out=A["tap"][:, ci, :],
                 in_=mrow[:, 0:1024]), [("tapv", P.n("tapv"))], ("st", 16), dma=True)
        P.op("sync", lambda e: e.nop(), [("st", P.n("st"))], ("fin", 1))
        return

    stream = AR.f32(32 * KB, 8, 2048)
    for tt in range(8):
        t = load(stream[:, tt, :], A["xs0"][tt], [("encd", 16)] if tt == 0 else [],
                 sem="dx")
    xldn = P.n("dx")

    emit_proj_ln(nc, P, AR, A, dict(
        wsrc="l0o", enc=enc_b, stream=stream, stg=stg, wcol=wcol,
        ident_f=ident_f, mrow=mrow, scr=scr, scr2s=scr2s, sml=sml,
        pre_wait=[("encd", 16)], stt_extra=[("dx", xldn)],
        sem_prefix="o0", nchunks=16), upto_tap=(tap == "x1" or tap == "attn0"),
        tapname=tap)

    if tap in ("x1", "attn0"):
        for tt in range(8):
            P.op("scalar", lambda e, tt=tt: e.dma_start(out=A["tap"][tt],
                 in_=stream[:, tt, :]), [("o0fin", P.n("o0fin"))], ("st", 16), dma=True)
        P.op("sync", lambda e: e.nop(), [("st", P.n("st"))], ("fin", 1))
        return

    P.barrier()
    env = dict(ident_f=ident_f, ident_b=ident_b, onesmat_f=onesmat_f,
               onesmat_b=onesmat_b, ones_b=ones_b, sml=sml, scr=scr, scr2s=scr2s,
               mrow=mrow, rowb=rowb, stream=stream, stg=stg, wcol=wcol,
               h128_b=h128_b, h64b_b=h64b_b, Mb=Mb, gqa_f=gqa_f, ssel_b=ssel_b)
    emit_phase2(nc, P, AR, A, env)
    if tap == "x2":
        for tt in range(8):
            P.op("scalar", lambda e, tt=tt: e.dma_start(out=A["tap"][tt],
                 in_=stream[:, tt, :]), [("p2fin", P.n("p2fin"))], ("st", 16), dma=True)
        P.op("sync", lambda e: e.nop(), [("st", P.n("st"))], ("fin", 1))
        return

    P.barrier()
    emit_phase34(nc, P, AR, A, env)
    if tap == "x3":
        for tt in range(8):
            P.op("scalar", lambda e, tt=tt: e.dma_start(out=A["tap"][tt],
                 in_=stream[:, tt, :]), [("o1fin", P.n("o1fin"))], ("st", 16), dma=True)
        P.op("sync", lambda e: e.nop(), [("st", P.n("st"))], ("fin", 1))
        return

    P.barrier()
    emit_phase5(nc, P, AR, A, env)


def emit_ln(P, A, ctx, pfx, stream, partials, npart):
    mrow, sml = ctx["mrow"], ctx["sml"]
    scr2s = ctx["scr2s"]

    def vop(emit, waits=(), sem="vs", eng="vector"):
        return P.op(eng, emit, waits, (sem, 1) if sem else None)

    for tt in range(8):
        vop(lambda e, tt=tt: e.reduce_sum(sml[:, 0:1],
            scr2s[:, tt * npart:tt * npart + npart], AX), [], sem=pfx + "ln")
        vop(lambda e, tt=tt: e.activation(mrow[:, 0:1024], stream[:, tt, 0:1024],
            ACT.Square, accum_out=sml[:, 1:2]), [(pfx + "ln", P.n(pfx + "ln"))],
            sem=pfx + "ln", eng="scalar")
        vop(lambda e, tt=tt: e.activation(mrow[:, 0:1024], stream[:, tt, 1024:2048],
            ACT.Square, accum_out=sml[:, 9:10]), [], sem=pfx + "ln", eng="scalar")
        vop(lambda e: e.tensor_add(sml[:, 1:2], sml[:, 1:2], sml[:, 9:10]),
            [(pfx + "ln", P.n(pfx + "ln"))], sem=pfx + "ln")
        vop(lambda e: e.tensor_scalar(sml[:, 2:3], sml[:, 0:1], 1.0 / D, None, ALU.mult),
            [], sem=None)
        vop(lambda e: e.tensor_scalar(sml[:, 3:4], sml[:, 1:2], 1.0 / D, None, ALU.mult),
            [], sem=None)
        vop(lambda e: e.tensor_mul(sml[:, 4:5], sml[:, 2:3], sml[:, 2:3]), [], sem=None)
        vop(lambda e: e.tensor_sub(sml[:, 5:6], sml[:, 3:4], sml[:, 4:5]),
            [], sem=pfx + "ln")
        vop(lambda e: e.activation(sml[:, 6:7], sml[:, 5:6], ACT.Sqrt,
            bias=sml[:, 49:50]), [(pfx + "ln", P.n(pfx + "ln"))],
            sem=pfx + "ln", eng="scalar")
        vop(lambda e: e.reciprocal(sml[:, 7:8], sml[:, 6:7]),
            [(pfx + "ln", P.n(pfx + "ln"))], sem=pfx + "ln")
        vop(lambda e, tt=tt: e.tensor_scalar(stream[:, tt, :], stream[:, tt, :],
            sml[:, 2:3], sml[:, 7:8], ALU.subtract, ALU.mult),
            [(pfx + "ln", P.n(pfx + "ln"))], sem=pfx + "fin")


def emit_proj_ln(nc, P, AR, A, ctx, upto_tap=False, tapname=None):
    PA, PB, PC, PAh = A["PA"], A["PB"], A["PC"], A["PAh"]
    pfx = ctx["sem_prefix"]
    enc, stream, stg, wcol = ctx["enc"], ctx["stream"], ctx["stg"], ctx["wcol"]
    ident_f, mrow, scr, scr2, sml = (ctx["ident_f"], ctx["mrow"], ctx["scr"],
                                     ctx["scr2s"], ctx["sml"])
    wsrc = A[ctx["wsrc"]]

    def load(dst, src, waits=(), sem="ld"):
        return P.op("sync", lambda e, d=dst, s=src: e.dma_start(out=d, in_=s),
                    waits, (sem, 16), dma=True)

    def vop(emit, waits=(), sem="vs", eng="vector"):
        return P.op(eng, emit, waits, (sem, 1) if sem else None)

    def mm(out, lhsT, rhs, start, stop, waits=(), sem=None):
        return P.op("tensor", lambda e, o=out, l=lhsT, r=rhs, s1=start, s2=stop:
                    e.matmul(o, l, r, start=s1, stop=s2), waits,
                    (sem, 1) if sem else None)

    for oc in range(16):
        w = list(ctx["pre_wait"]) if oc < 2 else [(pfx + "w", oc - 1),
                                                   (pfx + "mm", oc - 1)]
        t = load(stg[oc % 2][:, :], wsrc[oc].rearrange("p a b -> p (a b)"), w,
                 sem=f"ds{oc % 2}")
        vop(lambda e, oc=oc, s=stg[oc % 2]:
            e.tensor_copy(wcol[oc % 2][:, :, :], _shape(s[:, :], (16, 128))),
            [(f"ds{oc % 2}", t)], sem=pfx + "w", eng="gpsimd")
    for oc in range(16):
        wv = wcol[oc % 2]
        half = PAh[oc % 2]
        for eh in range(16):
            waits = []
            if eh == 0:
                waits = [(pfx + "w", oc + 1)] + ([(pfx + "ev", oc - 1)] if oc >= 2 else [])
            for j in range(2):
                mm(half[:, j * 512:(j + 1) * 512], wv[:, eh, :],
                   enc[:, eh, j * 512:(j + 1) * 512], eh == 0, eh == 15,
                   waits if (eh == 0 and j == 0) else (),
                   pfx + "mm" if (eh == 15 and j == 1) else None)
        mmn = P.n(pfx + "mm")
        ev = vop(lambda e, half=half: e.tensor_copy(scr[:, 0:1024], half),
                 [(pfx + "mm", mmn), (pfx + "tr", oc)], sem=pfx + "ev")
        for tt in range(8):
            P.op("tensor", lambda e, tt=tt: e.transpose(
                PB[:, tt * 128:(tt + 1) * 128], scr[:, tt * 128:(tt + 1) * 128],
                ident_f[:, :]),
                [(pfx + "ev", P.n(pfx + "ev")), (pfx + "st", 8 * oc)] if tt == 0 else [],
                (pfx + "tr", 1) if tt == 7 else None)
        trn = P.n(pfx + "tr")
        for tt in range(8):
            w = [(pfx + "tr", trn)] + (list(ctx.get("stt_extra", ())) if oc == 0 else [])
            vop(lambda e, oc=oc, tt=tt: e.scalar_tensor_tensor(
                stream[:, tt, oc * 128:(oc + 1) * 128], PB[:, tt * 128:(tt + 1) * 128],
                1.0, stream[:, tt, oc * 128:(oc + 1) * 128], ALU.mult, ALU.add,
                accum_out=scr2[:, tt * 16 + oc:tt * 16 + oc + 1]),
                w if tt == 0 else [], sem=pfx + "st")
    stn = P.n(pfx + "st")
    P.op("vector", lambda e: e.nop(), [(pfx + "st", stn)], (pfx + "lngate", 1))
    emit_ln(P, A, ctx, pfx, stream, None, 16)


def emit_phase2(nc, P, AR, A, env):
    PA, PB, PC, PT, PAh = A["PA"], A["PB"], A["PC"], A["PT"], A["PAh"]
    ident_f, ident_b = env["ident_f"], env["ident_b"]
    onesmat_f, sml, scr = env["onesmat_f"], env["sml"], env["scr"]
    scr2s, mrow, rowb, stream = env["scr2s"], env["mrow"], env["rowb"], env["stream"]
    MQ = MAGIC

    xq_fm = AR.bf(0, 16, 1024)
    xqt = AR.bf(96 * KB, 2048)
    h_fm = AR.bf(96 * KB, 44, 1024)
    stg2 = [AR.f32(32 * KB + i * 8 * KB, 2048) for i in range(2)]
    w1col = [AR.bf(48 * KB + i * 4 * KB, 16, 128) for i in range(2)] + \
            [AR.bf(86 * KB + i * 4 * KB, 16, 128) for i in range(2)]
    w1colf = [AR.bf(48 * KB + i * 4 * KB, 2048) for i in range(2)] + \
             [AR.bf(86 * KB + i * 4 * KB, 2048) for i in range(2)]
    sws = AR.f32(56 * KB, 176)
    sxb = AR.f32(57 * KB, 1024)
    rshb = AR.f32(61 * KB, 1024)
    w1int = [AR.bf(65 * KB + i * 4 * KB, 2048) for i in range(2)]
    qscr = AR.f32(74 * KB, 2048)
    habs = AR.f32(82 * KB, 1024)
    qstg = [AR.f32(i * 5632, 1408) for i in range(3)]
    w2col = AR.bf(17 * KB, 44, 128)
    w2colf = AR.bf(17 * KB, 5632)
    x2fm = AR.bf(96 * KB, 16, 1024)
    x2sc = AR.bf(128 * KB, 2048)

    def vop(emit, waits=(), sem="p2v", eng="vector"):
        return P.op(eng, emit, waits, (sem, 1) if sem else None)

    def mm(out, lhsT, rhs, start, stop, waits=(), sem=None):
        return P.op("tensor", lambda e, o=out, l=lhsT, r=rhs, s1=start, s2=stop:
                    e.matmul(o, l, r, start=s1, stop=s2), waits,
                    (sem, 1) if sem else None)

    def tr(out, in_, idn, waits=(), sem=None):
        return P.op("tensor", lambda e, o=out, i=in_, d=idn: e.transpose(o, i, d),
                    waits, (sem, 1) if sem else None)

    o0f = P.n("o0fin")
    for tt in range(8):
        w = [("o0fin", o0f)] if tt == 0 else []
        vop(lambda e, tt=tt: e.reduce_max(sml[:, 50:51], stream[:, tt, :], AX,
            apply_absolute_value=True), w)
        vop(lambda e, tt=tt: e.tensor_scalar(sml[:, 64 + tt:65 + tt], sml[:, 50:51],
            1.0 / 127.0, 1e-8, ALU.mult, ALU.max))
        vop(lambda e, tt=tt: e.reciprocal(sml[:, 52:53], sml[:, 64 + tt:65 + tt]))
        vop(lambda e, tt=tt: e.tensor_mul(sml[:, 53:54], sml[:, 64 + tt:65 + tt],
            sml[:, 52:53]))
        vop(lambda e: e.tensor_scalar(sml[:, 53:54], sml[:, 53:54], 2.0, -1.0,
            ALU.subtract, ALU.mult))
        vop(lambda e: e.tensor_mul(sml[:, 52:53], sml[:, 52:53], sml[:, 53:54]))
        for j2 in range(2):
            vop(lambda e, tt=tt, j2=j2: e.tensor_scalar(scr[:, 0:1024],
                stream[:, tt, j2 * 1024:(j2 + 1) * 1024], sml[:, 52:53], MQ,
                ALU.mult, ALU.add))
            vop(lambda e, j2=j2: e.tensor_scalar(xqt[:, j2 * 1024:(j2 + 1) * 1024],
                scr[:, 0:1024], MQ, None, ALU.subtract))
        qn = P.n("p2v")
        for w8 in range(2):
            for i in range(8):
                dk = w8 * 8 + i
                tr(PT[:, i * 128:(i + 1) * 128], xqt[:, dk * 128:(dk + 1) * 128],
                   ident_b[:, :],
                   [("p2v", qn), ("p2tw", P.n("p2tw"))] if i == 0 else [],
                   "p2tr" if i == 7 else None)
            trn = P.n("p2tr")
            vop(lambda e, tt=tt, w8=w8: e.tensor_copy(
                xq_fm[:, w8 * 8:(w8 + 1) * 8, tt * 128:(tt + 1) * 128],
                _shape(PT[:, 0:1024], (8, 128))), [("p2tr", trn)], sem="p2tw")
        P.op("scalar", lambda e, tt=tt: e.dma_start(out=A["xsp"][tt],
             in_=stream[:, tt, :]), [("p2tw", P.n("p2tw"))], ("dsp", 16),
             dma=True, chain=False)

    for tt in range(8):
        tr(PC[0:1, 0:128], sml[:, 64 + tt:65 + tt], ident_f[:, :],
           [("p2v", P.n("p2v"))] if tt == 0 else [("p2sx", P.n("p2sx"))], "p2c")
        vop(lambda e, tt=tt: e.tensor_copy(rowb[0:1, tt * 128:(tt + 1) * 128],
            PC[0:1, 0:128]), [("p2c", P.n("p2c"))], sem="p2sx")
    mm(PB[:, 0:512], onesmat_f[0:1, :], rowb[0:1, 0:512], True, True,
       [("p2sx", P.n("p2sx"))])
    mm(PB[:, 512:1024], onesmat_f[0:1, :], rowb[0:1, 512:1024], True, True, [], "p2c")
    vop(lambda e: e.tensor_copy(sxb[:, :], PB[:, 0:1024]),
        [("p2c", P.n("p2c")), ("dsp", 128)], sem="p2sx")
    sxdone = P.n("p2sx")

    wqt, cpt, mmt, evt = {}, {}, {}, {}
    dlo, dhi = {}, {}
    prevab = [0]

    trtok = {}

    def w1_load(c):
        col = (c // 2) + 44 * (c % 2)
        w = [("p2wq", wqt[c - 2])] if c >= 2 else [("p2sx", sxdone)]
        dlo[c] = P.op("sync", lambda e, col=col: e.dma_start(
            out=stg2[c % 2][:, 0:1024], in_=A["l0w1"][col][:, 0:1024]), w,
            ("dwlo", 16), dma=True)
        dhi[c] = P.op("gpsimd", lambda e, col=col: e.dma_start(
            out=stg2[c % 2][:, 1024:2048], in_=A["l0w1"][col][:, 1024:2048]),
            list(w), ("dwhi", 16), dma=True, chain=False)

    def w1_quant(c):
        iv = 54 + (c % 2)
        vop(lambda e, c=c: e.reduce_max(sml[:, 50:51], stg2[c % 2][:, :], AX,
            apply_absolute_value=True), [("dwlo", dlo[c]), ("dwhi", dhi[c])])
        vop(lambda e, c=c: e.tensor_scalar(sws[:, c:c + 1], sml[:, 50:51],
            1.0 / 127.0, 1e-8, ALU.mult, ALU.max))
        vop(lambda e, c=c, iv=iv: e.reciprocal(sml[:, iv:iv + 1],
            sws[:, c:c + 1]))
        vop(lambda e, c=c, iv=iv: e.tensor_mul(sml[:, 53:54], sws[:, c:c + 1],
            sml[:, iv:iv + 1]))
        vop(lambda e: e.tensor_scalar(sml[:, 53:54], sml[:, 53:54], 2.0, -1.0,
            ALU.subtract, ALU.mult))
        ivt = vop(lambda e, iv=iv: e.tensor_mul(sml[:, iv:iv + 1],
            sml[:, iv:iv + 1], sml[:, 53:54]), [], sem="p2iv")
        q1w = [("p2iv", ivt)]
        if c >= 1:
            q1w.append(("p2wq", wqt[c - 1]))
        q1t = P.op("scalar", lambda e, c=c, iv=iv: e.activation(qscr[:, :],
            stg2[c % 2][:, :], ACT.Copy, bias=MQ, scale=sml[:, iv:iv + 1]),
            q1w, ("p2q1", 1))
        ww = [("p2q1", q1t)]
        if c >= 2:
            ww.append(("p2tr", trtok[c - 2]))
        wqt[c] = vop(lambda e, c=c: e.tensor_scalar(w1int[c % 2][:, :],
            qscr[:, :], MQ, None, ALU.subtract), ww, sem="p2wq")

    def w1_trans(c):
        for w8 in range(2):
            for i in range(8):
                dk = w8 * 8 + i
                trw = [("p2wq", wqt[c]), ("p2tw", P.n("p2tw"))] if i == 0 else []
                tr(PT[:, i * 128:(i + 1) * 128],
                   w1int[c % 2][:, dk * 128:(dk + 1) * 128], ident_b[:, :],
                   trw, "p2tr" if i == 7 else None)
            trn = P.n("p2tr")
            cw = [("p2tr", trn)]
            if c >= 4:
                cw.append(("p2mm", mmt[c - 4]))
            P.op("scalar", lambda e, c=c, w8=w8: e.activation(
                w1colf[c % 4][:, w8 * 1024:(w8 + 1) * 1024], PT[:, 0:1024],
                ACT.Copy), cw, ("p2tw", 1))
        cpt[c] = P.n("p2tw")
        trtok[c] = P.n("p2tr")

    def w1_mm(c):
        gi = c % 2
        p = c // 2
        for j in range(2):
            for dk in range(16):
                w = []
                if dk == 0:
                    w = [("p2tw", cpt[c])]
                    if p >= 1:
                        w.append(("p2ev", evt[(p - 1, j, gi)]))
                mm(PAh[gi][:, j * 512:(j + 1) * 512], w1col[c % 4][:, dk, :],
                   xq_fm[:, dk, j * 512:(j + 1) * 512], dk == 0, dk == 15,
                   w, "p2mm" if dk == 15 else None)
        mmt[c] = P.n("p2mm")

    cpv, mxv = {}, {}

    def evac_gate(p):
        for j in range(2):
            jo = slice(j * 512, (j + 1) * 512)
            w = [("p2mm", mmt[2 * p] - (1 - j))]
            if p >= 1:
                w.append(("p2ev", cpv[(p - 1, j)]))
            t1 = vop(lambda e, p=p, jo=jo: e.scalar_tensor_tensor(
                scr[:, jo], PAh[0][:, jo], sws[:, 2 * p:2 * p + 1],
                sxb[:, jo], ALU.mult, ALU.mult), w, sem="p2ev")
            evt[(p, j, 0)] = t1
            sw = [("p2ev", t1)]
            if p >= 1:
                sw.append(("p2ev", mxv[(p - 1, j)]))
            sg = P.op("scalar", lambda e, jo=jo: e.activation(mrow[:, jo],
                scr[:, jo], ACT.Sigmoid), sw, ("p2sg", 1))
            vop(lambda e, jo=jo: e.tensor_mul(scr[:, jo], scr[:, jo],
                mrow[:, jo]), [("p2sg", sg)], sem="p2ev")

    def evac_up(p):
        for j in range(2):
            jo = slice(j * 512, (j + 1) * 512)
            w = [("p2mm", mmt[2 * p + 1] - (1 - j))]
            t2 = vop(lambda e, p=p, jo=jo: e.tensor_scalar(
                mrow[:, jo], PAh[1][:, jo], sws[:, 2 * p + 1:2 * p + 2],
                None, ALU.mult), w, sem="p2ev")
            evt[(p, j, 1)] = t2
            hv = vop(lambda e, jo=jo: e.tensor_mul(scr[:, jo], scr[:, jo],
                mrow[:, jo]), [], sem="p2ev")
            cpv[(p, j)] = vop(lambda e, p=p, jo=jo: e.tensor_copy(
                h_fm[:, p, jo], scr[:, jo]), [], sem="p2ev")
            ab = P.op("scalar", lambda e, jo=jo: e.activation(mrow[:, jo],
                scr[:, jo], ACT.Abs), [("p2ev", hv)], ("p2sg", 1))
            if p == 0:
                mxv[(p, j)] = vop(lambda e, jo=jo: e.tensor_copy(habs[:, jo],
                    mrow[:, jo]), [("p2sg", ab)], sem="p2ev")
            else:
                mxv[(p, j)] = vop(lambda e, jo=jo: e.tensor_max(habs[:, jo],
                    habs[:, jo], mrow[:, jo]), [("p2sg", ab)], sem="p2ev")

    w1_load(0)
    w1_load(1)
    w1_quant(0)
    w1_trans(0)
    for c in range(88):
        if c + 2 < 88:
            w1_load(c + 2)
        if c + 1 < 88:
            w1_quant(c + 1)
            w1_trans(c + 1)
        w1_mm(c)
        if c % 2 == 0:
            evac_gate(c // 2)
        else:
            evac_up(c // 2)
    evA = P.n("p2ev")

    for tt in range(8):
        tr(PB[:, tt * 128:(tt + 1) * 128], habs[:, tt * 128:(tt + 1) * 128],
           ident_f[:, :], [("p2ev", evA)] if tt == 0 else [],
           "p2c" if tt == 7 else None)
    cn = P.n("p2c")
    for tt in range(8):
        vop(lambda e, tt=tt: e.reduce_max(sml[:, 50:51],
            PB[:, tt * 128:(tt + 1) * 128], AX), [("p2c", cn)] if tt == 0 else [])
        vop(lambda e, tt=tt: e.tensor_scalar(sml[:, 80 + tt:81 + tt], sml[:, 50:51],
            1.0 / 127.0, 1e-8, ALU.mult, ALU.max))
        vop(lambda e, tt=tt: e.reciprocal(sml[:, 88 + tt:89 + tt],
            sml[:, 80 + tt:81 + tt]))
        vop(lambda e, tt=tt: e.tensor_mul(sml[:, 53:54], sml[:, 80 + tt:81 + tt],
            sml[:, 88 + tt:89 + tt]))
        vop(lambda e: e.tensor_scalar(sml[:, 53:54], sml[:, 53:54], 2.0, -1.0,
            ALU.subtract, ALU.mult))
        vop(lambda e, tt=tt: e.tensor_mul(sml[:, 88 + tt:89 + tt],
            sml[:, 88 + tt:89 + tt], sml[:, 53:54]))
        vop(lambda e, tt=tt: e.tensor_mul(sml[:, 80 + tt:81 + tt],
            sml[:, 80 + tt:81 + tt], sml[:, 64 + tt:65 + tt]), [], sem="p2hs")
    hsn = P.n("p2hs")
    for tt in range(8):
        tr(PC[0:1, 0:128], sml[:, 88 + tt:89 + tt], ident_f[:, :],
           [("p2hs", hsn)] if tt == 0 else [("p2sx", P.n("p2sx"))], "p2c")
        vop(lambda e, tt=tt: e.tensor_copy(rowb[0:1, tt * 128:(tt + 1) * 128],
            PC[0:1, 0:128]), [("p2c", P.n("p2c"))], sem="p2sx")
    mm(PB[:, 0:512], onesmat_f[0:1, :], rowb[0:1, 0:512], True, True,
       [("p2sx", P.n("p2sx"))])
    mm(PB[:, 512:1024], onesmat_f[0:1, :], rowb[0:1, 512:1024], True, True, [], "p2c")
    vop(lambda e: e.tensor_copy(rshb[:, :], PB[:, 0:1024]),
        [("p2c", P.n("p2c"))], sem="p2hs")

    for p in range(44):
        w = [("p2hs", P.n("p2hs"))] if p == 0 else []
        vop(lambda e, p=p: e.tensor_mul(scr[:, 0:1024], h_fm[:, p, :],
            rshb[:, :]), w)
        vop(lambda e, p=p: e.tensor_scalar(h_fm[:, p, :], scr[:, 0:1024], MQ, MQ,
            ALU.add, ALU.subtract), [], sem="p2hq")
    hqn = P.n("p2hq")

    for i, tt in enumerate((0, 1, 2, 5, 6, 7, 3, 4)):
        w = []
        if i == 0:
            w = [("p2wq", wqt[87]), ("p2mm", mmt[87]),
                 ("p2hs", P.n("p2hs")), ("dsp", P.n("dsp"))]
        elif tt == 3:
            w = [("p2hq", hqn)]
        P.op("sync", lambda e, tt=tt: e.dma_start(out=stream[:, tt, :],
             in_=A["xsp"][tt]), w, ("dx2", 16), dma=True)
    x2ld = P.n("dx2")

    contok = {}
    dql = {}
    mmt2, evt2, cpt2 = {}, {}, {}

    def w2_load(kk):
        if kk >= 128:
            return
        c2, q4 = kk // 8, kk % 4
        sl = kk % 3
        if kk >= 3:
            w = [("p2q2" if (kk - 3) % 8 < 4 else "p2t2", contok[kk - 3])]
        else:
            w = [("p2mm", mmt[87])]
        qd = ("sync", "dq0") if kk % 2 == 0 else ("scalar", "dq1")
        dql[kk] = P.op(qd[0], lambda e, c2=c2, q4=q4, sl=sl: e.dma_start(
            out=qstg[sl][:, :],
            in_=A["l0w2"][c2][:, q4 * 1408:(q4 + 1) * 1408]), w,
            (qd[1], 16), dma=True, chain=(qd[0] == "sync"))

    def w2_maxq(c2, q4):
        kk = c2 * 8 + q4
        sl = kk % 3
        dw = [(("dq0", "dq1")[kk % 2], dql[kk])]
        if q4 == 0:
            contok[kk] = vop(lambda e, sl=sl: e.reduce_max(sml[:, 50:51],
                qstg[sl][:, :], AX, apply_absolute_value=True), dw, sem="p2q2")
        else:
            vop(lambda e, sl=sl: e.reduce_max(sml[:, 51:52], qstg[sl][:, :], AX,
                apply_absolute_value=True), dw, sem=None)
            contok[kk] = vop(lambda e: e.tensor_max(sml[:, 50:51], sml[:, 50:51],
                sml[:, 51:52]), [], sem="p2q2")

    ivt2 = {}

    def w2_scale(c2):
        iv = 56 + (c2 % 2)
        vop(lambda e, c2=c2: e.tensor_scalar(sml[:, 96 + c2:97 + c2],
            sml[:, 50:51], 1.0 / 127.0, 1e-8, ALU.mult, ALU.max))
        vop(lambda e, c2=c2, iv=iv: e.reciprocal(sml[:, iv:iv + 1],
            sml[:, 96 + c2:97 + c2]))
        vop(lambda e, c2=c2, iv=iv: e.tensor_mul(sml[:, 53:54],
            sml[:, 96 + c2:97 + c2], sml[:, iv:iv + 1]))
        vop(lambda e: e.tensor_scalar(sml[:, 53:54], sml[:, 53:54], 2.0, -1.0,
            ALU.subtract, ALU.mult))
        ivt2[c2] = vop(lambda e, iv=iv: e.tensor_mul(sml[:, iv:iv + 1],
            sml[:, iv:iv + 1], sml[:, 53:54]), [], sem="p2iv")

    def w2_quant(c2, q4):
        kk = c2 * 8 + 4 + q4
        sl = kk % 3
        iv = 56 + (c2 % 2)
        q1w = [(("dq0", "dq1")[kk % 2], dql[kk]), ("p2iv", ivt2[c2])]
        q1t = P.op("scalar", lambda e, sl=sl, iv=iv: e.activation(
            qstg[sl][:, :], qstg[sl][:, :], ACT.Copy, bias=MQ,
            scale=sml[:, iv:iv + 1]), q1w, ("p2q1", 1))
        qt = vop(lambda e, sl=sl: e.tensor_scalar(qstg[sl][:, :], qstg[sl][:, :],
            MQ, None, ALU.subtract), [("p2q1", q1t)], sem="p2q2")
        for wv, lo, hi in ((0, 0, 8), (1, 8, 11)):
            trw = []
            if wv == 0:
                trw = [("p2q2", qt), ("p2w2c", P.n("p2w2c"))]
                if c2 >= 1 and q4 == 0:
                    trw.append(("p2st", P.n("p2st")))
            else:
                trw = [("p2w2c", P.n("p2w2c"))]
            for i in range(lo, hi):
                tr(PB[:, (i - lo) * 128:(i - lo + 1) * 128],
                   qstg[sl][:, i * 128:(i + 1) * 128], ident_f[:, :],
                   trw if i == lo else [], "p2t2" if i == hi - 1 else None)
            trn = P.n("p2t2")
            cw = [("p2t2", trn)]
            if c2 >= 1 and q4 == 0 and wv == 0:
                cw.append(("p2m2", mmt2[c2 - 1]))
            P.op("scalar", lambda e, q4=q4, lo=lo, hi=hi: e.activation(
                w2colf[:, (q4 * 11 + lo) * 128:(q4 * 11 + hi) * 128],
                PB[:, 0:(hi - lo) * 128], ACT.Copy), cw, ("p2w2c", 1))
        contok[kk] = P.n("p2t2")

    def w2_mm(c2):
        cpt2[c2] = P.n("p2w2c")
        for ci in range(44):
            for j in range(2):
                w = []
                if ci == 0 and j == 0:
                    w = [("p2w2c", cpt2[c2]), ("p2hq", hqn)]
                    if c2 >= 2:
                        w.append(("p2ev2", evt2[c2 - 2]))
                    else:
                        w.append(("p2ev", evA))
                mm(PAh[c2 % 2][:, j * 512:(j + 1) * 512], w2col[:, ci, :],
                   h_fm[:, ci, j * 512:(j + 1) * 512], ci == 0, ci == 43,
                   w, "p2m2" if (ci == 43 and j == 1) else None)
        mmt2[c2] = P.n("p2m2")

    def evac2(c2):
        vop(lambda e, c2=c2: e.tensor_scalar(scr[:, 0:1024], PAh[c2 % 2],
            sml[:, 96 + c2:97 + c2], None, ALU.mult),
            [("p2m2", mmt2[c2]), ("p2dt", P.n("p2dt"))], sem="p2ev2")
        evt2[c2] = P.n("p2ev2")
        for i in range(8):
            tr(PB[:, i * 128:(i + 1) * 128], scr[:, i * 128:(i + 1) * 128],
               ident_f[:, :],
               [("p2ev2", evt2[c2]), ("p2st", P.n("p2st")),
                ("p2w2c", P.n("p2w2c"))] if i == 0 else [],
               "p2dt" if i == 7 else None)
        dtn = P.n("p2dt")
        for tt in range(8):
            w = [("p2dt", dtn)] + ([("dx2", x2ld)] if c2 == 0 else [])
            vop(lambda e, c2=c2, tt=tt: e.scalar_tensor_tensor(
                stream[:, tt, c2 * 128:(c2 + 1) * 128],
                PB[:, tt * 128:(tt + 1) * 128], sml[:, 80 + tt:81 + tt],
                stream[:, tt, c2 * 128:(c2 + 1) * 128], ALU.mult, ALU.add,
                accum_out=scr2s[:, tt * 16 + c2:tt * 16 + c2 + 1]),
                w if tt == 0 else [], sem="p2st")

    w2_load(0)
    w2_load(1)
    w2_load(2)
    for c2 in range(16):
        for q4 in range(4):
            w2_maxq(c2, q4)
            w2_load(c2 * 8 + q4 + 3)
        w2_scale(c2)
        for q4 in range(4):
            w2_quant(c2, q4)
            w2_load(c2 * 8 + 4 + q4 + 3)
        w2_mm(c2)
        evac2(c2)

    ctx2 = dict(mrow=mrow, sml=sml, scr2s=scr2s)
    emit_ln(P, A, ctx2, "p2", stream, None, 16)
    for tt in range(8):
        P.op("scalar", lambda e, tt=tt: e.dma_start(out=A["xsp"][tt],
             in_=stream[:, tt, :]), [("p2fin", P.n("p2fin"))] if tt == 0 else [],
             ("dsp2", 16), dma=True)
        vop(lambda e, tt=tt: e.tensor_copy(x2sc[:, :], stream[:, tt, :]),
            [("p2fin", P.n("p2fin"))])
        qn = P.n("p2v")
        for w8 in range(2):
            for i in range(8):
                dk = w8 * 8 + i
                tr(PT[:, i * 128:(i + 1) * 128], x2sc[:, dk * 128:(dk + 1) * 128],
                   ident_b[:, :],
                   [("p2v", qn), ("p2xw", P.n("p2xw"))] if i == 0 else [],
                   "p2tr" if i == 7 else None)
            trn = P.n("p2tr")
            vop(lambda e, tt=tt, w8=w8: e.tensor_copy(
                x2fm[:, w8 * 8:(w8 + 1) * 8, tt * 128:(tt + 1) * 128],
                _shape(PT[:, 0:1024], (8, 128))), [("p2tr", trn)], sem="p2xw")
    P.op("vector", lambda e: e.nop(), [("p2xw", P.n("p2xw"))], ("p2done", 1))


def emit_phase34(nc, P, AR, A, env):
    PA, PB, PC, PT, PAh = A["PA"], A["PB"], A["PC"], A["PT"], A["PAh"]
    ident_f, ident_b = env["ident_f"], env["ident_b"]
    onesmat_f, ones_b = env["onesmat_f"], env["ones_b"]
    sml, scr, scr2s = env["sml"], env["scr"], env["scr2s"]
    mrow, rowb, stream = env["mrow"], env["rowb"], env["stream"]
    ssel_b = env["ssel_b"]
    stg, wcol = env["stg"], env["wcol"]

    q_fm = AR.bf(0, 16, 1024)
    x2fm = AR.bf(96 * KB, 16, 1024)
    kstg = AR.bf(128 * KB, 1024)
    vstg = AR.bf(130 * KB, 512)
    vcol = AR.bf(132 * KB, 16, 512)
    enc_b = AR.bf(96 * KB, 16, 1024)
    pt_b = AR.bf(128 * KB, 16, 1024)
    Kh = [AR.bf(160 * KB + i * 4 * KB, 16, 128) for i in range(2)]
    Vh = [AR.bf(168 * KB + i * 4 * KB, 16, 128) for i in range(2)]
    mask_sb = AR.bf(176 * KB, 8, 2, 128)
    p_sb2 = AR.bf(180 * KB, 2048)

    def vop(emit, waits=(), sem="p3v", eng="vector"):
        return P.op(eng, emit, waits, (sem, 1) if sem else None)

    def mm(out, lhsT, rhs, start, stop, waits=(), sem=None):
        return P.op("tensor", lambda e, o=out, l=lhsT, r=rhs, s1=start, s2=stop:
                    e.matmul(o, l, r, start=s1, stop=s2), waits,
                    (sem, 1) if sem else None)

    def tr(out, in_, idn, waits=(), sem=None):
        return P.op("tensor", lambda e, o=out, i=in_, d=idn: e.transpose(o, i, d),
                    waits, (sem, 1) if sem else None)

    def rowmove(dst, src, waits, sem="dmv"):
        return P.op("gpsimd", lambda e, d=dst, s=src: e.dma_start(out=d, in_=s),
                    waits, (sem, 16), dma=True)

    p2d = P.n("p2done")

    for ci in range(16):
        w = [("p2done", p2d)] if ci < 2 else [("qw1", ci - 1), ("qm1", ci - 1)]
        t = P.op("sync", lambda e, ci=ci: e.dma_start(out=stg[ci % 2][:, :],
             in_=A["l1q"][ci].rearrange("p a b -> p (a b)")), w,
             (f"ds{ci % 2}", 16), dma=True)
        vop(lambda e, ci=ci: e.tensor_copy(wcol[ci % 2][:, :, :],
            _shape(stg[ci % 2][:, :], (16, 128))), [(f"ds{ci % 2}", t)],
            sem="qw1", eng="gpsimd")
    for ci in range(16):
        half = PAh[ci % 2]
        for dk in range(16):
            for j in range(2):
                w = []
                if dk == 0 and j == 0:
                    w = [("qw1", ci + 1)] + ([("qe1", ci - 1)] if ci >= 2 else [])
                mm(half[:, j * 512:(j + 1) * 512], wcol[ci % 2][:, dk, :],
                   x2fm[:, dk, j * 512:(j + 1) * 512], dk == 0, dk == 15,
                   w, "qm1" if (dk == 15 and j == 1) else None)
        mmn = P.n("qm1")
        sq = vop(lambda e, half=half: e.activation(scr.bitcast(BF16)[:, 0:1024], half,
                 ACT.Square), [("qm1", mmn)], sem="ss1", eng="scalar")
        for j in range(2):
            mm(PB[0:30, j * 512:(j + 1) * 512], ssel_b[:, ci, 0:30],
               scr.bitcast(BF16)[:, j * 512:(j + 1) * 512], ci == 0, ci == 15,
               [("ss1", sq)], "qst1" if (ci == 15 and j == 1) else None)
        vop(lambda e, ci=ci, half=half: e.tensor_copy(q_fm[:, ci, :], half),
            [("ss1", sq)], sem="qe1")
    ste = vop(lambda e: e.tensor_copy(mrow[0:16, 0:1024], PB[0:16, :]),
              [("qst1", 1)], sem="p3v")
    q2mn = P.n("p3v")

    for ci in range(16):
        w = [("qe1", 16)] if ci < 2 else [("kw1", ci - 1), ("km1", ci - 1)]
        t = P.op("sync", lambda e, ci=ci: e.dma_start(out=stg[ci % 2][:, :],
             in_=A["l1k"][ci].rearrange("p a b -> p (a b)")), w,
             (f"ds{ci % 2}", 16), dma=True)
        vop(lambda e, ci=ci: e.tensor_copy(wcol[ci % 2][:, :, :],
            _shape(stg[ci % 2][:, :], (16, 128))), [(f"ds{ci % 2}", t)],
            sem="kw1", eng="gpsimd")
    for ci in range(16):
        half = PAh[ci % 2]
        for dk in range(16):
            for j in range(2):
                w = []
                if dk == 0 and j == 0:
                    w = [("kw1", ci + 1)] + ([("dkc", 16 * (ci - 1))] if ci >= 2 else [])
                mm(half[:, j * 512:(j + 1) * 512], wcol[ci % 2][:, dk, :],
                   x2fm[:, dk, j * 512:(j + 1) * 512], dk == 0, dk == 15,
                   w, "km1" if (dk == 15 and j == 1) else None)
        mmn = P.n("km1")
        sq = vop(lambda e, half=half: e.activation(scr.bitcast(BF16)[:, 0:1024], half,
                 ACT.Square), [("km1", mmn)], sem="ss1", eng="scalar")
        for j in range(2):
            mm(PB[0:30, j * 512:(j + 1) * 512], ssel_b[:, ci, 0:30],
               scr.bitcast(BF16)[:, j * 512:(j + 1) * 512], ci == 0, ci == 15,
               [("ss1", sq), ("p3v", q2mn)], "kst1" if (ci == 15 and j == 1) else None)
        ke = vop(lambda e, ci=ci, half=half: e.tensor_copy(kstg[:, :], half),
                 [("ss1", sq), ("dkc", P.n("dkc"))], sem="p3v")
        P.op("scalar", lambda e, ci=ci: e.dma_start(out=A["cc_k"][ci],
             in_=kstg[:, :]), [("p3v", ke)], ("dkc", 16), dma=True)
    ken = P.n("p3v")
    ste2 = vop(lambda e: e.tensor_copy(mrow[32:48, 0:1024], PB[0:16, :]),
               [("kst1", 1)], sem="p3v")
    vop(lambda e: e.reduce_max(sml[32:48, 25:26], mrow[32:48, 0:1024], AX),
        [("p3v", ste2)], sem="p3v")
    mvs = rowmove(sml[0:16, 25:26], sml[32:48, 25:26], [("p3v", P.n("p3v"))])
    st3 = vop(lambda e: e.tensor_copy(sml[0:16, 26:27], sml[0:16, 25:26]),
              [("dmv", P.n("dmv"))], sem="p3v")
    P.op("scalar", lambda e: e.dma_start(out=A["cc_s"].ap(), in_=sml[0:16, 25:27]),
         [("p3v", st3)], ("dks", 16), dma=True)

    for og in range(4):
        for dk in range(16):
            w = [("km1", 16)] if (og == 0 and dk < 2) else \
                [("vw1", 16 * og + dk - 1)]
            if og > 0 and dk == 0:
                w = w + [("vm1", 8 * og)]
            t = P.op("sync", lambda e, og=og, dk=dk: e.dma_start(
                 out=stg[dk % 2][:, 0:512], in_=A["l1v"][dk][:, og * 512:(og + 1) * 512]),
                 w, (f"ds{dk % 2}", 16), dma=True)
            vop(lambda e, dk=dk: e.tensor_copy(vcol[:, dk, :], stg[dk % 2][:, 0:512]),
                [(f"ds{dk % 2}", t)], sem="vw1", eng="gpsimd")
        for lt in range(8):
            slot = PA[:, (lt % 4) * 512:(lt % 4 + 1) * 512]
            for dk in range(16):
                w = []
                if dk == 0:
                    w = [("vw1", 16 * (og + 1))] + \
                        ([("dvc", 16 * (8 * og + lt - 3))] if (8 * og + lt) >= 4 else [])
                    if og == 0 and lt == 0:
                        w.append(("p3v", ken))
                mm(slot, x2fm[:, dk, lt * 128:(lt + 1) * 128], vcol[:, dk, :],
                   dk == 0, dk == 15, w, "vm1" if dk == 15 else None)
            mmn = P.n("vm1")
            ve = vop(lambda e, slot=slot: e.tensor_copy(vstg[:, :], slot),
                     [("vm1", mmn), ("dvc", P.n("dvc"))], sem="p3v")
            P.op("scalar", lambda e, lt=lt, og=og: e.dma_start(
                 out=A["cc_v"][lt][:, og * 512:(og + 1) * 512], in_=vstg[:, :]),
                 [("p3v", ve)], ("dvc", 16), dma=True)

    RG = [[0, 1], [2, 3], [4, 5], [6, 7]]
    P.op("gpsimd", lambda e: e.collective_compute(
        "AllGather", ALU.bypass, replica_groups=RG,
        ins=[A["cc_k"].ap().opt()], outs=[A["gg_k"].ap().opt()]),
        [("dkc", P.n("dkc")), ("dvc", P.n("dvc")), ("dks", P.n("dks"))], ("ccs", 1))
    P.op("gpsimd", lambda e: e.collective_compute(
        "AllGather", ALU.bypass, replica_groups=RG,
        ins=[A["cc_v"].ap().opt()], outs=[A["gg_v"].ap().opt()]), [], ("ccs", 1))
    P.op("gpsimd", lambda e: e.collective_compute(
        "AllGather", ALU.bypass, replica_groups=RG,
        ins=[A["cc_s"].ap().opt()], outs=[A["gg_s"].ap().opt()]), [], ("ccs", 1))

    t = P.op("sync", lambda e: e.dma_start(out=sml[0:16, 20:22], in_=A["gg_s"][0]),
             [("ccs", 3)], ("dgs", 16), dma=True)
    t2 = P.op("sync", lambda e: e.dma_start(out=sml[0:16, 22:24], in_=A["gg_s"][1]),
              [], ("dgs2", 16), dma=True)
    vop(lambda e: e.tensor_max(sml[0:16, 28:29], sml[0:16, 20:21], sml[0:16, 22:23]),
        [("dgs", t), ("dgs2", t2)], sem="p3v")
    e3 = vop(lambda e: e.tensor_scalar(sml[0:16, 28:29], sml[0:16, 28:29], -INV2C,
             None, ALU.mult), [], sem="p3v")
    tr(PC[0:1, 0:16], sml[0:16, 28:29], ident_f[0:16, 0:16], [("p3v", e3)], "p3m")
    e4 = vop(lambda e: e.tensor_copy(sml[0:1, 4:20], PC[0:1, 0:16]),
             [("p3m", P.n("p3m"))], sem="p3v")
    mm(PC[0:128, 0:16], onesmat_f[0:1, :], sml[0:1, 4:20], True, True,
       [("p3v", e4)], "p3m")
    Mb2 = sml[:, 108:124]
    e5 = vop(lambda e: e.tensor_copy(sml[:, 108:124], PC[:, 0:16]),
             [("p3m", P.n("p3m"))], sem="p3v")
    mbn = P.n("p3v")

    tm0 = P.op("sync", lambda e: e.dma_start(out=stg[0][:, :],
          in_=A["maskd"].rearrange("p a b f -> p (a b f)")), [("ccs", 3)],
          ("ds0", 16), dma=True)
    tm_ = vop(lambda e: e.tensor_copy(mask_sb[:, :, :, :],
              _shape(stg[0][:, :], (8, 2, 128))), [("ds0", tm0)], sem="dmk")

    negm2 = _shape(scr2s[:, 0:128], (8, 16))
    for qt in range(8):
        P.op("tensor", lambda e, qt=qt: e.transpose(PC[:, 0:16],
             mrow[0:16, qt * 128:(qt + 1) * 128], ident_f[0:16, 0:16]),
             [("p3v", P.n("p3v")), ("ccs", 3)] if qt == 0 else [("nmt2", qt)], ("p3m", 1))
        vop(lambda e, qt=qt: e.scalar_tensor_tensor(negm2[:, qt, :],
            PC[:, 0:16], -INV2C, Mb2[:, 0:16], ALU.mult, ALU.add),
            [("p3m", P.n("p3m"))], sem="nmt2")
    nmtn2 = P.n("nmt2")

    for qh in range(16):
        for r in range(2):
            tk = P.op("sync", lambda e, qh=qh, r=r: e.dma_start(
                 out=Kh[qh % 2][:, r:16:2, :],
                 in_=A["gg_k"][r, qh].rearrange("p (l f) -> p l f", f=128)),
                 ([("ccs", 3), ("dmk", 1)] if qh < 2 else [("p4kf", qh - 1)])
                 if r == 0 else [],
                 (f"dkh{qh % 2}", 16), dma=True)
            tv = P.op("sync", lambda e, qh=qh, r=r: e.dma_start(
                 out=Vh[qh % 2][:, r:16:2, :],
                 in_=A["gg_v"][r][:, :, qh * 128:(qh + 1) * 128].rearrange(
                     "l p f -> p l f")),
                 [], (f"dvh{qh % 2}", 16), dma=True)
        base = P.n("exp1")
        for qt in range(8):
            ns = (2 * qt + 2) * 128
            npc = (ns + 511) // 512
            for j in range(npc):
                c0, c1 = j * 512, min((j + 1) * 512, ns)
                w = []
                if j == 0:
                    if qh == 0 and qt == 0:
                        w = [("nmt2", nmtn2), ("dmk", 1), (f"dkh{qh % 2}", tk)]
                    else:
                        w = [("ptr1", P.n("ptr1"))]
                    if qt == 0:
                        w.append((f"dkh{qh % 2}", tk))
                        if qh:
                            w.append(("bcd1", qh))
                mm(PA[:, c0:c1], q_fm[:, qh, qt * 128:(qt + 1) * 128],
                   Kh[qh % 2].rearrange("p a f -> p (a f)")[:, c0:c1],
                   True, True, w, "smm1" if j == npc - 1 else None)
            smm = P.n("smm1")
            mk = vop(lambda e, qt=qt, ns=ns: e.tensor_add(PA[:, 2 * qt * 128:ns],
                     PA[:, 2 * qt * 128:ns], mask_sb[:, qt, :, :].rearrange(
                     "p a f -> p (a f)")), [("smm1", smm)], sem="p3v")
            vop(lambda e, qh=qh, qt=qt, ns=ns: e.activation(p_sb2[:, 0:ns],
                PA[:, 0:ns], ACT.Exp, bias=negm2[:, qt, qh:qh + 1], scale=ISQ,
                accum_out=sml[:, 50 + qt:51 + qt]),
                [("p3v", mk)], sem="exp1", eng="scalar")
            expn = P.n("exp1")
            nst = 2 * qt + 2
            for w8 in range((nst + 7) // 8):
                lo = w8 * 8
                hi = min(lo + 8, nst)
                for i in range(lo, hi):
                    tr(PT[:, (i - lo) * 128:(i - lo + 1) * 128],
                       p_sb2[:, i * 128:(i + 1) * 128], ident_b[:, :],
                       [("exp1", expn), ("ptw1", P.n("ptw1"))] if i == lo else [],
                       "ptr1" if i == hi - 1 else None)
                trn = P.n("ptr1")
                nw = hi - lo
                pw = [("ptr1", trn)] + ([("avm1", qh)] if (qt == 0 and w8 == 0 and qh)
                                        else [])
                vop(lambda e, qt=qt, lo=lo, nw=nw: e.tensor_copy(
                    pt_b[:, lo:lo + nw, qt * 128:(qt + 1) * 128],
                    _shape(PT[:, 0:nw * 128], (nw, 128))), pw, sem="ptw1")
        for qt in range(8):
            tr(PC[0:1, 0:128], sml[:, 50 + qt:51 + qt], ident_f[:, :],
               [("ptw1", P.n("ptw1"))] if qt == 0 else [("rcd1", P.n("rcd1"))], "p3m")
            vop(lambda e, qt=qt: e.tensor_copy(rowb[0:1, qt * 128:(qt + 1) * 128],
                PC[0:1, 0:128]), [("p3m", P.n("p3m"))], sem="rcd1")
        vop(lambda e: e.reciprocal(rowb[0:1, 0:1024], rowb[0:1, 0:1024]),
            [], sem="rcd1")
        mm(PAh[0][:, 0:512], onesmat_f[0:1, :], rowb[0:1, 0:512], True, True,
           [("rcd1", P.n("rcd1"))])
        mm(PAh[0][:, 512:1024], onesmat_f[0:1, :], rowb[0:1, 512:1024],
           True, True, [], "p3m")
        bcn = P.n("p3m")
        ev = vop(lambda e: e.tensor_copy(scr[:, 0:1024], PAh[0]), [("p3m", bcn)],
                 sem="bcd1")
        for st in range(16):
            qs = (st // 2) * 128
            for j0 in range(2):
                lo, hi = j0 * 512, (j0 + 1) * 512
                if qs >= hi:
                    continue
                last = (st == 7) if j0 == 0 else (st == 15)
                aw = []
                if st == 0 and j0 == 0:
                    aw = ([("encd1", qh)] if qh else []) + [(f"dvh{qh % 2}", tv)]
                mm(PB[:, max(qs, lo):hi], Vh[qh % 2][:, st, :],
                   pt_b[:, st, max(qs, lo):hi], st == 0, last,
                   aw, "avm1" if (st == 15 and j0 == 1) else None)
        avn = P.n("avm1")
        P.op("tensor", lambda e: e.nop(), [("avm1", avn)], ("p4kf", 1))
        vop(lambda e, qh=qh: e.tensor_mul(enc_b[:, qh, :], PB[:, 0:1024],
            scr[:, 0:1024]), [("bcd1", P.n("bcd1")), ("avm1", avn)], sem="encd1")

    env2 = dict(env)
    emit_proj_ln(nc, P, AR, A, dict(
        wsrc="l1o", enc=enc_b, stream=stream, stg=stg, wcol=wcol,
        ident_f=ident_f, mrow=mrow, scr=scr, scr2s=scr2s, sml=sml,
        pre_wait=[("encd1", 16)], stt_extra=[],
        sem_prefix="o1", nchunks=16))




def emit_phase5(nc, P, AR, A, env):
    PA, PB, PC, PT, PAh = A["PA"], A["PB"], A["PC"], A["PT"], A["PAh"]
    ident_b = env["ident_b"]
    ident_f = env["ident_f"]
    sml, scr = env["sml"], env["scr"]
    scr2s, mrow, stream = env["scr2s"], env["mrow"], env["stream"]
    h128_b, h64b_b = env["h128_b"], env["h64b_b"]
    FSC = 1.0 / math.sqrt(H1)

    x3fm = AR.bf(0, 16, 512)
    w3c5 = [AR.bf(16 * KB + i * 4 * KB, 16, 128) for i in range(2)]
    w3c5f = [AR.bf(16 * KB + i * 4 * KB, 2048) for i in range(2)]
    ha = [AR.bf(24 * KB + i * KB, 512) for i in range(2)]
    sil = AR.f32(26 * KB, 512)
    fwc = [AR.bf(28 * KB + i * KB, 512) for i in range(2)]
    pstag = [AR.bf(30 * KB + i * KB, 512) for i in range(2)]
    g1T = AR.bf(96 * KB, 64, 512)
    stgw1 = AR.f32(160 * KB, 2048)
    stgw3 = AR.f32(168 * KB, 2048)
    wc5 = [AR.bf(176 * KB + i * 4 * KB, 16, 128) for i in range(2)]
    wc5f = [AR.bf(176 * KB + i * 4 * KB, 2048) for i in range(2)]
    w2s = [AR.f32(160 * KB + i * 2 * KB, 512) for i in range(2)]
    w2q = [AR.bf(164 * KB + i * KB, 512) for i in range(2)]
    x3c = AR.bf(166 * KB, 2048)

    def vop(emit, waits=(), sem="p5v", eng="vector"):
        return P.op(eng, emit, waits, (sem, 1) if sem else None)

    def mm(out, lhsT, rhs, start, stop, waits=(), sem=None):
        return P.op("tensor", lambda e, o=out, l=lhsT, r=rhs, s1=start, s2=stop:
                    e.matmul(o, l, r, start=s1, stop=s2), waits,
                    (sem, 1) if sem else None)

    def tr(out, in_, idn, waits=(), sem=None):
        return P.op("tensor", lambda e, o=out, i=in_, d=idn: e.transpose(o, i, d),
                    waits, (sem, 1) if sem else None)

    o1f = P.n("o1fin")
    for g in range(2):
        for ti in range(4):
            tt = g * 4 + ti
            w = []
            if ti == 0:
                w = [("o1fin", o1f)] if g == 0 else \
                    [("p5dm", P.n("p5dm")), ("p5wq", P.n("p5wq"))]
            vop(lambda e, tt=tt: e.tensor_copy(x3c[:, :], stream[:, tt, :]), w)
            cn = P.n("p5v")
            for w8 in range(2):
                for i in range(8):
                    dk = w8 * 8 + i
                    trw = []
                    if i == 0:
                        trw = [("p5v", cn), ("p5w", P.n("p5w"))]
                        if ti == 0 and w8 == 0:
                            trw += [("p5f", P.n("p5f")), ("p5g", P.n("p5g"))]
                    tr(PT[:, i * 128:(i + 1) * 128], x3c[:, dk * 128:(dk + 1) * 128],
                       ident_b[:, :], trw, "p5t" if i == 7 else None)
                trn = P.n("p5t")
                vop(lambda e, ti=ti, w8=w8: e.tensor_copy(
                    x3fm[:, w8 * 8:(w8 + 1) * 8, ti * 128:(ti + 1) * 128],
                    _shape(PT[:, 0:1024], (8, 128))), [("p5t", trn)], sem="p5w")
        x3n = P.n("p5w")

        qtok, q3tok, mmtok, hatok, p1tok, g1tok = {}, {}, {}, {}, {}, {}
        dA, dB, dC, dD, satok = {}, {}, {}, {}, {}

        def emit_scatter(a, g=g):
            pp = a % 2
            w = [("p5p1", p1tok[a])]
            if a >= 2:
                w += [("dsa", satok[a - 2][0]), ("dsb", satok[a - 2][1])]
            g1tok[a] = vop(lambda e, pp=pp: e.tensor_copy(
                pstag[pp][:, :], PB[:, pp * 512:pp * 512 + 512]), w, sem="p5g")
            sw = [("p5g", g1tok[a])]
            if g == 1 and a == 0:
                sw.append(("p5m2", P.n("p5m2")))
            ta = P.op("gpsimd", lambda e, a=a, pp=pp: e.dma_start(
                out=g1T[a:a + 1, :, :], in_=pstag[pp][0:64, :]), sw,
                ("dsa", 16), dma=True, chain=False)
            tb = P.op("gpsimd", lambda e, a=a, pp=pp: e.dma_start(
                out=g1T[64 + a:65 + a, :, :], in_=pstag[pp][64:128, :]),
                list(sw), ("dsb", 16), dma=True, chain=False)
            satok[a] = (ta, tb)

        def emit_loads(a):
            w1w = [("p5w", x3n)] if a == 0 else [("p5q", qtok[a - 1])]
            w3w = [("p5w", x3n)] if a == 0 else [("p5q3", q3tok[a - 1])]
            src1 = A["l1w1"][a].rearrange("p c f -> p (c f)")
            src3 = A["l1w3"][a].rearrange("p c f -> p (c f)")
            dA[a] = P.op("sync", lambda e, s=src1: e.dma_start(
                out=stgw1[:, 0:1024], in_=s[:, 0:1024]), w1w, ("dwA", 16), dma=True)
            dB[a] = P.op("gpsimd", lambda e, s=src1: e.dma_start(
                out=stgw1[:, 1024:2048], in_=s[:, 1024:2048]), list(w1w),
                ("dwB", 16), dma=True, chain=False)
            dC[a] = P.op("scalar", lambda e, s=src3: e.dma_start(
                out=stgw3[:, 0:1024], in_=s[:, 0:1024]), w3w, ("dwC", 16),
                dma=True, chain=False)
            dD[a] = P.op("sync", lambda e, s=src3: e.dma_start(
                out=stgw3[:, 1024:2048], in_=s[:, 1024:2048]), list(w3w),
                ("dwD", 16), dma=True)

        def emit_casts(a):
            cw = [("dwA", dA[a]), ("dwB", dB[a])]
            c3 = [("dwC", dC[a]), ("dwD", dD[a])]
            if a >= 2:
                cw.append(("p5m", mmtok[a - 2]))
                c3.append(("p5m", mmtok[a - 2]))
            qtok[a] = vop(lambda e, a=a: e.tensor_copy(
                wc5f[a % 2][:, :], stgw1[:, :]), cw, sem="p5q")
            q3tok[a] = P.op("scalar", lambda e, a=a: e.activation(
                w3c5f[a % 2][:, :], stgw3[:, :], ACT.Copy), c3, ("p5q3", 1))

        emit_loads(0)
        emit_casts(0)
        for a in range(64):
            if a + 1 < 64:
                emit_loads(a + 1)
                emit_casts(a + 1)
            ga = (a % 2) * 512
            ua = 1024 + (a % 2) * 512
            for dk in range(16):
                w = []
                if dk == 0:
                    w = [("p5q", qtok[a]), ("p5q3", q3tok[a])]
                    if a >= 2:
                        w.append(("p5e", hatok[a - 2]))
                mm(PA[:, ga:ga + 512], wc5[a % 2][:, dk, :], x3fm[:, dk, :],
                   dk == 0, dk == 15, w, None)
                mm(PA[:, ua:ua + 512], w3c5[a % 2][:, dk, :], x3fm[:, dk, :],
                   dk == 0, dk == 15, [], "p5m" if dk == 15 else None)
            mmtok[a] = P.n("p5m")
            if a >= 1:
                pp = (a - 1) % 2
                p1tok[a - 1] = mm(PB[:, pp * 512:pp * 512 + 512], h128_b[:, :],
                                  ha[pp][:, :], True, True,
                                  [("p5e", hatok[a - 1])] +
                                  ([("p5g", g1tok[a - 3])] if a >= 3 else []),
                                  "p5p1")
            sgw = [("p5m", mmtok[a])]
            if a >= 1:
                sgw.append(("p5e", hatok[a - 1]))
            sg = P.op("scalar", lambda e, ga=ga: e.activation(
                sil[:, :], PA[:, ga:ga + 512], ACT.Sigmoid), sgw, ("sl5", 1))
            vop(lambda e, ga=ga: e.tensor_mul(sil[:, :], PA[:, ga:ga + 512],
                sil[:, :]), [("sl5", sg)], sem=None)
            hw = [("p5p1", p1tok[a - 2])] if a >= 2 else []
            hatok[a] = vop(lambda e, a=a, ua=ua: e.tensor_mul(
                ha[a % 2][:, :], sil[:, :], PA[:, ua:ua + 512]), hw, sem="p5e")
            if a >= 1:
                emit_scatter(a - 1)
        p1tok[63] = mm(PB[:, 512:1024], h128_b[:, :], ha[1][:, :], True, True,
                       [("p5e", hatok[63]), ("p5g", g1tok[61])], "p5p1")
        emit_scatter(63)
        g1n = P.n("p5g")
        qnA, q3nA = P.n("p5q"), P.n("p5q3")
        dsn = (P.n("dsa"), P.n("dsb"))

        w2tok, wqtok, p2tok, fetok, dmtok = {}, {}, {}, {}, {}

        def emit_bloads(k):
            oq, chi = k // 64, k % 64
            sl = k % 2
            w2w = [("p5wq", wqtok[k - 2])] if k >= 2 else \
                  [("p5q", qnA), ("p5q3", q3nA)]
            ta = P.op("sync", lambda e, oq=oq, chi=chi, sl=sl: e.dma_start(
                out=w2s[sl][0:64, :],
                in_=A["l1w2"][oq, chi, 0:64, :]), w2w,
                ("dw2a", 16), dma=True)
            tb = P.op("scalar", lambda e, oq=oq, chi=chi, sl=sl: e.dma_start(
                out=w2s[sl][64:128, :],
                in_=A["l1w2"][oq, chi, 64:128, :]), list(w2w),
                ("dw2b", 16), dma=True, chain=False)
            w2tok[k] = (ta, tb)

        def emit_down(k):
            oq, chi = k // 64, k % 64
            sl = k % 2
            for ot in range(4):
                w = []
                if ot == 0:
                    w = [("p5f", fetok[k]), ("p5wq", wqtok[k])]
                    if chi == 0 and (oq or g):
                        w.append(("p5ev", P.n("p5ev")))
                mm(PA[:, ot * 512:(ot + 1) * 512],
                   w2q[sl][:, ot * 128:(ot + 1) * 128],
                   fwc[sl][:, :], chi == 0, chi == 63, w,
                   "p5dm" if ot == 3 else None)
            dmtok[k] = P.n("p5dm")

        def emit_evac(kbase):
            oq = kbase // 64
            dmn = dmtok[kbase + 63]
            for ot in range(4):
                vop(lambda e, ot=ot: e.tensor_scalar(scr[:, 0:512],
                    PA[:, ot * 512:(ot + 1) * 512], FSC, None, ALU.mult),
                    [("p5dm", dmn)] if ot == 0 else [("p5dt", P.n("p5dt"))],
                    sem="p5ev")
                evn = P.n("p5ev")
                for ti in range(4):
                    tr(PB[:, ti * 128:(ti + 1) * 128],
                       scr[:, ti * 128:(ti + 1) * 128], ident_f[:, :],
                       [("p5ev", evn), ("p5st", P.n("p5st"))] if ti == 0 else [],
                       "p5dt" if ti == 3 else None)
                dtn = P.n("p5dt")
                for ti in range(4):
                    tt = g * 4 + ti
                    oc = oq * 4 + ot
                    vop(lambda e, tt=tt, oc=oc, ti=ti: e.scalar_tensor_tensor(
                        stream[:, tt, oc * 128:(oc + 1) * 128],
                        PB[:, ti * 128:(ti + 1) * 128], 1.0,
                        stream[:, tt, oc * 128:(oc + 1) * 128], ALU.mult, ALU.add,
                        accum_out=scr2s[:, ti * 16 + oc:ti * 16 + oc + 1]),
                        [("p5dt", dtn)] if ti == 0 else [], sem="p5st")

        emit_bloads(0)
        for k in range(256):
            oq, chi = k // 64, k % 64
            sl = k % 2
            pth = k % 2
            if k + 1 < 256:
                emit_bloads(k + 1)
            cw = [("dw2a", w2tok[k][0]), ("dw2b", w2tok[k][1])]
            if k >= 2:
                cw.append(("p5dm", dmtok[k - 2]))
            wqtok[k] = vop(lambda e, sl=sl: e.tensor_copy(w2q[sl][:, :],
                           w2s[sl][:, :]), cw, sem="p5wq")
            p2dst = PC[:, 0:512] if pth == 0 else PB[:, 512:1024]
            pw = [("dsa", dsn[0]), ("dsb", dsn[1])] if k < 2 else \
                 [("p5f", fetok[k - 2])]
            mm(p2dst, h64b_b[:, :], g1T[:, chi, :], True, True, pw, "p5m2")
            p2tok[k] = P.n("p5m2")
            fetok[k] = vop(lambda e, sl=sl, d=p2dst: e.tensor_copy(
                fwc[sl][:, :], d), [("p5m2", p2tok[k])], sem="p5f")
            if k >= 1:
                emit_down(k - 1)
            if chi == 0 and k >= 64:
                emit_evac(k - 64)
        emit_down(255)
        emit_evac(192)
        for ti in range(4):
            tt = g * 4 + ti
            vop(lambda e, ti=ti: e.reduce_sum(sml[:, 0:1],
                scr2s[:, ti * 16:ti * 16 + 16], AX), [], sem="p5ln")
            vop(lambda e, tt=tt: e.activation(mrow[:, 0:1024], stream[:, tt, 0:1024],
                ACT.Square, accum_out=sml[:, 1:2]), [("p5ln", P.n("p5ln"))],
                sem="p5ln", eng="scalar")
            vop(lambda e, tt=tt: e.activation(mrow[:, 0:1024], stream[:, tt, 1024:2048],
                ACT.Square, accum_out=sml[:, 9:10]), [], sem="p5ln", eng="scalar")
            vop(lambda e: e.tensor_add(sml[:, 1:2], sml[:, 1:2], sml[:, 9:10]),
                [("p5ln", P.n("p5ln"))], sem="p5ln")
            vop(lambda e: e.tensor_scalar(sml[:, 2:3], sml[:, 0:1], 1.0 / D, None,
                ALU.mult))
            vop(lambda e: e.tensor_scalar(sml[:, 3:4], sml[:, 1:2], 1.0 / D, None,
                ALU.mult))
            vop(lambda e: e.tensor_mul(sml[:, 4:5], sml[:, 2:3], sml[:, 2:3]))
            vop(lambda e: e.tensor_sub(sml[:, 5:6], sml[:, 3:4], sml[:, 4:5]),
                [], sem="p5ln")
            vop(lambda e: e.activation(sml[:, 6:7], sml[:, 5:6], ACT.Sqrt,
                bias=sml[:, 49:50]), [("p5ln", P.n("p5ln"))], sem="p5ln",
                eng="scalar")
            vop(lambda e: e.reciprocal(sml[:, 7:8], sml[:, 6:7]),
                [("p5ln", P.n("p5ln"))], sem="p5ln")
            vop(lambda e, tt=tt: e.tensor_scalar(stream[:, tt, :], stream[:, tt, :],
                sml[:, 2:3], sml[:, 7:8], ALU.subtract, ALU.mult),
                [("p5ln", P.n("p5ln"))], sem="p5fin")
            qd = ("scalar", "doutA") if ti % 2 == 0 else ("gpsimd", "doutB")
            P.op(qd[0], lambda e, tt=tt: e.dma_start(out=A["out"][tt],
                 in_=stream[:, tt, :]), [("p5fin", P.n("p5fin"))], (qd[1], 16),
                 dma=True, chain=False)
    P.op("sync", lambda e: e.nop(),
         [("doutA", P.n("doutA")), ("doutB", P.n("doutB"))], ("fin", 1))


def _hadamard(n):
    h = np.array([[1.0]], dtype=np.float32)
    while h.shape[0] < n:
        h = np.block([[h, h], [h, -h]])
    return h.astype(np.float32)


def _colblocks(wt, nci, ndk):
    ind, outd = wt.shape
    assert ind == ndk * 128 and outd == nci * 128
    return np.ascontiguousarray(
        wt.reshape(ndk, 128, nci, 128).transpose(2, 1, 0, 3))


def _rope_tables(cols):
    inv = 1.0 / (10000.0 ** (np.arange(0, HD, 2, dtype=np.float32) / HD))
    ang = cols[:, None].astype(np.float32) * inv[None, :]
    c = np.cos(ang).T
    s = np.sin(ang).T
    ct = np.concatenate([c, c], axis=0)
    st = np.concatenate([-s, s], axis=0)
    return np.ascontiguousarray(ct), np.ascontiguousarray(st)


_SHARED_CACHE = {}


def _prep_shared(inputs):
    key = id(inputs.get("L0_attn_q_kernel"))
    if _SHARED_CACHE.get("key") == key:
        return _SHARED_CACHE["val"]
    f32 = np.float32
    S = {}
    S["l0q"] = _colblocks(np.asarray(inputs["L0_attn_q_kernel"], f32), 16, 16)
    S["l0k"] = _colblocks(np.asarray(inputs["L0_attn_k_kernel"], f32), 4, 16)
    S["l0v"] = np.ascontiguousarray(np.asarray(inputs["L0_attn_v_kernel"], f32).reshape(16, 128, 512))
    S["l0o"] = _colblocks(np.asarray(inputs["L0_attn_out_kernel"], f32), 16, 16)
    S["l0w1"] = np.ascontiguousarray(np.asarray(inputs["L0_ffn_w_gate_up"], f32).reshape(88, 128, D))
    S["l0w2"] = np.ascontiguousarray(np.asarray(inputs["L0_ffn_w_down"], f32).reshape(16, 128, I0))
    S["l1q"] = _colblocks(np.asarray(inputs["L1_attn_Wq"], f32).T, 16, 16)
    S["l1k"] = _colblocks(np.asarray(inputs["L1_attn_Wk"], f32).T, 16, 16)
    S["l1v"] = np.ascontiguousarray(np.asarray(inputs["L1_attn_Wv"], f32).T.reshape(16, 128, D))
    S["l1o"] = _colblocks(np.asarray(inputs["L1_attn_Wo"], f32).T, 16, 16)
    w1 = np.asarray(inputs["L1_ffn_W1"], f32)
    w3 = np.asarray(inputs["L1_ffn_W3"], f32)
    S["l1w1"] = np.ascontiguousarray(w1.reshape(64, 128, 16, 128).transpose(0, 3, 2, 1))
    S["l1w3"] = np.ascontiguousarray(w3.reshape(64, 128, 16, 128).transpose(0, 3, 2, 1))
    w2t = np.ascontiguousarray(np.asarray(inputs["L1_ffn_W2"], f32).T)
    S["l1w2"] = np.ascontiguousarray(
        w2t.reshape(64, 64, 2, D).transpose(1, 2, 0, 3).reshape(64, 128, 4, 512)
        .transpose(2, 0, 1, 3))
    S["ident"] = np.eye(128, dtype=f32)
    perm = np.empty(128, np.int64)
    perm[:64] = 2 * np.arange(64)
    perm[64:] = 2 * np.arange(64) + 1
    S["h128"] = np.ascontiguousarray(_hadamard(128)[:, perm])
    h64 = _hadamard(64)
    S["h64b"] = np.block([[h64, np.zeros((64, 64), f32)],
                          [np.zeros((64, 64), f32), h64]]).astype(f32)
    sel = np.zeros((4, 16), f32)
    for kv in range(4):
        sel[kv, 4 * kv:4 * kv + 4] = 1.0
    S["gqa"] = sel
    swp = np.zeros((128, 128), f32)
    for i in range(128):
        swp[(i + 64) % 128, i] = 1.0
    S["swp"] = swp
    cf, sf = _rope_tables(np.arange(T))
    S["ropec_f"], S["ropes_f"] = cf, sf
    S["epsc"] = np.tile(np.array([[1e-6, 1e-5]], np.float32), (128, 1))
    M_Q2H, M_QTOT, M_KT0, M_KT1 = 0, 16, 17, 18
    M_K2H0, M_K2H1 = 20, 24
    sel = np.zeros((24, 128, 32), np.float32)
    for ci in range(16):
        sel[ci, :, M_Q2H + ci] = 1.0
        sel[ci, :, M_QTOT] = 1.0
    for ci in range(4):
        sel[16 + ci, :, M_K2H0 + ci] = 1.0
        sel[16 + ci, :, M_KT0] = 1.0
        sel[20 + ci, :, M_K2H1 + ci] = 1.0
        sel[20 + ci, :, M_KT1] = 1.0
    S["ssel"] = np.ascontiguousarray(sel.transpose(1, 0, 2))
    _SHARED_CACHE["key"] = key
    _SHARED_CACHE["val"] = S
    return S


def _prep_core(inputs, c, S):
    f32 = np.float32
    b, h = c // 2, c % 2
    own = [h + 2 * i for i in range(8)]
    x = np.asarray(inputs["L0_attn_x"], f32)[b]
    xt = x.reshape(16, 128, D)
    M = dict(S)
    M["xs0"] = np.ascontiguousarray(xt[own])
    xT = np.ascontiguousarray(x.T)
    M["xfm"] = xT.reshape(16, 128, T)
    M["xfmo"] = np.ascontiguousarray(
        xT.reshape(16, 128, 16, 128)[:, :, own, :].reshape(16, 128, 1024))
    owncols = np.concatenate([np.arange(a * 128, (a + 1) * 128) for a in own])
    co, so = _rope_tables(owncols)
    M["ropec_o"], M["ropes_o"] = co, so
    mask = np.zeros((8, 2, 128, 128), f32)
    for qt in range(8):
        for j in range(2):
            abs_s = (2 * qt + j) * 128 + np.arange(128)[:, None]
            abs_q = (2 * qt + h) * 128 + np.arange(128)[None, :]
            mask[qt, j] = np.where(abs_s <= abs_q, 0.0, NEG)
    M["maskd"] = np.ascontiguousarray(mask.transpose(3, 0, 1, 2))
    return M


_NC_CACHE = {}


def _get_nc(tap, upto):
    key = (tap, upto)
    if key not in _NC_CACHE:
        _NC_CACHE[key] = build(tap=tap, upto=upto)
    return _NC_CACHE[key]


def kernel(**inputs):
    tap = DEBUG_TAP
    nc = _get_nc(tap, None)
    S = _prep_shared(inputs)
    in_maps = [_prep_core(inputs, c, S) for c in range(8)]
    res = run_bass_kernel_spmd(nc, in_maps, core_ids=list(range(8)))
    if tap:
        return [np.asarray(r["tap_" + tap]) for r in res.results]
    out = np.empty((B, T, D), np.float32)
    for c in range(8):
        b, h = c // 2, c % 2
        o = np.asarray(res.results[c]["out"])
        for i in range(8):
            a = h + 2 * i
            out[b, a * 128:(a + 1) * 128, :] = o[i]
    return out


# version 14 (final)
# speedup vs baseline: 1.0423x; 1.0423x over previous
import math
from contextlib import ExitStack

import numpy as np
import ml_dtypes

import concourse.bass as bass
import concourse.mybir as mybir
from concourse.bass_utils import run_bass_kernel_spmd

F32 = mybir.dt.float32
BF16 = mybir.dt.bfloat16
AX = mybir.AxisListType.X
ALU = mybir.AluOpType
ACT = mybir.ActivationFunctionType

MAGIC = float(1.5 * 2**23)
NEG = -1e30

B, T, D = 4, 2048, 2048
NH, HD = 16, 128
I0 = 5632
H1 = 8192
ISQ = 1.0 / math.sqrt(HD)
INV2C = 0.5 * ISQ
KB = 1024

DEBUG_TAP = None
ENGINES = ("sync", "vector", "scalar", "tensor", "gpsimd")


CHAINED = ("vector", "scalar", "gpsimd")


class Plan:

    def __init__(self):
        self.q = {e: [] for e in ENGINES}
        self.cnt = {}
        self.last = {}
        self.phase = 0
        self.span = {}
        self.nbar = 0
        self.dmasems = set()
        self.swdge = set()

    def _touch(self, sem):
        sp = self.span.setdefault(sem, [self.phase, self.phase])
        sp[1] = self.phase

    def op(self, eng, emit, waits=(), inc=None, dma=False, chain=True):
        post = None
        waits = list(waits)
        if eng in CHAINED and chain:
            if eng in self.last:
                waits.append(self.last[eng])
            if inc is None:
                inc = ("_ch_" + eng, 1)
        if inc is not None:
            s, a = inc
            if dma:
                self.dmasems.add(s)
                prev = self.cnt.get(s, 0)
                if prev:
                    waits.append((s, prev))
            if eng == "gpsimd":
                self.swdge.add(s)
            self.cnt[s] = self.cnt.get(s, 0) + a
            post = self.cnt[s]
            self._touch(s)
            if eng in CHAINED and chain:
                self.last[eng] = (s, post)
        for ss, vv in waits:
            self._touch(ss)
        self.q[eng].append((tuple(waits), emit, inc, self.phase))
        return post

    def n(self, sem):
        return self.cnt.get(sem, 0)

    def barrier(self):
        self.nbar += 1
        tgt = 5 * self.nbar
        dw = [(s, self.n(s)) for s in sorted(self.dmasems) if self.n(s) > 0]
        self.op("sync", lambda en: en.nop(), dw, None)
        for e in ENGINES:
            self.op(e, lambda en: en.nop(), [], ("BAR", 1))
        self.phase += 1
        for e in ENGINES:
            self.op(e, lambda en: en.nop(), [("BAR", tgt)], None)

    def emit_all(self, nc, block, stack):
        persistent = [s for s, sp in self.span.items()
                      if sp[0] != sp[1] or s == "BAR" or s in self.swdge]
        by_phase = {}
        for s, sp in self.span.items():
            if s not in persistent:
                by_phase.setdefault(sp[0], []).append(s)
        nrecycle = max((len(v) for v in by_phase.values()), default=0)
        npool = len(persistent) + nrecycle
        avail = 104
        assert npool <= avail, (len(persistent), nrecycle)
        handles = {}
        base = {}
        phys = [stack.enter_context(nc.semaphore(f"S{i}")) for i in range(npool)]
        physcnt = [0] * npool
        for i, s in enumerate(persistent):
            handles[s] = phys[i]
            base[s] = 0
        for ph, names in by_phase.items():
            for j, s in enumerate(sorted(names)):
                k = len(persistent) + j
                handles[s] = phys[k]
                base[s] = physcnt[k]
                physcnt[k] += self.cnt[s]
        for eng in ENGINES:
            plan = self.q[eng]
            if not plan:
                continue

            def body(e, plan=plan):
                seen = {}
                for waits, emit, inc, ph in plan:
                    for ss, vv in waits:
                        v = vv + base[ss]
                        if vv > 0 and seen.get(ss, -1) < v:
                            e.wait_ge(handles[ss], v)
                            seen[ss] = v
                    inst = emit(e)
                    if inc is not None:
                        ss, a = inc
                        if inst is None:
                            inst = e.nop()
                        inst.then_inc(handles[ss], a)

            getattr(block, eng)(body)


class Arena:
    def __init__(self, ap):
        self.ap = ap

    def f32(self, off, *shape):
        n = int(np.prod(shape))
        return _shape(self.ap[:, off:off + 4 * n].bitcast(F32), shape)

    def bf(self, off, *shape):
        n = int(np.prod(shape))
        return _shape(self.ap[:, off:off + 2 * n].bitcast(BF16), shape)


def _shape(v, shape):
    if len(shape) <= 1:
        return v
    pat_in = "p (" + " ".join(f"d{i}" for i in range(len(shape))) + ")"
    pat_out = "p " + " ".join(f"d{i}" for i in range(len(shape)))
    kw = {f"d{i}": int(s) for i, s in enumerate(shape)}
    return v.rearrange(pat_in + " -> " + pat_out, **kw)


TAP_SHAPES = {
    "x1": [8, 128, D], "x2": [8, 128, D], "x3": [8, 128, D],
    "q0": [128, 16, 1024], "k0": [128, 4, T], "v0": [128, 16, 512],
    "enc0": [128, 16, 1024], "attn0": [8, 128, D],
}


def build(tap=None, upto=None):
    nc = bass.Bass("TRN2", target_bir_lowering=False)
    P = Plan()

    def din(name, shape):
        return nc.dram_tensor(name, shape, F32, kind="ExternalInput").ap()

    A = {}
    A["xs0"] = din("xs0", [8, 128, D])
    A["xfm"] = din("xfm", [16, 128, T])
    A["xfmo"] = din("xfmo", [16, 128, 1024])
    A["l0q"] = din("l0q", [16, 128, 16, 128])
    A["l0k"] = din("l0k", [4, 128, 16, 128])
    A["l0v"] = din("l0v", [16, 128, 512])
    A["l0o"] = din("l0o", [16, 128, 16, 128])
    A["l0w1"] = din("l0w1", [88, 128, D])
    A["l0w2"] = din("l0w2", [16, 128, I0])
    A["l1q"] = din("l1q", [16, 128, 16, 128])
    A["l1k"] = din("l1k", [16, 128, 16, 128])
    A["l1v"] = din("l1v", [16, 128, D])
    A["l1o"] = din("l1o", [16, 128, 16, 128])
    A["l1w1"] = din("l1w1", [64, 128, 16, 128])
    A["l1w3"] = din("l1w3", [64, 128, 16, 128])
    A["l1w2"] = din("l1w2", [4, 64, 128, 512])
    A["ident"] = din("ident", [128, 128])
    A["h128"] = din("h128", [128, 128])
    A["h64b"] = din("h64b", [128, 128])
    A["gqa"] = din("gqa", [4, 16])
    A["ropec_o"] = din("ropec_o", [128, 1024])
    A["ropes_o"] = din("ropes_o", [128, 1024])
    A["ropec_f"] = din("ropec_f", [128, T])
    A["ropes_f"] = din("ropes_f", [128, T])
    A["maskd"] = din("maskd", [128, 8, 2, 128])
    A["swp"] = din("swp", [128, 128])
    A["epsc"] = din("epsc", [128, 2])
    A["ssel"] = din("ssel", [128, 24, 32])

    A["out"] = nc.dram_tensor("out", [8, 128, D], F32, kind="ExternalOutput").ap()

    A["cc_k"] = nc.dram_tensor("cc_k", [16, 128, 1024], BF16)
    A["cc_v"] = nc.dram_tensor("cc_v", [8, 128, D], BF16)
    A["cc_s"] = nc.dram_tensor("cc_s", [16, 2], F32)
    A["gg_k"] = nc.dram_tensor("gg_k", [2, 16, 128, 1024], BF16)
    A["gg_v"] = nc.dram_tensor("gg_v", [2, 8, 128, D], BF16)
    A["gg_s"] = nc.dram_tensor("gg_s", [2, 16, 2], F32)
    A["xsp"] = nc.dram_tensor("xsp", [8, 128, D], F32)
    A["hsp"] = nc.dram_tensor("hsp", [44, 128, 1024], F32)

    A["tap"] = None
    A["tapname"] = tap
    if tap:
        A["tap"] = nc.dram_tensor("tap_" + tap, TAP_SHAPES[tap], F32,
                                  kind="ExternalOutput").ap()

    with ExitStack() as stack:
        AR = Arena(stack.enter_context(
            nc.sbuf_tensor("arena", [128, 207 * KB], mybir.dt.uint8)).ap())
        A["PA"] = stack.enter_context(nc.psum_tensor("PA", [128, 2048], F32)).ap()
        A["PB"] = stack.enter_context(nc.psum_tensor("PB", [128, 1024], F32)).ap()
        A["PC"] = stack.enter_context(nc.psum_tensor("PC", [128, 512], F32)).ap()
        A["PT"] = stack.enter_context(nc.psum_tensor("PT", [128, 1024], BF16)).ap()
        A["PAh"] = [A["PA"][:, 0:1024], A["PA"][:, 1024:2048]]

        block = stack.enter_context(nc.Block())
        emit_program(nc, P, AR, A, upto)
        P.emit_all(nc, block, stack)

    return nc


def emit_program(nc, P, AR, A, upto):
    PA, PB, PC, PT, PAh = A["PA"], A["PB"], A["PC"], A["PT"], A["PAh"]
    tap = A["tapname"]

    C = 184 * KB
    ident_f = AR.f32(C, 128); C += 512
    ident_b = AR.bf(C, 128); C += 256
    h128_b = AR.bf(C, 128); C += 256
    h64b_b = AR.bf(C, 128); C += 256
    onesmat_b = AR.bf(C, 128); C += 256
    onesmat_f = AR.f32(C, 128); C += 512
    gqa_f = AR.f32(C, 16); C += 64
    swp_b = AR.bf(C, 128); C += 256
    ssel_b = AR.bf(C, 24, 32); C += 1536
    sml = AR.f32(C, 128); C += 512
    scr2s = AR.f32(C, 128); C += 512
    mrow = AR.f32(C, 1024); C += 4 * KB
    scr = AR.f32(C, 1024); C += 4 * KB
    rowb = AR.f32(C, 2048); C += 8 * KB
    assert C <= 207 * KB, C
    ones_b = onesmat_b[:, 0:1]
    scrb = scr.bitcast(BF16)
    M_Q2H, M_QTOT, M_KT0, M_KT1 = 0, 16, 17, 18
    M_K2H0, M_K2H1 = 20, 24

    def load(dst, src, waits=(), q="sync", sem="ld"):
        return P.op(q, lambda e, d=dst, s=src: e.dma_start(out=d, in_=s), waits,
                    (sem, 16), dma=True)

    def vop(emit, waits=(), sem="vs", eng="vector"):
        return P.op(eng, emit, waits, (sem, 1) if sem else None)

    def mm(out, lhsT, rhs, start, stop, waits=(), sem=None):
        return P.op("tensor",
                    lambda e, o=out, l=lhsT, r=rhs, st=start, sp=stop:
                    e.matmul(o, l, r, start=st, stop=sp),
                    waits, (sem, 1) if sem else None)

    def transpose(out, in_, idn, waits=(), sem=None):
        return P.op("tensor", lambda e, o=out, i=in_, d=idn: e.transpose(o, i, d),
                    waits, (sem, 1) if sem else None)

    t1 = load(ident_f[:, :], A["ident"], sem="dc")
    t2 = load(mrow[:, 0:128], A["h128"], sem="dc")
    t3 = load(mrow[:, 128:256], A["h64b"], sem="dc")
    t4 = load(sml[0:4, 16:32], A["gqa"], sem="dc")
    t5 = load(sml[:, 48:50], A["epsc"], sem="dc")
    vop(lambda e: e.memset(onesmat_b[:, :], 1.0), [], sem=None)
    vop(lambda e: e.memset(onesmat_f[:, :], 1.0), [], sem=None)
    vop(lambda e: e.tensor_copy(ident_b[:, :], ident_f[:, :]), [("dc", t1)], sem=None)
    vop(lambda e: e.tensor_copy(h128_b[:, :], mrow[:, 0:128]), [("dc", t2)], sem=None)
    vop(lambda e: e.tensor_copy(gqa_f[0:4, :], sml[0:4, 16:32]), [("dc", t4)], sem=None)
    t6 = load(scr[:, 0:768], A["ssel"].rearrange("p v c -> p (v c)"), sem="dc")
    vop(lambda e: e.tensor_copy(ssel_b[:, :, :], _shape(scr[:, 0:768], (24, 32))),
        [("dc", t6)], sem=None)
    c_done = vop(lambda e: e.tensor_copy(h64b_b[:, :], mrow[:, 128:256]),
                 [("dc", t3), ("dc", t5)])

    xfm_b = AR.bf(0, 16, T)
    q_b = AR.bf(64 * KB, 16, 1024)
    k_b = AR.bf(96 * KB, 4, T)
    v_b = AR.bf(112 * KB, 16, 512)
    xo_b = AR.bf(128 * KB, 16, 1024)
    stg = [AR.f32(160 * KB + i * 8 * KB, 2048) for i in range(2)]
    wcol = [AR.bf(176 * KB + i * 4 * KB, 16, 128) for i in range(2)]

    for dk in range(16):
        w = [("vs", c_done)] if dk < 2 else [("xfmc", dk - 1)]
        t = load(stg[dk % 2][:, :], A["xfm"][dk], w, sem=f"ds{dk % 2}")
        vop(lambda e, dk=dk, s=stg[dk % 2]: e.tensor_copy(xfm_b[:, dk, :], s[:, :]),
            [(f"ds{dk % 2}", t)], sem="xfmc")
    for dk in range(16):
        w = [("xfmc", 16)] if dk < 2 else [("xoc", dk - 1)]
        t = load(stg[dk % 2][:, 0:1024], A["xfmo"][dk], w, sem=f"ds{dk % 2}")
        vop(lambda e, dk=dk, s=stg[dk % 2]: e.tensor_copy(xo_b[:, dk, :], s[:, 0:1024]),
            [(f"ds{dk % 2}", t)], sem="xoc")

    for ci in range(16):
        w = [("xoc", 16)] if ci < 2 else [("qw", ci - 1), ("qmm", ci - 1)]
        t = load(stg[ci % 2][:, :], A["l0q"][ci].rearrange("p a b -> p (a b)"), w,
                 sem=f"ds{ci % 2}")
        vop(lambda e, ci=ci, s=stg[ci % 2]:
            e.tensor_copy(wcol[ci % 2][:, :, :], _shape(s[:, :], (16, 128))),
            [(f"ds{ci % 2}", t)], sem="qw", eng="gpsimd")
    for ci in range(16):
        wv = wcol[ci % 2]
        half = PAh[ci % 2]
        for dk in range(16):
            for j in range(2):
                waits = []
                if dk == 0 and j == 0:
                    waits = [("qw", ci + 1)] + ([("qev", ci - 1)] if ci >= 2 else [])
                mm(half[:, j * 512:(j + 1) * 512], wv[:, dk, :],
                   xo_b[:, dk, j * 512:(j + 1) * 512], dk == 0, dk == 15,
                   waits, "qmm" if (dk == 15 and j == 1) else None)
        mmn = P.n("qmm")
        sq = vop(lambda e, half=half: e.activation(scrb[:, 0:1024], half, ACT.Square),
                 [("qmm", mmn)], sem="ss", eng="scalar")
        for j in range(2):
            mm(PB[0:30, j * 512:(j + 1) * 512], ssel_b[:, ci, 0:30],
               scrb[:, j * 512:(j + 1) * 512], ci == 0, False, [("ss", sq)],
               "qst" if (ci == 15 and j == 1) else None)
        vop(lambda e, ci=ci, half=half: e.tensor_copy(q_b[:, ci, :], half),
            [("ss", sq)], sem="qev")

    for ci in range(4):
        w = [("qev", 16), ("qmm", 16)] if ci < 2 else [("kw", ci - 1), ("kmm", ci - 1)]
        t = load(stg[ci % 2][:, :], A["l0k"][ci].rearrange("p a b -> p (a b)"), w,
                 sem=f"ds{ci % 2}")
        vop(lambda e, ci=ci, s=stg[ci % 2]:
            e.tensor_copy(wcol[ci % 2][:, :, :], _shape(s[:, :], (16, 128))),
            [(f"ds{ci % 2}", t)], sem="kw", eng="gpsimd")
    for ci in range(4):
        wv = wcol[ci % 2]
        for dk in range(16):
            for j in range(4):
                waits = []
                if dk == 0 and j == 0:
                    waits = [("kw", ci + 1)] + ([("kev", ci)] if ci >= 1 else [])
                mm(PA[:, j * 512:(j + 1) * 512], wv[:, dk, :],
                   xfm_b[:, dk, j * 512:(j + 1) * 512], dk == 0, dk == 15,
                   waits, "kmm" if (dk == 15 and j == 3) else None)
        mmn = P.n("kmm")
        sq = vop(lambda e: e.activation(scrb[:, 0:2048], PA[:, :], ACT.Square),
                 [("kmm", mmn)], sem="ss", eng="scalar")
        for j in range(4):
            v = (16 + ci) if j < 2 else (20 + ci)
            colj = j % 2
            mm(PB[0:30, colj * 512:(colj + 1) * 512], ssel_b[:, v, 0:30],
               scrb[:, j * 512:(j + 1) * 512], False, ci == 3 and j >= 2,
               [("ss", sq)], "kst" if (ci == 3 and j == 3) else None)
        vop(lambda e, ci=ci: e.tensor_copy(k_b[:, ci, :], PA[:, :]),
            [("ss", sq)], sem="kev")

    l0v_b = AR.bf(128 * KB, 16, 512)
    for dk in range(16):
        w = [("kev", 4)] if dk < 2 else [("vw", dk - 1)]
        t = load(stg[dk % 2][:, 0:512], A["l0v"][dk], w, sem=f"ds{dk % 2}")
        vop(lambda e, dk=dk, s=stg[dk % 2]: e.tensor_copy(l0v_b[:, dk, :], s[:, 0:512]),
            [(f"ds{dk % 2}", t)], sem="vw", eng="gpsimd")
    for st in range(16):
        slot = PA[:, (st % 4) * 512:(st % 4 + 1) * 512]
        for dk in range(16):
            waits = []
            if dk == 0:
                waits = [("vw", 16)] + ([("vev", st - 3)] if st >= 4 else [])
            mm(slot, xfm_b[:, dk, st * 128:(st + 1) * 128], l0v_b[:, dk, :],
               dk == 0, dk == 15, waits, "vmm" if dk == 15 else None)
        mmn = P.n("vmm")
        vop(lambda e, st=st, slot=slot: e.tensor_copy(v_b[:, st, :], slot),
            [("vmm", mmn)], sem="vev")

    stev = vop(lambda e: e.tensor_copy(mrow[0:28, 0:1024], PB[0:28, :]),
               [("qst", 1), ("kst", 1)], sem="vs")

    def rowmove(dst, src, waits, sem="dm"):
        return P.op("gpsimd", lambda e, d=dst, s=src: e.dma_start(out=d, in_=s),
                    waits, (sem, 16), dma=True)

    mv1 = rowmove(rowb[0:1, 0:1024], mrow[M_QTOT:M_QTOT + 1, 0:1024], [("vs", stev)])
    s1 = vop(lambda e: e.activation(rowb[0:1, 0:1024], rowb[0:1, 0:1024],
             ACT.Sqrt, bias=sml[0:1, 48:49], scale=1.0 / D), [("dm", mv1)],
             sem="ss", eng="scalar")
    r1 = vop(lambda e: e.reciprocal(rowb[0:1, 0:1024], rowb[0:1, 0:1024]),
             [("ss", s1)], sem="vs")

    tq1 = load(stg[0][:, 0:1024], A["ropec_o"], [("vev", 16)], sem="ds0")
    tq2 = load(stg[0][:, 1024:2048], A["ropes_o"], [("vev", 16)], sem="ds1")
    mm(PB[:, 0:512], onesmat_f[0:1, :], rowb[0:1, 0:512], True, True, [("vs", r1)])
    bq = mm(PB[:, 512:1024], onesmat_f[0:1, :], rowb[0:1, 512:1024], True, True,
            [], "mmx")
    bqn = P.n("mmx")
    rc1 = vop(lambda e: e.tensor_mul(stg[0][:, 0:1024], stg[0][:, 0:1024], PB[:, 0:1024]),
              [("ds0", tq1), ("mmx", bqn)], sem="vs")
    rc2 = vop(lambda e: e.tensor_mul(stg[0][:, 1024:2048], stg[0][:, 1024:2048], PB[:, 0:1024]),
              [("ds1", tq2), ("mmx", bqn)], sem="vs")
    rcq, rsq_t = stg[0][:, 0:1024], stg[0][:, 1024:2048]

    rsq = vop(lambda e: e.tensor_mul(rowb[0:1, 0:1024], rowb[0:1, 0:1024],
              rowb[0:1, 0:1024]), [("mmx", bqn)], sem="vs")
    b1a = mm(PB[0:16, 0:512], onesmat_f[0:1, 0:16], rowb[0:1, 0:512], True, True,
             [("vs", rsq), ("vs", stev), ("vs", rc2)])
    b1b = mm(PB[0:16, 512:1024], onesmat_f[0:1, 0:16], rowb[0:1, 512:1024], True, True,
             [], "mmx")
    mq = vop(lambda e: e.tensor_mul(scr[0:16, 0:1024], mrow[M_Q2H:M_Q2H + 16, 0:1024],
             PB[0:16, 0:1024]), [("mmx", P.n("mmx"))], sem="vs")
    mq2 = vop(lambda e: e.tensor_copy(mrow[M_Q2H:M_Q2H + 16, 0:1024], scr[0:16, 0:1024]),
              [("vs", mq)], sem="vs")
    q2m = vop(lambda e: e.nop(), [("vs", mq2)], sem="vs")

    tsw = load(scr2s[:, 0:64], A["swp"][:, 0:64], [], sem="dc")
    tsw2 = load(scr2s[:, 64:128], A["swp"][:, 64:128], [], sem="dc")
    csw = vop(lambda e: e.tensor_copy(swp_b[:, :], scr2s[:, 0:128]),
              [("dc", tsw2)], sem="vs")

    for ci in range(16):
        half = PAh[ci % 2]
        w = [("vs", rc1), ("vs", rc2), ("vs", csw), ("vs", q2m)] if ci < 2 \
            else [("qrope", ci - 1)]
        for j in range(2):
            mm(half[:, j * 512:(j + 1) * 512], swp_b[:, :],
               q_b[:, ci, j * 512:(j + 1) * 512], True, True,
               w if j == 0 else (), "mmq" if j == 1 else None)
        mmn = P.n("mmq")
        vop(lambda e, ci=ci: e.tensor_mul(scr[:, 0:1024], q_b[:, ci, :], rcq),
            [("mmq", mmn)], sem=None)
        vop(lambda e, ci=ci, half=half: e.tensor_mul(q_b[:, ci, :], half, rsq_t),
            [], sem=None)
        vop(lambda e, ci=ci: e.tensor_add(q_b[:, ci, :], scr[:, 0:1024], q_b[:, ci, :]),
            [], sem="qrope")

    mv2 = rowmove(rowb[0:1, 0:1024], mrow[M_KT0:M_KT0 + 1, 0:1024],
                  [("vs", q2m), ("vs", rc2)])
    mv3 = rowmove(rowb[0:1, 1024:2048], mrow[M_KT1:M_KT1 + 1, 0:1024], [])
    s2 = vop(lambda e: e.activation(rowb[0:1, 0:1024], rowb[0:1, 0:1024],
             ACT.Sqrt, bias=sml[0:1, 48:49], scale=1.0 / 512), [("dm", mv2)],
             sem="ss", eng="scalar")
    s3 = vop(lambda e: e.activation(rowb[0:1, 1024:2048], rowb[0:1, 1024:2048],
             ACT.Sqrt, bias=sml[0:1, 48:49], scale=1.0 / 512), [("dm", mv3)],
             sem="ss", eng="scalar")
    r2 = vop(lambda e: e.reciprocal(rowb[0:1, 0:1024], rowb[0:1, 0:1024]),
             [("ss", s2)], sem="vs")
    r3 = vop(lambda e: e.reciprocal(rowb[0:1, 1024:2048], rowb[0:1, 1024:2048]),
             [("ss", s3)], sem="vs")

    tk1 = load(stg[0][:, :], A["ropec_f"], [("qrope", 16)], sem="ds0")
    tk2 = load(stg[1][:, :], A["ropes_f"], [("qrope", 16)], sem="ds1")
    for j in range(2):
        rkr = rowb[0:1, j * 1024:(j + 1) * 1024]
        mm(PB[:, 0:512], onesmat_f[0:1, :], rkr[:, 0:512], True, True,
           [("vs", r2), ("vs", r3)] if j == 0 else [("krope_t", 2 * j)])
        mm(PB[:, 512:1024], onesmat_f[0:1, :], rkr[:, 512:1024], True, True, [], "mmx")
        bn = P.n("mmx")
        vop(lambda e, j=j: e.tensor_mul(stg[0][:, j * 1024:(j + 1) * 1024],
            stg[0][:, j * 1024:(j + 1) * 1024], PB[:, 0:1024]),
            [("ds0", tk1), ("ds1", tk2), ("mmx", bn)], sem="krope_t")
        vop(lambda e, j=j: e.tensor_mul(stg[1][:, j * 1024:(j + 1) * 1024],
            stg[1][:, j * 1024:(j + 1) * 1024], PB[:, 0:1024]),
            [("mmx", bn)], sem="krope_t")
    rck, rsk = stg[0], stg[1]

    rks = vop(lambda e: e.tensor_mul(rowb[0:1, 0:2048], rowb[0:1, 0:2048],
              rowb[0:1, 0:2048]), [("krope_t", 4)], sem="vs")
    mv4 = rowmove(mrow[64:68, 0:1024], mrow[M_K2H0:M_K2H0 + 4, 0:1024],
                  [("vs", q2m)])
    mv5 = rowmove(mrow[32:36, 0:1024], mrow[M_K2H1:M_K2H1 + 4, 0:1024],
                  [("vs", q2m)])
    mm(PB[64:68, 0:512], onesmat_f[0:1, 0:4], rowb[0:1, 0:512], True, True,
       [("vs", rks), ("krope_t", 4)])
    b2 = mm(PB[64:68, 512:1024], onesmat_f[0:1, 0:4], rowb[0:1, 512:1024], True,
            True, [], "mmx")
    mk0 = vop(lambda e: e.tensor_mul(scr[64:68, 0:1024], mrow[64:68, 0:1024],
              PB[64:68, 0:1024]), [("mmx", P.n("mmx")), ("dm", mv4)], sem="vs")
    k2a = vop(lambda e: e.reduce_max(sml[64:68, 1:2], scr[64:68, 0:1024], AX),
              [("vs", mk0)], sem="vs")
    b3 = mm(PB[32:36, 0:512], onesmat_f[0:1, 0:4], rowb[0:1, 1024:1536], True, True,
            [("vs", k2a)])
    b3b = mm(PB[32:36, 512:1024], onesmat_f[0:1, 0:4], rowb[0:1, 1536:2048], True,
             True, [], "mmx")
    mk1 = vop(lambda e: e.tensor_mul(scr[32:36, 0:1024], mrow[32:36, 0:1024],
              PB[32:36, 0:1024]), [("mmx", P.n("mmx")), ("dm", mv5)], sem="vs")
    k2b = vop(lambda e: e.reduce_max(sml[32:36, 2:3], scr[32:36, 0:1024], AX),
              [("vs", mk1)], sem="vs")
    mv6 = rowmove(sml[64:68, 2:3], sml[32:36, 2:3], [("vs", k2b)])
    k2m0 = vop(lambda e: e.tensor_max(sml[64:68, 1:2], sml[64:68, 1:2],
               sml[64:68, 2:3]), [("dm", mv6)], sem="vs")
    mv7 = rowmove(sml[0:4, 1:2], sml[64:68, 1:2], [("vs", k2m0)])
    k2m = vop(lambda e: e.nop(), [("dm", mv7)], sem="vs")
    g1 = mm(PC[0:16, 0:1], gqa_f[0:4, 0:16], sml[0:4, 1:2], True, True,
            [("vs", k2m)], "mmx")
    e1 = vop(lambda e: e.tensor_copy(sml[0:16, 3:4], PC[0:16, 0:1]),
             [("mmx", P.n("mmx"))], sem="vs")
    e3 = vop(lambda e: e.tensor_scalar(sml[0:16, 3:4], sml[0:16, 3:4], -INV2C, None,
             ALU.mult), [("vs", e1)], sem="vs")
    tr1 = transpose(PC[0:1, 0:16], sml[0:16, 3:4], ident_f[0:16, 0:16],
                    [("vs", e3)], "mmx")
    e4 = vop(lambda e: e.tensor_copy(sml[0:1, 4:20], PC[0:1, 0:16]),
             [("mmx", P.n("mmx"))], sem="vs")
    g2 = mm(PC[0:128, 0:16], onesmat_f[0:1, :], sml[0:1, 4:20], True, True,
            [("vs", e4)], "mmx")
    Mb = sml[:, 32:48]
    e5 = vop(lambda e: e.tensor_copy(sml[:, 32:48], PC[:, 0:16]),
             [("mmx", P.n("mmx"))], sem="vs")

    for ci in range(4):
        w = [("krope_t", 4)] if ci == 0 else [("krope", 2 * ci)]
        for j in range(4):
            mm(PA[:, j * 512:(j + 1) * 512], swp_b[:, :],
               k_b[:, ci, j * 512:(j + 1) * 512], True, True,
               w if j == 0 else (), "mmq" if j == 3 else None)
        mmn = P.n("mmq")
        for j in range(2):
            vop(lambda e, ci=ci, j=j: e.tensor_mul(scr[:, 0:1024],
                k_b[:, ci, j * 1024:(j + 1) * 1024], rck[:, j * 1024:(j + 1) * 1024]),
                [("mmq", mmn)] if j == 0 else [], sem=None)
            vop(lambda e, ci=ci, j=j: e.tensor_mul(k_b[:, ci, j * 1024:(j + 1) * 1024],
                PA[:, j * 1024:(j + 1) * 1024], rsk[:, j * 1024:(j + 1) * 1024]),
                [], sem=None)
            vop(lambda e, ci=ci, j=j: e.tensor_add(k_b[:, ci, j * 1024:(j + 1) * 1024],
                scr[:, 0:1024], k_b[:, ci, j * 1024:(j + 1) * 1024]), [], sem="krope")

    if tap in ("q0", "k0", "v0"):
        src = {"q0": q_b, "k0": k_b, "v0": v_b}[tap]
        w = {"q0": ("qrope", 16), "k0": ("krope", 8), "v0": ("vev", 16)}[tap]
        nch, width = TAP_SHAPES[tap][1], TAP_SHAPES[tap][2]
        for ci in range(nch):
            for j0 in range(0, width, 1024):
                jw = min(1024, width - j0)
                vop(lambda e, ci=ci, j0=j0, jw=jw, src=src:
                    e.tensor_copy(mrow[:, 0:jw], src[:, ci, j0:j0 + jw]), [w], sem="tapv")
                P.op("scalar", lambda e, ci=ci, j0=j0, jw=jw:
                     e.dma_start(out=A["tap"][:, ci, j0:j0 + jw], in_=mrow[:, 0:jw]),
                     [("tapv", P.n("tapv"))], ("st", 16), dma=True)
        P.op("sync", lambda e: e.nop(), [("st", P.n("st"))], ("fin", 1))
        return

    pt_b = AR.bf(128 * KB, 16, 1024)
    p_sb = AR.bf(120 * KB, 2048)
    enc_b = AR.bf(0, 16, 1024)
    negm_t = AR.f32(160 * KB, 8, 16)
    p_sb = [AR.bf(162 * KB, 2048), AR.bf(166 * KB, 2048)]
    tr = transpose
    for qt in range(8):
        tr(PC[:, 0:16], mrow[M_Q2H:M_Q2H + 16, qt * 128:(qt + 1) * 128],
           ident_f[0:16, 0:16], [("vs", e5)] if qt == 0 else [("nmt", qt)], "mmx")
        vop(lambda e, qt=qt: e.scalar_tensor_tensor(negm_t[:, qt, :],
            PC[:, 0:16], -INV2C, Mb[:, 0:16], ALU.mult, ALU.add),
            [("mmx", P.n("mmx"))], sem="nmt")
    nmtn = P.n("nmt")

    lasttr = {}
    for qh in range(16):
        kv = qh // 4
        base = P.n("exp0")
        for qt in range(8):
            for j in range(4):
                w = []
                if j == 0:
                    if qh == 0 and qt == 0:
                        w = [("krope", 8), ("qrope", 16), ("nmt", nmtn)]
                    else:
                        w = [("exp0", P.n("exp0"))]
                    if qt == 0 and qh:
                        w.append(("bcd", qh))
                mm(PA[:, j * 512:(j + 1) * 512], q_b[:, qh, qt * 128:(qt + 1) * 128],
                   k_b[:, kv, j * 512:(j + 1) * 512], True, True, w,
                   "smm" if j == 3 else None)
            smm = P.n("smm")
            ew = [("smm", smm)] + ([("avmm", qh)] if (qh and qt == 0) else [])
            if (qt % 2) in lasttr:
                ew.append(("ptr", lasttr[qt % 2]))
            vop(lambda e, qh=qh, qt=qt: e.activation(p_sb[qt % 2][:, :], PA[:, :],
                ACT.Exp, bias=negm_t[:, qt, qh:qh + 1], scale=ISQ,
                accum_out=sml[:, 50 + qt:51 + qt]),
                ew, sem="exp0", eng="scalar")
            expn = P.n("exp0")
            for w8 in range(2):
                for i in range(8):
                    st = w8 * 8 + i
                    tr(PT[:, i * 128:(i + 1) * 128],
                       p_sb[qt % 2][:, st * 128:(st + 1) * 128],
                       ident_b[:, :],
                       [("exp0", expn), ("ptw", P.n("ptw"))] if i == 0 else [],
                       "ptr" if i == 7 else None)
                trn = P.n("ptr")
                pw = [("ptr", trn)] + ([("avmm", qh)] if (qt == 0 and w8 == 0 and qh)
                                       else [])
                vop(lambda e, qt=qt, w8=w8: e.tensor_copy(
                    pt_b[:, w8 * 8:(w8 + 1) * 8, qt * 128:(qt + 1) * 128],
                    _shape(PT[:, 0:1024], (8, 128))), pw, sem="ptw")
            lasttr[qt % 2] = P.n("ptr")
        for qt in range(8):
            tr(PC[0:1, 0:128], sml[:, 50 + qt:51 + qt], ident_f[:, :],
               [("ptw", P.n("ptw"))] if qt == 0 else [("rcd", P.n("rcd"))], "mmx")
            vop(lambda e, qt=qt: e.tensor_copy(rowb[0:1, qt * 128:(qt + 1) * 128],
                PC[0:1, 0:128]), [("mmx", P.n("mmx"))], sem="rcd")
        rc2_ = vop(lambda e: e.reciprocal(rowb[0:1, 0:1024], rowb[0:1, 0:1024]),
                   [], sem="rcd")
        mm(PAh[0][:, 0:512], onesmat_f[0:1, :], rowb[0:1, 0:512], True, True,
           [("rcd", P.n("rcd"))])
        bc = mm(PAh[0][:, 512:1024], onesmat_f[0:1, :], rowb[0:1, 512:1024],
                True, True, [], "mmx")
        bcn = P.n("mmx")
        ev = vop(lambda e: e.tensor_copy(scr[:, 0:1024], PAh[0]), [("mmx", bcn)],
                 sem="bcd")
        for st in range(16):
            for j in range(2):
                aw = []
                if st == 0 and j == 0:
                    aw = [("encd", qh)] if qh else []
                mm(PB[:, j * 512:(j + 1) * 512], v_b[:, st, kv * 128:(kv + 1) * 128],
                   pt_b[:, st, j * 512:(j + 1) * 512], st == 0, st == 15,
                   aw, "avmm" if (st == 15 and j == 1) else None)
        avn = P.n("avmm")
        vop(lambda e, qh=qh: e.tensor_mul(enc_b[:, qh, :], PB[:, 0:1024], scr[:, 0:1024]),
            [("bcd", P.n("bcd")), ("avmm", avn)], sem="encd")

    if tap == "enc0":
        for ci in range(16):
            vop(lambda e, ci=ci: e.tensor_copy(mrow[:, 0:1024], enc_b[:, ci, :]),
                [("encd", 16)], sem="tapv")
            P.op("scalar", lambda e, ci=ci: e.dma_start(out=A["tap"][:, ci, :],
                 in_=mrow[:, 0:1024]), [("tapv", P.n("tapv"))], ("st", 16), dma=True)
        P.op("sync", lambda e: e.nop(), [("st", P.n("st"))], ("fin", 1))
        return

    stream = AR.f32(32 * KB, 8, 2048)
    for tt in range(8):
        t = load(stream[:, tt, :], A["xs0"][tt], [("encd", 16)] if tt == 0 else [],
                 sem="dx")
    xldn = P.n("dx")

    emit_proj_ln(nc, P, AR, A, dict(
        wsrc="l0o", enc=enc_b, stream=stream, stg=stg, wcol=wcol,
        ident_f=ident_f, mrow=mrow, scr=scr, scr2s=scr2s, sml=sml,
        pre_wait=[("encd", 16)], stt_extra=[("dx", xldn)],
        sem_prefix="o0", nchunks=16), upto_tap=(tap == "x1" or tap == "attn0"),
        tapname=tap)

    if tap in ("x1", "attn0"):
        for tt in range(8):
            P.op("scalar", lambda e, tt=tt: e.dma_start(out=A["tap"][tt],
                 in_=stream[:, tt, :]), [("o0fin", P.n("o0fin"))], ("st", 16), dma=True)
        P.op("sync", lambda e: e.nop(), [("st", P.n("st"))], ("fin", 1))
        return

    P.barrier()
    env = dict(ident_f=ident_f, ident_b=ident_b, onesmat_f=onesmat_f,
               onesmat_b=onesmat_b, ones_b=ones_b, sml=sml, scr=scr, scr2s=scr2s,
               mrow=mrow, rowb=rowb, stream=stream, stg=stg, wcol=wcol,
               h128_b=h128_b, h64b_b=h64b_b, Mb=Mb, gqa_f=gqa_f, ssel_b=ssel_b)
    emit_phase2(nc, P, AR, A, env)
    if tap == "x2":
        for tt in range(8):
            P.op("scalar", lambda e, tt=tt: e.dma_start(out=A["tap"][tt],
                 in_=stream[:, tt, :]), [("p2fin", P.n("p2fin"))], ("st", 16), dma=True)
        P.op("sync", lambda e: e.nop(), [("st", P.n("st"))], ("fin", 1))
        return

    P.barrier()
    emit_phase34(nc, P, AR, A, env)
    if tap == "x3":
        for tt in range(8):
            P.op("scalar", lambda e, tt=tt: e.dma_start(out=A["tap"][tt],
                 in_=stream[:, tt, :]), [("o1fin", P.n("o1fin"))], ("st", 16), dma=True)
        P.op("sync", lambda e: e.nop(), [("st", P.n("st"))], ("fin", 1))
        return

    P.barrier()
    emit_phase5(nc, P, AR, A, env)


def emit_ln(P, A, ctx, pfx, stream, partials, npart):
    mrow, sml = ctx["mrow"], ctx["sml"]
    scr2s = ctx["scr2s"]

    def vop(emit, waits=(), sem="vs", eng="vector"):
        return P.op(eng, emit, waits, (sem, 1) if sem else None)

    for tt in range(8):
        vop(lambda e, tt=tt: e.reduce_sum(sml[:, 0:1],
            scr2s[:, tt * npart:tt * npart + npart], AX), [], sem=pfx + "ln")
        vop(lambda e, tt=tt: e.activation(mrow[:, 0:1024], stream[:, tt, 0:1024],
            ACT.Square, accum_out=sml[:, 1:2]), [(pfx + "ln", P.n(pfx + "ln"))],
            sem=pfx + "ln", eng="scalar")
        vop(lambda e, tt=tt: e.activation(mrow[:, 0:1024], stream[:, tt, 1024:2048],
            ACT.Square, accum_out=sml[:, 9:10]), [], sem=pfx + "ln", eng="scalar")
        vop(lambda e: e.tensor_add(sml[:, 1:2], sml[:, 1:2], sml[:, 9:10]),
            [(pfx + "ln", P.n(pfx + "ln"))], sem=pfx + "ln")
        vop(lambda e: e.tensor_scalar(sml[:, 2:3], sml[:, 0:1], 1.0 / D, None, ALU.mult),
            [], sem=None)
        vop(lambda e: e.tensor_scalar(sml[:, 3:4], sml[:, 1:2], 1.0 / D, None, ALU.mult),
            [], sem=None)
        vop(lambda e: e.tensor_mul(sml[:, 4:5], sml[:, 2:3], sml[:, 2:3]), [], sem=None)
        vop(lambda e: e.tensor_sub(sml[:, 5:6], sml[:, 3:4], sml[:, 4:5]),
            [], sem=pfx + "ln")
        vop(lambda e: e.activation(sml[:, 6:7], sml[:, 5:6], ACT.Sqrt,
            bias=sml[:, 49:50]), [(pfx + "ln", P.n(pfx + "ln"))],
            sem=pfx + "ln", eng="scalar")
        vop(lambda e: e.reciprocal(sml[:, 7:8], sml[:, 6:7]),
            [(pfx + "ln", P.n(pfx + "ln"))], sem=pfx + "ln")
        vop(lambda e, tt=tt: e.tensor_scalar(stream[:, tt, :], stream[:, tt, :],
            sml[:, 2:3], sml[:, 7:8], ALU.subtract, ALU.mult),
            [(pfx + "ln", P.n(pfx + "ln"))], sem=pfx + "fin")


def emit_proj_ln(nc, P, AR, A, ctx, upto_tap=False, tapname=None):
    PA, PB, PC, PAh = A["PA"], A["PB"], A["PC"], A["PAh"]
    pfx = ctx["sem_prefix"]
    enc, stream, stg, wcol = ctx["enc"], ctx["stream"], ctx["stg"], ctx["wcol"]
    ident_f, mrow, scr, scr2, sml = (ctx["ident_f"], ctx["mrow"], ctx["scr"],
                                     ctx["scr2s"], ctx["sml"])
    wsrc = A[ctx["wsrc"]]

    def load(dst, src, waits=(), sem="ld"):
        return P.op("sync", lambda e, d=dst, s=src: e.dma_start(out=d, in_=s),
                    waits, (sem, 16), dma=True)

    def vop(emit, waits=(), sem="vs", eng="vector"):
        return P.op(eng, emit, waits, (sem, 1) if sem else None)

    def mm(out, lhsT, rhs, start, stop, waits=(), sem=None):
        return P.op("tensor", lambda e, o=out, l=lhsT, r=rhs, s1=start, s2=stop:
                    e.matmul(o, l, r, start=s1, stop=s2), waits,
                    (sem, 1) if sem else None)

    for oc in range(16):
        w = list(ctx["pre_wait"]) if oc < 2 else [(pfx + "w", oc - 1),
                                                   (pfx + "mm", oc - 1)]
        t = load(stg[oc % 2][:, :], wsrc[oc].rearrange("p a b -> p (a b)"), w,
                 sem=f"ds{oc % 2}")
        vop(lambda e, oc=oc, s=stg[oc % 2]:
            e.tensor_copy(wcol[oc % 2][:, :, :], _shape(s[:, :], (16, 128))),
            [(f"ds{oc % 2}", t)], sem=pfx + "w", eng="gpsimd")
    for oc in range(16):
        wv = wcol[oc % 2]
        half = PAh[oc % 2]
        for eh in range(16):
            waits = []
            if eh == 0:
                waits = [(pfx + "w", oc + 1)] + ([(pfx + "ev", oc - 1)] if oc >= 2 else [])
            for j in range(2):
                mm(half[:, j * 512:(j + 1) * 512], wv[:, eh, :],
                   enc[:, eh, j * 512:(j + 1) * 512], eh == 0, eh == 15,
                   waits if (eh == 0 and j == 0) else (),
                   pfx + "mm" if (eh == 15 and j == 1) else None)
        mmn = P.n(pfx + "mm")
        ev = vop(lambda e, half=half: e.tensor_copy(scr[:, 0:1024], half),
                 [(pfx + "mm", mmn), (pfx + "tr", oc)], sem=pfx + "ev")
        for tt in range(8):
            P.op("tensor", lambda e, tt=tt: e.transpose(
                PB[:, tt * 128:(tt + 1) * 128], scr[:, tt * 128:(tt + 1) * 128],
                ident_f[:, :]),
                [(pfx + "ev", P.n(pfx + "ev")), (pfx + "st", 8 * oc)] if tt == 0 else [],
                (pfx + "tr", 1) if tt == 7 else None)
        trn = P.n(pfx + "tr")
        for tt in range(8):
            w = [(pfx + "tr", trn)] + (list(ctx.get("stt_extra", ())) if oc == 0 else [])
            vop(lambda e, oc=oc, tt=tt: e.scalar_tensor_tensor(
                stream[:, tt, oc * 128:(oc + 1) * 128], PB[:, tt * 128:(tt + 1) * 128],
                1.0, stream[:, tt, oc * 128:(oc + 1) * 128], ALU.mult, ALU.add,
                accum_out=scr2[:, tt * 16 + oc:tt * 16 + oc + 1]),
                w if tt == 0 else [], sem=pfx + "st")
    stn = P.n(pfx + "st")
    P.op("vector", lambda e: e.nop(), [(pfx + "st", stn)], (pfx + "lngate", 1))
    emit_ln(P, A, ctx, pfx, stream, None, 16)


def emit_phase2(nc, P, AR, A, env):
    PA, PB, PC, PT, PAh = A["PA"], A["PB"], A["PC"], A["PT"], A["PAh"]
    ident_f, ident_b = env["ident_f"], env["ident_b"]
    onesmat_f, sml, scr = env["onesmat_f"], env["sml"], env["scr"]
    scr2s, mrow, rowb, stream = env["scr2s"], env["mrow"], env["rowb"], env["stream"]
    MQ = MAGIC

    xq_fm = AR.bf(0, 16, 1024)
    xqt = AR.bf(96 * KB, 2048)
    h_fm = AR.bf(96 * KB, 44, 1024)
    stg2 = [AR.f32(32 * KB + i * 8 * KB, 2048) for i in range(2)]
    w1col = [AR.bf(48 * KB + i * 4 * KB, 16, 128) for i in range(2)] + \
            [AR.bf(86 * KB + i * 4 * KB, 16, 128) for i in range(2)]
    sws = AR.f32(56 * KB, 176)
    sxb = AR.f32(57 * KB, 1024)
    rshb = AR.f32(61 * KB, 1024)
    w1int = [AR.bf(65 * KB + i * 4 * KB, 2048) for i in range(2)]
    qscr = AR.f32(74 * KB, 2048)
    habs = AR.f32(82 * KB, 1024)
    qstg = [AR.f32(i * 5632, 1408) for i in range(2)]
    w2col = AR.bf(11 * KB, 44, 128)
    x2fm = AR.bf(96 * KB, 16, 1024)
    x2sc = AR.bf(128 * KB, 2048)

    def vop(emit, waits=(), sem="p2v", eng="vector"):
        return P.op(eng, emit, waits, (sem, 1) if sem else None)

    def mm(out, lhsT, rhs, start, stop, waits=(), sem=None):
        return P.op("tensor", lambda e, o=out, l=lhsT, r=rhs, s1=start, s2=stop:
                    e.matmul(o, l, r, start=s1, stop=s2), waits,
                    (sem, 1) if sem else None)

    def tr(out, in_, idn, waits=(), sem=None):
        return P.op("tensor", lambda e, o=out, i=in_, d=idn: e.transpose(o, i, d),
                    waits, (sem, 1) if sem else None)

    o0f = P.n("o0fin")
    for tt in range(8):
        w = [("o0fin", o0f)] if tt == 0 else []
        vop(lambda e, tt=tt: e.reduce_max(sml[:, 50:51], stream[:, tt, :], AX,
            apply_absolute_value=True), w)
        vop(lambda e, tt=tt: e.tensor_scalar(sml[:, 64 + tt:65 + tt], sml[:, 50:51],
            1.0 / 127.0, 1e-8, ALU.mult, ALU.max))
        vop(lambda e, tt=tt: e.reciprocal(sml[:, 52:53], sml[:, 64 + tt:65 + tt]))
        vop(lambda e, tt=tt: e.tensor_mul(sml[:, 53:54], sml[:, 64 + tt:65 + tt],
            sml[:, 52:53]))
        vop(lambda e: e.tensor_scalar(sml[:, 53:54], sml[:, 53:54], 2.0, -1.0,
            ALU.subtract, ALU.mult))
        vop(lambda e: e.tensor_mul(sml[:, 52:53], sml[:, 52:53], sml[:, 53:54]))
        for j2 in range(2):
            vop(lambda e, tt=tt, j2=j2: e.tensor_scalar(scr[:, 0:1024],
                stream[:, tt, j2 * 1024:(j2 + 1) * 1024], sml[:, 52:53], MQ,
                ALU.mult, ALU.add))
            vop(lambda e, j2=j2: e.tensor_scalar(xqt[:, j2 * 1024:(j2 + 1) * 1024],
                scr[:, 0:1024], MQ, None, ALU.subtract))
        qn = P.n("p2v")
        for w8 in range(2):
            for i in range(8):
                dk = w8 * 8 + i
                tr(PT[:, i * 128:(i + 1) * 128], xqt[:, dk * 128:(dk + 1) * 128],
                   ident_b[:, :],
                   [("p2v", qn), ("p2tw", P.n("p2tw"))] if i == 0 else [],
                   "p2tr" if i == 7 else None)
            trn = P.n("p2tr")
            vop(lambda e, tt=tt, w8=w8: e.tensor_copy(
                xq_fm[:, w8 * 8:(w8 + 1) * 8, tt * 128:(tt + 1) * 128],
                _shape(PT[:, 0:1024], (8, 128))), [("p2tr", trn)], sem="p2tw")
        P.op("scalar", lambda e, tt=tt: e.dma_start(out=A["xsp"][tt],
             in_=stream[:, tt, :]), [("p2tw", P.n("p2tw"))], ("dsp", 16),
             dma=True, chain=False)

    for tt in range(8):
        tr(PC[0:1, 0:128], sml[:, 64 + tt:65 + tt], ident_f[:, :],
           [("p2v", P.n("p2v"))] if tt == 0 else [("p2sx", P.n("p2sx"))], "p2c")
        vop(lambda e, tt=tt: e.tensor_copy(rowb[0:1, tt * 128:(tt + 1) * 128],
            PC[0:1, 0:128]), [("p2c", P.n("p2c"))], sem="p2sx")
    mm(PB[:, 0:512], onesmat_f[0:1, :], rowb[0:1, 0:512], True, True,
       [("p2sx", P.n("p2sx"))])
    mm(PB[:, 512:1024], onesmat_f[0:1, :], rowb[0:1, 512:1024], True, True, [], "p2c")
    vop(lambda e: e.tensor_copy(sxb[:, :], PB[:, 0:1024]),
        [("p2c", P.n("p2c")), ("dsp", 128)], sem="p2sx")
    sxdone = P.n("p2sx")

    wqt, cpt, mmt, evt = {}, {}, {}, {}
    dlo, dhi = {}, {}
    prevab = [0]

    trtok = {}

    def w1_load(c):
        col = (c // 2) + 44 * (c % 2)
        w = [("p2wq", wqt[c - 2])] if c >= 2 else [("p2sx", sxdone)]
        dlo[c] = P.op("sync", lambda e, col=col: e.dma_start(
            out=stg2[c % 2][:, 0:1024], in_=A["l0w1"][col][:, 0:1024]), w,
            ("dwlo", 16), dma=True)
        dhi[c] = P.op("gpsimd", lambda e, col=col: e.dma_start(
            out=stg2[c % 2][:, 1024:2048], in_=A["l0w1"][col][:, 1024:2048]),
            list(w), ("dwhi", 16), dma=True, chain=False)

    def w1_quant(c):
        vop(lambda e, c=c: e.reduce_max(sml[:, 50:51], stg2[c % 2][:, :], AX,
            apply_absolute_value=True), [("dwlo", dlo[c]), ("dwhi", dhi[c])])
        vop(lambda e, c=c: e.tensor_scalar(sws[:, c:c + 1], sml[:, 50:51],
            1.0 / 127.0, 1e-8, ALU.mult, ALU.max))
        vop(lambda e, c=c: e.reciprocal(sml[:, 52:53], sws[:, c:c + 1]))
        vop(lambda e, c=c: e.tensor_mul(sml[:, 53:54], sws[:, c:c + 1],
            sml[:, 52:53]))
        vop(lambda e: e.tensor_scalar(sml[:, 53:54], sml[:, 53:54], 2.0, -1.0,
            ALU.subtract, ALU.mult))
        vop(lambda e: e.tensor_mul(sml[:, 52:53], sml[:, 52:53], sml[:, 53:54]))
        vop(lambda e, c=c: e.tensor_scalar(qscr[:, :], stg2[c % 2][:, :],
            sml[:, 52:53], MQ, ALU.mult, ALU.add))
        ww = [("p2tr", trtok[c - 2])] if c >= 2 else []
        wqt[c] = vop(lambda e, c=c: e.tensor_scalar(w1int[c % 2][:, :],
            qscr[:, :], MQ, None, ALU.subtract), ww, sem="p2wq")

    def w1_trans(c):
        for w8 in range(2):
            for i in range(8):
                dk = w8 * 8 + i
                trw = [("p2wq", wqt[c]), ("p2tw", P.n("p2tw"))] if i == 0 else []
                tr(PT[:, i * 128:(i + 1) * 128],
                   w1int[c % 2][:, dk * 128:(dk + 1) * 128], ident_b[:, :],
                   trw, "p2tr" if i == 7 else None)
            trn = P.n("p2tr")
            cw = [("p2tr", trn)]
            if c >= 4:
                cw.append(("p2mm", mmt[c - 4]))
            vop(lambda e, c=c, w8=w8: e.tensor_copy(
                w1col[c % 4][:, w8 * 8:(w8 + 1) * 8, :],
                _shape(PT[:, 0:1024], (8, 128))), cw, sem="p2tw")
        cpt[c] = P.n("p2tw")
        trtok[c] = P.n("p2tr")

    def w1_mm(c):
        gi = c % 2
        p = c // 2
        for j in range(2):
            for dk in range(16):
                w = []
                if dk == 0:
                    w = [("p2tw", cpt[c])]
                    if p >= 1:
                        w.append(("p2ev", evt[(p - 1, j, gi)]))
                mm(PAh[gi][:, j * 512:(j + 1) * 512], w1col[c % 4][:, dk, :],
                   xq_fm[:, dk, j * 512:(j + 1) * 512], dk == 0, dk == 15,
                   w, "p2mm" if dk == 15 else None)
        mmt[c] = P.n("p2mm")

    def evac_p(p):
        for j in range(2):
            jo = slice(j * 512, (j + 1) * 512)
            w = [("p2mm", mmt[2 * p + 1] - (1 - j))]
            if prevab[0]:
                w.append(("p2sg", prevab[0]))
            t1 = vop(lambda e, p=p, jo=jo: e.scalar_tensor_tensor(
                scr[:, 0:512], PAh[0][:, jo], sws[:, 2 * p:2 * p + 1],
                sxb[:, jo], ALU.mult, ALU.mult), w, sem="p2ev")
            evt[(p, j, 0)] = t1
            sg = P.op("scalar", lambda e: e.activation(mrow[:, 0:512],
                scr[:, 0:512], ACT.Sigmoid), [("p2ev", t1)], ("p2sg", 1))
            vop(lambda e: e.tensor_mul(scr[:, 0:512], scr[:, 0:512],
                mrow[:, 0:512]), [("p2sg", sg)], sem=None)
            t2 = vop(lambda e, p=p, jo=jo: e.scalar_tensor_tensor(
                scr[:, 512:1024], PAh[1][:, jo], sws[:, 2 * p + 1:2 * p + 2],
                sxb[:, jo], ALU.mult, ALU.mult), [], sem="p2ev")
            evt[(p, j, 1)] = t2
            hv = vop(lambda e: e.tensor_mul(scr[:, 0:512], scr[:, 0:512],
                scr[:, 512:1024]), [], sem="p2ev")
            vop(lambda e, p=p, jo=jo: e.tensor_copy(h_fm[:, p, jo],
                scr[:, 0:512]), [], sem=None)
            ab = P.op("scalar", lambda e: e.activation(mrow[:, 512:1024],
                scr[:, 0:512], ACT.Abs), [("p2ev", hv)], ("p2sg", 1))
            prevab[0] = ab
            if p == 0:
                vop(lambda e, jo=jo: e.tensor_copy(habs[:, jo],
                    mrow[:, 512:1024]), [("p2sg", ab)], sem="p2ev")
            else:
                vop(lambda e, jo=jo: e.tensor_max(habs[:, jo], habs[:, jo],
                    mrow[:, 512:1024]), [("p2sg", ab)], sem="p2ev")

    w1_load(0)
    w1_load(1)
    w1_quant(0)
    for c in range(88):
        if c + 2 < 88:
            w1_load(c + 2)
        if c + 1 < 88:
            w1_quant(c + 1)
        w1_trans(c)
        w1_mm(c)
        if c % 2 == 1:
            evac_p(c // 2)
    evA = P.n("p2ev")

    for tt in range(8):
        tr(PB[:, tt * 128:(tt + 1) * 128], habs[:, tt * 128:(tt + 1) * 128],
           ident_f[:, :], [("p2ev", evA)] if tt == 0 else [],
           "p2c" if tt == 7 else None)
    cn = P.n("p2c")
    for tt in range(8):
        vop(lambda e, tt=tt: e.reduce_max(sml[:, 50:51],
            PB[:, tt * 128:(tt + 1) * 128], AX), [("p2c", cn)] if tt == 0 else [])
        vop(lambda e, tt=tt: e.tensor_scalar(sml[:, 80 + tt:81 + tt], sml[:, 50:51],
            1.0 / 127.0, 1e-8, ALU.mult, ALU.max))
        vop(lambda e, tt=tt: e.reciprocal(sml[:, 88 + tt:89 + tt],
            sml[:, 80 + tt:81 + tt]))
        vop(lambda e, tt=tt: e.tensor_mul(sml[:, 53:54], sml[:, 80 + tt:81 + tt],
            sml[:, 88 + tt:89 + tt]))
        vop(lambda e: e.tensor_scalar(sml[:, 53:54], sml[:, 53:54], 2.0, -1.0,
            ALU.subtract, ALU.mult))
        vop(lambda e, tt=tt: e.tensor_mul(sml[:, 88 + tt:89 + tt],
            sml[:, 88 + tt:89 + tt], sml[:, 53:54]), [], sem="p2hs")
    hsn = P.n("p2hs")
    for tt in range(8):
        tr(PC[0:1, 0:128], sml[:, 88 + tt:89 + tt], ident_f[:, :],
           [("p2hs", hsn)] if tt == 0 else [("p2sx", P.n("p2sx"))], "p2c")
        vop(lambda e, tt=tt: e.tensor_copy(rowb[0:1, tt * 128:(tt + 1) * 128],
            PC[0:1, 0:128]), [("p2c", P.n("p2c"))], sem="p2sx")
    mm(PB[:, 0:512], onesmat_f[0:1, :], rowb[0:1, 0:512], True, True,
       [("p2sx", P.n("p2sx"))])
    mm(PB[:, 512:1024], onesmat_f[0:1, :], rowb[0:1, 512:1024], True, True, [], "p2c")
    vop(lambda e: e.tensor_copy(rshb[:, :], PB[:, 0:1024]),
        [("p2c", P.n("p2c"))], sem="p2hs")

    for p in range(44):
        w = [("p2hs", P.n("p2hs"))] if p == 0 else []
        vop(lambda e, p=p: e.tensor_mul(scr[:, 0:1024], h_fm[:, p, :],
            rshb[:, :]), w)
        vop(lambda e, p=p: e.tensor_scalar(h_fm[:, p, :], scr[:, 0:1024], MQ, MQ,
            ALU.add, ALU.subtract), [], sem="p2hq")
    hqn = P.n("p2hq")

    for i, tt in enumerate((0, 1, 2, 5, 6, 7, 3, 4)):
        w = []
        if i == 0:
            w = [("p2wq", wqt[87]), ("p2mm", mmt[87]),
                 ("p2hs", P.n("p2hs")), ("dsp", P.n("dsp"))]
        elif tt == 3:
            w = [("p2hq", hqn)]
        P.op("sync", lambda e, tt=tt: e.dma_start(out=stream[:, tt, :],
             in_=A["xsp"][tt]), w, ("dx2", 16), dma=True)
    x2ld = P.n("dx2")

    contok = {}
    dql = {}
    mmt2, evt2, cpt2 = {}, {}, {}

    def w2_load(kk):
        if kk >= 128:
            return
        c2, q4 = kk // 8, kk % 4
        sl = kk % 2
        if kk >= 2:
            w = [("p2q2" if (kk - 2) % 8 < 4 else "p2t2", contok[kk - 2])]
        else:
            w = [("p2mm", mmt[87])]
        qd = ("sync", "dq0") if sl == 0 else ("scalar", "dq1")
        dql[kk] = P.op(qd[0], lambda e, c2=c2, q4=q4, sl=sl: e.dma_start(
            out=qstg[sl][:, :],
            in_=A["l0w2"][c2][:, q4 * 1408:(q4 + 1) * 1408]), w,
            (qd[1], 16), dma=True, chain=(qd[0] == "sync"))

    def w2_maxq(c2, q4):
        kk = c2 * 8 + q4
        sl = kk % 2
        dw = [(("dq0", "dq1")[sl], dql[kk])]
        if q4 == 0:
            contok[kk] = vop(lambda e, sl=sl: e.reduce_max(sml[:, 50:51],
                qstg[sl][:, :], AX, apply_absolute_value=True), dw, sem="p2q2")
        else:
            vop(lambda e, sl=sl: e.reduce_max(sml[:, 51:52], qstg[sl][:, :], AX,
                apply_absolute_value=True), dw, sem=None)
            contok[kk] = vop(lambda e: e.tensor_max(sml[:, 50:51], sml[:, 50:51],
                sml[:, 51:52]), [], sem="p2q2")

    def w2_scale(c2):
        vop(lambda e, c2=c2: e.tensor_scalar(sml[:, 96 + c2:97 + c2],
            sml[:, 50:51], 1.0 / 127.0, 1e-8, ALU.mult, ALU.max))
        vop(lambda e, c2=c2: e.reciprocal(sml[:, 52:53], sml[:, 96 + c2:97 + c2]))
        vop(lambda e, c2=c2: e.tensor_mul(sml[:, 53:54], sml[:, 96 + c2:97 + c2],
            sml[:, 52:53]))
        vop(lambda e: e.tensor_scalar(sml[:, 53:54], sml[:, 53:54], 2.0, -1.0,
            ALU.subtract, ALU.mult))
        vop(lambda e: e.tensor_mul(sml[:, 52:53], sml[:, 52:53], sml[:, 53:54]))

    def w2_quant(c2, q4):
        kk = c2 * 8 + 4 + q4
        sl = kk % 2
        dw = [(("dq0", "dq1")[sl], dql[kk])]
        vop(lambda e, sl=sl: e.tensor_scalar(qstg[sl][:, :], qstg[sl][:, :],
            sml[:, 52:53], MQ, ALU.mult, ALU.add), dw)
        qt = vop(lambda e, sl=sl: e.tensor_scalar(qstg[sl][:, :], qstg[sl][:, :],
            MQ, None, ALU.subtract), [], sem="p2q2")
        for wv, lo, hi in ((0, 0, 8), (1, 8, 11)):
            trw = []
            if wv == 0:
                trw = [("p2q2", qt), ("p2w2c", P.n("p2w2c"))]
                if c2 >= 1 and q4 == 0:
                    trw.append(("p2st", P.n("p2st")))
            else:
                trw = [("p2w2c", P.n("p2w2c"))]
            for i in range(lo, hi):
                tr(PB[:, (i - lo) * 128:(i - lo + 1) * 128],
                   qstg[sl][:, i * 128:(i + 1) * 128], ident_f[:, :],
                   trw if i == lo else [], "p2t2" if i == hi - 1 else None)
            trn = P.n("p2t2")
            cw = [("p2t2", trn)]
            if c2 >= 1 and q4 == 0 and wv == 0:
                cw.append(("p2m2", mmt2[c2 - 1]))
            vop(lambda e, q4=q4, lo=lo, hi=hi: e.tensor_copy(
                w2col[:, q4 * 11 + lo:q4 * 11 + hi, :],
                _shape(PB[:, 0:(hi - lo) * 128], (hi - lo, 128))), cw,
                sem="p2w2c")
        contok[kk] = P.n("p2t2")

    def w2_mm(c2):
        cpt2[c2] = P.n("p2w2c")
        for ci in range(44):
            for j in range(2):
                w = []
                if ci == 0 and j == 0:
                    w = [("p2w2c", cpt2[c2]), ("p2hq", hqn)]
                    if c2 >= 2:
                        w.append(("p2ev2", evt2[c2 - 2]))
                    else:
                        w.append(("p2ev", evA))
                mm(PAh[c2 % 2][:, j * 512:(j + 1) * 512], w2col[:, ci, :],
                   h_fm[:, ci, j * 512:(j + 1) * 512], ci == 0, ci == 43,
                   w, "p2m2" if (ci == 43 and j == 1) else None)
        mmt2[c2] = P.n("p2m2")

    def evac2(c2):
        vop(lambda e, c2=c2: e.tensor_scalar(scr[:, 0:1024], PAh[c2 % 2],
            sml[:, 96 + c2:97 + c2], None, ALU.mult),
            [("p2m2", mmt2[c2]), ("p2dt", P.n("p2dt"))], sem="p2ev2")
        evt2[c2] = P.n("p2ev2")
        for i in range(8):
            tr(PB[:, i * 128:(i + 1) * 128], scr[:, i * 128:(i + 1) * 128],
               ident_f[:, :],
               [("p2ev2", evt2[c2]), ("p2st", P.n("p2st")),
                ("p2w2c", P.n("p2w2c"))] if i == 0 else [],
               "p2dt" if i == 7 else None)
        dtn = P.n("p2dt")
        for tt in range(8):
            w = [("p2dt", dtn)] + ([("dx2", x2ld)] if c2 == 0 else [])
            vop(lambda e, c2=c2, tt=tt: e.scalar_tensor_tensor(
                stream[:, tt, c2 * 128:(c2 + 1) * 128],
                PB[:, tt * 128:(tt + 1) * 128], sml[:, 80 + tt:81 + tt],
                stream[:, tt, c2 * 128:(c2 + 1) * 128], ALU.mult, ALU.add,
                accum_out=scr2s[:, tt * 16 + c2:tt * 16 + c2 + 1]),
                w if tt == 0 else [], sem="p2st")

    w2_load(0)
    w2_load(1)
    for c2 in range(16):
        for q4 in range(4):
            w2_maxq(c2, q4)
            w2_load(c2 * 8 + q4 + 2)
        w2_scale(c2)
        for q4 in range(4):
            w2_quant(c2, q4)
            w2_load(c2 * 8 + 4 + q4 + 2)
        w2_mm(c2)
        evac2(c2)

    ctx2 = dict(mrow=mrow, sml=sml, scr2s=scr2s)
    emit_ln(P, A, ctx2, "p2", stream, None, 16)
    for tt in range(8):
        P.op("scalar", lambda e, tt=tt: e.dma_start(out=A["xsp"][tt],
             in_=stream[:, tt, :]), [("p2fin", P.n("p2fin"))] if tt == 0 else [],
             ("dsp2", 16), dma=True)
        vop(lambda e, tt=tt: e.tensor_copy(x2sc[:, :], stream[:, tt, :]),
            [("p2fin", P.n("p2fin"))])
        qn = P.n("p2v")
        for w8 in range(2):
            for i in range(8):
                dk = w8 * 8 + i
                tr(PT[:, i * 128:(i + 1) * 128], x2sc[:, dk * 128:(dk + 1) * 128],
                   ident_b[:, :],
                   [("p2v", qn), ("p2xw", P.n("p2xw"))] if i == 0 else [],
                   "p2tr" if i == 7 else None)
            trn = P.n("p2tr")
            vop(lambda e, tt=tt, w8=w8: e.tensor_copy(
                x2fm[:, w8 * 8:(w8 + 1) * 8, tt * 128:(tt + 1) * 128],
                _shape(PT[:, 0:1024], (8, 128))), [("p2tr", trn)], sem="p2xw")
    P.op("vector", lambda e: e.nop(), [("p2xw", P.n("p2xw"))], ("p2done", 1))


def emit_phase34(nc, P, AR, A, env):
    PA, PB, PC, PT, PAh = A["PA"], A["PB"], A["PC"], A["PT"], A["PAh"]
    ident_f, ident_b = env["ident_f"], env["ident_b"]
    onesmat_f, ones_b = env["onesmat_f"], env["ones_b"]
    sml, scr, scr2s = env["sml"], env["scr"], env["scr2s"]
    mrow, rowb, stream = env["mrow"], env["rowb"], env["stream"]
    ssel_b = env["ssel_b"]
    stg, wcol = env["stg"], env["wcol"]

    q_fm = AR.bf(0, 16, 1024)
    x2fm = AR.bf(96 * KB, 16, 1024)
    kstg = AR.bf(128 * KB, 1024)
    vstg = AR.bf(130 * KB, 512)
    vcol = AR.bf(132 * KB, 16, 512)
    enc_b = AR.bf(96 * KB, 16, 1024)
    pt_b = AR.bf(128 * KB, 16, 1024)
    Kh = [AR.bf(160 * KB + i * 4 * KB, 16, 128) for i in range(2)]
    Vh = [AR.bf(168 * KB + i * 4 * KB, 16, 128) for i in range(2)]
    mask_sb = AR.bf(176 * KB, 8, 2, 128)
    p_sb2 = AR.bf(180 * KB, 2048)

    def vop(emit, waits=(), sem="p3v", eng="vector"):
        return P.op(eng, emit, waits, (sem, 1) if sem else None)

    def mm(out, lhsT, rhs, start, stop, waits=(), sem=None):
        return P.op("tensor", lambda e, o=out, l=lhsT, r=rhs, s1=start, s2=stop:
                    e.matmul(o, l, r, start=s1, stop=s2), waits,
                    (sem, 1) if sem else None)

    def tr(out, in_, idn, waits=(), sem=None):
        return P.op("tensor", lambda e, o=out, i=in_, d=idn: e.transpose(o, i, d),
                    waits, (sem, 1) if sem else None)

    def rowmove(dst, src, waits, sem="dmv"):
        return P.op("gpsimd", lambda e, d=dst, s=src: e.dma_start(out=d, in_=s),
                    waits, (sem, 16), dma=True)

    p2d = P.n("p2done")

    for ci in range(16):
        w = [("p2done", p2d)] if ci < 2 else [("qw1", ci - 1), ("qm1", ci - 1)]
        t = P.op("sync", lambda e, ci=ci: e.dma_start(out=stg[ci % 2][:, :],
             in_=A["l1q"][ci].rearrange("p a b -> p (a b)")), w,
             (f"ds{ci % 2}", 16), dma=True)
        vop(lambda e, ci=ci: e.tensor_copy(wcol[ci % 2][:, :, :],
            _shape(stg[ci % 2][:, :], (16, 128))), [(f"ds{ci % 2}", t)],
            sem="qw1", eng="gpsimd")
    for ci in range(16):
        half = PAh[ci % 2]
        for dk in range(16):
            for j in range(2):
                w = []
                if dk == 0 and j == 0:
                    w = [("qw1", ci + 1)] + ([("qe1", ci - 1)] if ci >= 2 else [])
                mm(half[:, j * 512:(j + 1) * 512], wcol[ci % 2][:, dk, :],
                   x2fm[:, dk, j * 512:(j + 1) * 512], dk == 0, dk == 15,
                   w, "qm1" if (dk == 15 and j == 1) else None)
        mmn = P.n("qm1")
        sq = vop(lambda e, half=half: e.activation(scr.bitcast(BF16)[:, 0:1024], half,
                 ACT.Square), [("qm1", mmn)], sem="ss1", eng="scalar")
        for j in range(2):
            mm(PB[0:30, j * 512:(j + 1) * 512], ssel_b[:, ci, 0:30],
               scr.bitcast(BF16)[:, j * 512:(j + 1) * 512], ci == 0, ci == 15,
               [("ss1", sq)], "qst1" if (ci == 15 and j == 1) else None)
        vop(lambda e, ci=ci, half=half: e.tensor_copy(q_fm[:, ci, :], half),
            [("ss1", sq)], sem="qe1")
    ste = vop(lambda e: e.tensor_copy(mrow[0:16, 0:1024], PB[0:16, :]),
              [("qst1", 1)], sem="p3v")
    q2mn = P.n("p3v")

    for ci in range(16):
        w = [("qe1", 16)] if ci < 2 else [("kw1", ci - 1), ("km1", ci - 1)]
        t = P.op("sync", lambda e, ci=ci: e.dma_start(out=stg[ci % 2][:, :],
             in_=A["l1k"][ci].rearrange("p a b -> p (a b)")), w,
             (f"ds{ci % 2}", 16), dma=True)
        vop(lambda e, ci=ci: e.tensor_copy(wcol[ci % 2][:, :, :],
            _shape(stg[ci % 2][:, :], (16, 128))), [(f"ds{ci % 2}", t)],
            sem="kw1", eng="gpsimd")
    for ci in range(16):
        half = PAh[ci % 2]
        for dk in range(16):
            for j in range(2):
                w = []
                if dk == 0 and j == 0:
                    w = [("kw1", ci + 1)] + ([("dkc", 16 * (ci - 1))] if ci >= 2 else [])
                mm(half[:, j * 512:(j + 1) * 512], wcol[ci % 2][:, dk, :],
                   x2fm[:, dk, j * 512:(j + 1) * 512], dk == 0, dk == 15,
                   w, "km1" if (dk == 15 and j == 1) else None)
        mmn = P.n("km1")
        sq = vop(lambda e, half=half: e.activation(scr.bitcast(BF16)[:, 0:1024], half,
                 ACT.Square), [("km1", mmn)], sem="ss1", eng="scalar")
        for j in range(2):
            mm(PB[0:30, j * 512:(j + 1) * 512], ssel_b[:, ci, 0:30],
               scr.bitcast(BF16)[:, j * 512:(j + 1) * 512], ci == 0, ci == 15,
               [("ss1", sq), ("p3v", q2mn)], "kst1" if (ci == 15 and j == 1) else None)
        ke = vop(lambda e, ci=ci, half=half: e.tensor_copy(kstg[:, :], half),
                 [("ss1", sq), ("dkc", P.n("dkc"))], sem="p3v")
        P.op("scalar", lambda e, ci=ci: e.dma_start(out=A["cc_k"][ci],
             in_=kstg[:, :]), [("p3v", ke)], ("dkc", 16), dma=True)
    ken = P.n("p3v")
    ste2 = vop(lambda e: e.tensor_copy(mrow[32:48, 0:1024], PB[0:16, :]),
               [("kst1", 1)], sem="p3v")
    vop(lambda e: e.reduce_max(sml[32:48, 25:26], mrow[32:48, 0:1024], AX),
        [("p3v", ste2)], sem="p3v")
    mvs = rowmove(sml[0:16, 25:26], sml[32:48, 25:26], [("p3v", P.n("p3v"))])
    st3 = vop(lambda e: e.tensor_copy(sml[0:16, 26:27], sml[0:16, 25:26]),
              [("dmv", P.n("dmv"))], sem="p3v")
    P.op("scalar", lambda e: e.dma_start(out=A["cc_s"].ap(), in_=sml[0:16, 25:27]),
         [("p3v", st3)], ("dks", 16), dma=True)

    for og in range(4):
        for dk in range(16):
            w = [("km1", 16)] if (og == 0 and dk < 2) else \
                [("vw1", 16 * og + dk - 1)]
            if og > 0 and dk == 0:
                w = w + [("vm1", 8 * og)]
            t = P.op("sync", lambda e, og=og, dk=dk: e.dma_start(
                 out=stg[dk % 2][:, 0:512], in_=A["l1v"][dk][:, og * 512:(og + 1) * 512]),
                 w, (f"ds{dk % 2}", 16), dma=True)
            vop(lambda e, dk=dk: e.tensor_copy(vcol[:, dk, :], stg[dk % 2][:, 0:512]),
                [(f"ds{dk % 2}", t)], sem="vw1", eng="gpsimd")
        for lt in range(8):
            slot = PA[:, (lt % 4) * 512:(lt % 4 + 1) * 512]
            for dk in range(16):
                w = []
                if dk == 0:
                    w = [("vw1", 16 * (og + 1))] + \
                        ([("dvc", 16 * (8 * og + lt - 3))] if (8 * og + lt) >= 4 else [])
                    if og == 0 and lt == 0:
                        w.append(("p3v", ken))
                mm(slot, x2fm[:, dk, lt * 128:(lt + 1) * 128], vcol[:, dk, :],
                   dk == 0, dk == 15, w, "vm1" if dk == 15 else None)
            mmn = P.n("vm1")
            ve = vop(lambda e, slot=slot: e.tensor_copy(vstg[:, :], slot),
                     [("vm1", mmn), ("dvc", P.n("dvc"))], sem="p3v")
            P.op("scalar", lambda e, lt=lt, og=og: e.dma_start(
                 out=A["cc_v"][lt][:, og * 512:(og + 1) * 512], in_=vstg[:, :]),
                 [("p3v", ve)], ("dvc", 16), dma=True)

    RG = [[0, 1], [2, 3], [4, 5], [6, 7]]
    P.op("gpsimd", lambda e: e.collective_compute(
        "AllGather", ALU.bypass, replica_groups=RG,
        ins=[A["cc_k"].ap().opt()], outs=[A["gg_k"].ap().opt()]),
        [("dkc", P.n("dkc")), ("dvc", P.n("dvc")), ("dks", P.n("dks"))], ("ccs", 1))
    P.op("gpsimd", lambda e: e.collective_compute(
        "AllGather", ALU.bypass, replica_groups=RG,
        ins=[A["cc_v"].ap().opt()], outs=[A["gg_v"].ap().opt()]), [], ("ccs", 1))
    P.op("gpsimd", lambda e: e.collective_compute(
        "AllGather", ALU.bypass, replica_groups=RG,
        ins=[A["cc_s"].ap().opt()], outs=[A["gg_s"].ap().opt()]), [], ("ccs", 1))

    t = P.op("sync", lambda e: e.dma_start(out=sml[0:16, 20:22], in_=A["gg_s"][0]),
             [("ccs", 3)], ("dgs", 16), dma=True)
    t2 = P.op("sync", lambda e: e.dma_start(out=sml[0:16, 22:24], in_=A["gg_s"][1]),
              [], ("dgs2", 16), dma=True)
    vop(lambda e: e.tensor_max(sml[0:16, 28:29], sml[0:16, 20:21], sml[0:16, 22:23]),
        [("dgs", t), ("dgs2", t2)], sem="p3v")
    e3 = vop(lambda e: e.tensor_scalar(sml[0:16, 28:29], sml[0:16, 28:29], -INV2C,
             None, ALU.mult), [], sem="p3v")
    tr(PC[0:1, 0:16], sml[0:16, 28:29], ident_f[0:16, 0:16], [("p3v", e3)], "p3m")
    e4 = vop(lambda e: e.tensor_copy(sml[0:1, 4:20], PC[0:1, 0:16]),
             [("p3m", P.n("p3m"))], sem="p3v")
    mm(PC[0:128, 0:16], onesmat_f[0:1, :], sml[0:1, 4:20], True, True,
       [("p3v", e4)], "p3m")
    Mb2 = sml[:, 108:124]
    e5 = vop(lambda e: e.tensor_copy(sml[:, 108:124], PC[:, 0:16]),
             [("p3m", P.n("p3m"))], sem="p3v")
    mbn = P.n("p3v")

    tm0 = P.op("sync", lambda e: e.dma_start(out=stg[0][:, :],
          in_=A["maskd"].rearrange("p a b f -> p (a b f)")), [("ccs", 3)],
          ("ds0", 16), dma=True)
    tm_ = vop(lambda e: e.tensor_copy(mask_sb[:, :, :, :],
              _shape(stg[0][:, :], (8, 2, 128))), [("ds0", tm0)], sem="dmk")

    negm2 = _shape(scr2s[:, 0:128], (8, 16))
    for qt in range(8):
        P.op("tensor", lambda e, qt=qt: e.transpose(PC[:, 0:16],
             mrow[0:16, qt * 128:(qt + 1) * 128], ident_f[0:16, 0:16]),
             [("p3v", P.n("p3v")), ("ccs", 3)] if qt == 0 else [("nmt2", qt)], ("p3m", 1))
        vop(lambda e, qt=qt: e.scalar_tensor_tensor(negm2[:, qt, :],
            PC[:, 0:16], -INV2C, Mb2[:, 0:16], ALU.mult, ALU.add),
            [("p3m", P.n("p3m"))], sem="nmt2")
    nmtn2 = P.n("nmt2")

    for qh in range(16):
        for r in range(2):
            tk = P.op("sync", lambda e, qh=qh, r=r: e.dma_start(
                 out=Kh[qh % 2][:, r:16:2, :],
                 in_=A["gg_k"][r, qh].rearrange("p (l f) -> p l f", f=128)),
                 ([("ccs", 3), ("dmk", 1)] if qh < 2 else [("p4kf", qh - 1)])
                 if r == 0 else [],
                 (f"dkh{qh % 2}", 16), dma=True)
            tv = P.op("sync", lambda e, qh=qh, r=r: e.dma_start(
                 out=Vh[qh % 2][:, r:16:2, :],
                 in_=A["gg_v"][r][:, :, qh * 128:(qh + 1) * 128].rearrange(
                     "l p f -> p l f")),
                 [], (f"dvh{qh % 2}", 16), dma=True)
        base = P.n("exp1")
        for qt in range(8):
            ns = (2 * qt + 2) * 128
            npc = (ns + 511) // 512
            for j in range(npc):
                c0, c1 = j * 512, min((j + 1) * 512, ns)
                w = []
                if j == 0:
                    if qh == 0 and qt == 0:
                        w = [("nmt2", nmtn2), ("dmk", 1), (f"dkh{qh % 2}", tk)]
                    else:
                        w = [("ptr1", P.n("ptr1"))]
                    if qt == 0:
                        w.append((f"dkh{qh % 2}", tk))
                        if qh:
                            w.append(("bcd1", qh))
                mm(PA[:, c0:c1], q_fm[:, qh, qt * 128:(qt + 1) * 128],
                   Kh[qh % 2].rearrange("p a f -> p (a f)")[:, c0:c1],
                   True, True, w, "smm1" if j == npc - 1 else None)
            smm = P.n("smm1")
            mk = vop(lambda e, qt=qt, ns=ns: e.tensor_add(PA[:, 2 * qt * 128:ns],
                     PA[:, 2 * qt * 128:ns], mask_sb[:, qt, :, :].rearrange(
                     "p a f -> p (a f)")), [("smm1", smm)], sem="p3v")
            vop(lambda e, qh=qh, qt=qt, ns=ns: e.activation(p_sb2[:, 0:ns],
                PA[:, 0:ns], ACT.Exp, bias=negm2[:, qt, qh:qh + 1], scale=ISQ,
                accum_out=sml[:, 50 + qt:51 + qt]),
                [("p3v", mk)], sem="exp1", eng="scalar")
            expn = P.n("exp1")
            nst = 2 * qt + 2
            for w8 in range((nst + 7) // 8):
                lo = w8 * 8
                hi = min(lo + 8, nst)
                for i in range(lo, hi):
                    tr(PT[:, (i - lo) * 128:(i - lo + 1) * 128],
                       p_sb2[:, i * 128:(i + 1) * 128], ident_b[:, :],
                       [("exp1", expn), ("ptw1", P.n("ptw1"))] if i == lo else [],
                       "ptr1" if i == hi - 1 else None)
                trn = P.n("ptr1")
                nw = hi - lo
                pw = [("ptr1", trn)] + ([("avm1", qh)] if (qt == 0 and w8 == 0 and qh)
                                        else [])
                vop(lambda e, qt=qt, lo=lo, nw=nw: e.tensor_copy(
                    pt_b[:, lo:lo + nw, qt * 128:(qt + 1) * 128],
                    _shape(PT[:, 0:nw * 128], (nw, 128))), pw, sem="ptw1")
        for qt in range(8):
            tr(PC[0:1, 0:128], sml[:, 50 + qt:51 + qt], ident_f[:, :],
               [("ptw1", P.n("ptw1"))] if qt == 0 else [("rcd1", P.n("rcd1"))], "p3m")
            vop(lambda e, qt=qt: e.tensor_copy(rowb[0:1, qt * 128:(qt + 1) * 128],
                PC[0:1, 0:128]), [("p3m", P.n("p3m"))], sem="rcd1")
        vop(lambda e: e.reciprocal(rowb[0:1, 0:1024], rowb[0:1, 0:1024]),
            [], sem="rcd1")
        mm(PAh[0][:, 0:512], onesmat_f[0:1, :], rowb[0:1, 0:512], True, True,
           [("rcd1", P.n("rcd1"))])
        mm(PAh[0][:, 512:1024], onesmat_f[0:1, :], rowb[0:1, 512:1024],
           True, True, [], "p3m")
        bcn = P.n("p3m")
        ev = vop(lambda e: e.tensor_copy(scr[:, 0:1024], PAh[0]), [("p3m", bcn)],
                 sem="bcd1")
        for st in range(16):
            qs = (st // 2) * 128
            for j0 in range(2):
                lo, hi = j0 * 512, (j0 + 1) * 512
                if qs >= hi:
                    continue
                last = (st == 7) if j0 == 0 else (st == 15)
                aw = []
                if st == 0 and j0 == 0:
                    aw = ([("encd1", qh)] if qh else []) + [(f"dvh{qh % 2}", tv)]
                mm(PB[:, max(qs, lo):hi], Vh[qh % 2][:, st, :],
                   pt_b[:, st, max(qs, lo):hi], st == 0, last,
                   aw, "avm1" if (st == 15 and j0 == 1) else None)
        avn = P.n("avm1")
        P.op("tensor", lambda e: e.nop(), [("avm1", avn)], ("p4kf", 1))
        vop(lambda e, qh=qh: e.tensor_mul(enc_b[:, qh, :], PB[:, 0:1024],
            scr[:, 0:1024]), [("bcd1", P.n("bcd1")), ("avm1", avn)], sem="encd1")

    env2 = dict(env)
    emit_proj_ln(nc, P, AR, A, dict(
        wsrc="l1o", enc=enc_b, stream=stream, stg=stg, wcol=wcol,
        ident_f=ident_f, mrow=mrow, scr=scr, scr2s=scr2s, sml=sml,
        pre_wait=[("encd1", 16)], stt_extra=[],
        sem_prefix="o1", nchunks=16))




def emit_phase5(nc, P, AR, A, env):
    PA, PB, PC, PT, PAh = A["PA"], A["PB"], A["PC"], A["PT"], A["PAh"]
    ident_b = env["ident_b"]
    ident_f = env["ident_f"]
    sml, scr = env["sml"], env["scr"]
    scr2s, mrow, stream = env["scr2s"], env["mrow"], env["stream"]
    h128_b, h64b_b = env["h128_b"], env["h64b_b"]
    FSC = 1.0 / math.sqrt(H1)

    x3fm = AR.bf(0, 16, 512)
    w3c5 = [AR.bf(16 * KB + i * 4 * KB, 16, 128) for i in range(2)]
    w3c5f = [AR.bf(16 * KB + i * 4 * KB, 2048) for i in range(2)]
    ha = [AR.bf(24 * KB + i * KB, 512) for i in range(2)]
    sil = AR.f32(26 * KB, 512)
    fwc = [AR.bf(28 * KB + i * KB, 512) for i in range(2)]
    pstag = [AR.bf(30 * KB + i * KB, 512) for i in range(2)]
    g1T = AR.bf(96 * KB, 64, 512)
    stgw1 = AR.f32(160 * KB, 2048)
    stgw3 = AR.f32(168 * KB, 2048)
    wc5 = [AR.bf(176 * KB + i * 4 * KB, 16, 128) for i in range(2)]
    wc5f = [AR.bf(176 * KB + i * 4 * KB, 2048) for i in range(2)]
    w2s = [AR.f32(160 * KB + i * 2 * KB, 512) for i in range(2)]
    w2q = [AR.bf(164 * KB + i * KB, 512) for i in range(2)]
    x3c = AR.bf(166 * KB, 2048)

    def vop(emit, waits=(), sem="p5v", eng="vector"):
        return P.op(eng, emit, waits, (sem, 1) if sem else None)

    def mm(out, lhsT, rhs, start, stop, waits=(), sem=None):
        return P.op("tensor", lambda e, o=out, l=lhsT, r=rhs, s1=start, s2=stop:
                    e.matmul(o, l, r, start=s1, stop=s2), waits,
                    (sem, 1) if sem else None)

    def tr(out, in_, idn, waits=(), sem=None):
        return P.op("tensor", lambda e, o=out, i=in_, d=idn: e.transpose(o, i, d),
                    waits, (sem, 1) if sem else None)

    o1f = P.n("o1fin")
    for g in range(2):
        for ti in range(4):
            tt = g * 4 + ti
            w = []
            if ti == 0:
                w = [("o1fin", o1f)] if g == 0 else \
                    [("p5dm", P.n("p5dm")), ("p5wq", P.n("p5wq"))]
            vop(lambda e, tt=tt: e.tensor_copy(x3c[:, :], stream[:, tt, :]), w)
            cn = P.n("p5v")
            for w8 in range(2):
                for i in range(8):
                    dk = w8 * 8 + i
                    trw = []
                    if i == 0:
                        trw = [("p5v", cn), ("p5w", P.n("p5w"))]
                        if ti == 0 and w8 == 0:
                            trw += [("p5f", P.n("p5f")), ("p5g", P.n("p5g"))]
                    tr(PT[:, i * 128:(i + 1) * 128], x3c[:, dk * 128:(dk + 1) * 128],
                       ident_b[:, :], trw, "p5t" if i == 7 else None)
                trn = P.n("p5t")
                vop(lambda e, ti=ti, w8=w8: e.tensor_copy(
                    x3fm[:, w8 * 8:(w8 + 1) * 8, ti * 128:(ti + 1) * 128],
                    _shape(PT[:, 0:1024], (8, 128))), [("p5t", trn)], sem="p5w")
        x3n = P.n("p5w")

        qtok, q3tok, mmtok, hatok, p1tok, g1tok = {}, {}, {}, {}, {}, {}
        dA, dB, dC, dD, satok = {}, {}, {}, {}, {}

        def emit_scatter(a, g=g):
            pp = a % 2
            w = [("p5p1", p1tok[a])]
            if a >= 2:
                w += [("dsa", satok[a - 2][0]), ("dsb", satok[a - 2][1])]
            g1tok[a] = vop(lambda e, pp=pp: e.tensor_copy(
                pstag[pp][:, :], PB[:, pp * 512:pp * 512 + 512]), w, sem="p5g")
            sw = [("p5g", g1tok[a])]
            if g == 1 and a == 0:
                sw.append(("p5m2", P.n("p5m2")))
            ta = P.op("gpsimd", lambda e, a=a, pp=pp: e.dma_start(
                out=g1T[a:a + 1, :, :], in_=pstag[pp][0:64, :]), sw,
                ("dsa", 16), dma=True, chain=False)
            tb = P.op("gpsimd", lambda e, a=a, pp=pp: e.dma_start(
                out=g1T[64 + a:65 + a, :, :], in_=pstag[pp][64:128, :]),
                list(sw), ("dsb", 16), dma=True, chain=False)
            satok[a] = (ta, tb)

        def emit_loads(a):
            w1w = [("p5w", x3n)] if a == 0 else [("p5q", qtok[a - 1])]
            w3w = [("p5w", x3n)] if a == 0 else [("p5q3", q3tok[a - 1])]
            src1 = A["l1w1"][a].rearrange("p c f -> p (c f)")
            src3 = A["l1w3"][a].rearrange("p c f -> p (c f)")
            dA[a] = P.op("sync", lambda e, s=src1: e.dma_start(
                out=stgw1[:, 0:1024], in_=s[:, 0:1024]), w1w, ("dwA", 16), dma=True)
            dB[a] = P.op("gpsimd", lambda e, s=src1: e.dma_start(
                out=stgw1[:, 1024:2048], in_=s[:, 1024:2048]), list(w1w),
                ("dwB", 16), dma=True, chain=False)
            dC[a] = P.op("scalar", lambda e, s=src3: e.dma_start(
                out=stgw3[:, 0:1024], in_=s[:, 0:1024]), w3w, ("dwC", 16),
                dma=True, chain=False)
            dD[a] = P.op("sync", lambda e, s=src3: e.dma_start(
                out=stgw3[:, 1024:2048], in_=s[:, 1024:2048]), list(w3w),
                ("dwD", 16), dma=True)

        def emit_casts(a):
            cw = [("dwA", dA[a]), ("dwB", dB[a])]
            c3 = [("dwC", dC[a]), ("dwD", dD[a])]
            if a >= 2:
                cw.append(("p5m", mmtok[a - 2]))
                c3.append(("p5m", mmtok[a - 2]))
            qtok[a] = vop(lambda e, a=a: e.tensor_copy(
                wc5f[a % 2][:, :], stgw1[:, :]), cw, sem="p5q")
            q3tok[a] = P.op("scalar", lambda e, a=a: e.activation(
                w3c5f[a % 2][:, :], stgw3[:, :], ACT.Copy), c3, ("p5q3", 1))

        emit_loads(0)
        emit_casts(0)
        for a in range(64):
            if a + 1 < 64:
                emit_loads(a + 1)
                emit_casts(a + 1)
            ga = (a % 2) * 512
            ua = 1024 + (a % 2) * 512
            for dk in range(16):
                w = []
                if dk == 0:
                    w = [("p5q", qtok[a]), ("p5q3", q3tok[a])]
                    if a >= 2:
                        w.append(("p5e", hatok[a - 2]))
                mm(PA[:, ga:ga + 512], wc5[a % 2][:, dk, :], x3fm[:, dk, :],
                   dk == 0, dk == 15, w, None)
                mm(PA[:, ua:ua + 512], w3c5[a % 2][:, dk, :], x3fm[:, dk, :],
                   dk == 0, dk == 15, [], "p5m" if dk == 15 else None)
            mmtok[a] = P.n("p5m")
            if a >= 1:
                pp = (a - 1) % 2
                p1tok[a - 1] = mm(PB[:, pp * 512:pp * 512 + 512], h128_b[:, :],
                                  ha[pp][:, :], True, True,
                                  [("p5e", hatok[a - 1])] +
                                  ([("p5g", g1tok[a - 3])] if a >= 3 else []),
                                  "p5p1")
            sgw = [("p5m", mmtok[a])]
            if a >= 1:
                sgw.append(("p5e", hatok[a - 1]))
            sg = P.op("scalar", lambda e, ga=ga: e.activation(
                sil[:, :], PA[:, ga:ga + 512], ACT.Sigmoid), sgw, ("sl5", 1))
            vop(lambda e, ga=ga: e.tensor_mul(sil[:, :], PA[:, ga:ga + 512],
                sil[:, :]), [("sl5", sg)], sem=None)
            hw = [("p5p1", p1tok[a - 2])] if a >= 2 else []
            hatok[a] = vop(lambda e, a=a, ua=ua: e.tensor_mul(
                ha[a % 2][:, :], sil[:, :], PA[:, ua:ua + 512]), hw, sem="p5e")
            if a >= 1:
                emit_scatter(a - 1)
        p1tok[63] = mm(PB[:, 512:1024], h128_b[:, :], ha[1][:, :], True, True,
                       [("p5e", hatok[63]), ("p5g", g1tok[61])], "p5p1")
        emit_scatter(63)
        g1n = P.n("p5g")
        qnA, q3nA = P.n("p5q"), P.n("p5q3")
        dsn = (P.n("dsa"), P.n("dsb"))

        w2tok, wqtok, p2tok, fetok, dmtok = {}, {}, {}, {}, {}

        def emit_bloads(k):
            oq, chi = k // 64, k % 64
            sl = k % 2
            w2w = [("p5wq", wqtok[k - 2])] if k >= 2 else \
                  [("p5q", qnA), ("p5q3", q3nA)]
            ta = P.op("sync", lambda e, oq=oq, chi=chi, sl=sl: e.dma_start(
                out=w2s[sl][0:64, :],
                in_=A["l1w2"][oq, chi, 0:64, :]), w2w,
                ("dw2a", 16), dma=True)
            tb = P.op("scalar", lambda e, oq=oq, chi=chi, sl=sl: e.dma_start(
                out=w2s[sl][64:128, :],
                in_=A["l1w2"][oq, chi, 64:128, :]), list(w2w),
                ("dw2b", 16), dma=True, chain=False)
            w2tok[k] = (ta, tb)

        def emit_down(k):
            oq, chi = k // 64, k % 64
            sl = k % 2
            for ot in range(4):
                w = []
                if ot == 0:
                    w = [("p5f", fetok[k]), ("p5wq", wqtok[k])]
                    if chi == 0 and (oq or g):
                        w.append(("p5ev", P.n("p5ev")))
                mm(PA[:, ot * 512:(ot + 1) * 512],
                   w2q[sl][:, ot * 128:(ot + 1) * 128],
                   fwc[sl][:, :], chi == 0, chi == 63, w,
                   "p5dm" if ot == 3 else None)
            dmtok[k] = P.n("p5dm")

        def emit_evac(kbase):
            oq = kbase // 64
            dmn = dmtok[kbase + 63]
            for ot in range(4):
                vop(lambda e, ot=ot: e.tensor_scalar(scr[:, 0:512],
                    PA[:, ot * 512:(ot + 1) * 512], FSC, None, ALU.mult),
                    [("p5dm", dmn)] if ot == 0 else [("p5dt", P.n("p5dt"))],
                    sem="p5ev")
                evn = P.n("p5ev")
                for ti in range(4):
                    tr(PB[:, ti * 128:(ti + 1) * 128],
                       scr[:, ti * 128:(ti + 1) * 128], ident_f[:, :],
                       [("p5ev", evn), ("p5st", P.n("p5st"))] if ti == 0 else [],
                       "p5dt" if ti == 3 else None)
                dtn = P.n("p5dt")
                for ti in range(4):
                    tt = g * 4 + ti
                    oc = oq * 4 + ot
                    vop(lambda e, tt=tt, oc=oc, ti=ti: e.scalar_tensor_tensor(
                        stream[:, tt, oc * 128:(oc + 1) * 128],
                        PB[:, ti * 128:(ti + 1) * 128], 1.0,
                        stream[:, tt, oc * 128:(oc + 1) * 128], ALU.mult, ALU.add,
                        accum_out=scr2s[:, ti * 16 + oc:ti * 16 + oc + 1]),
                        [("p5dt", dtn)] if ti == 0 else [], sem="p5st")

        emit_bloads(0)
        for k in range(256):
            oq, chi = k // 64, k % 64
            sl = k % 2
            pth = k % 2
            if k + 1 < 256:
                emit_bloads(k + 1)
            cw = [("dw2a", w2tok[k][0]), ("dw2b", w2tok[k][1])]
            if k >= 2:
                cw.append(("p5dm", dmtok[k - 2]))
            wqtok[k] = vop(lambda e, sl=sl: e.tensor_copy(w2q[sl][:, :],
                           w2s[sl][:, :]), cw, sem="p5wq")
            p2dst = PC[:, 0:512] if pth == 0 else PB[:, 512:1024]
            pw = [("dsa", dsn[0]), ("dsb", dsn[1])] if k < 2 else \
                 [("p5f", fetok[k - 2])]
            mm(p2dst, h64b_b[:, :], g1T[:, chi, :], True, True, pw, "p5m2")
            p2tok[k] = P.n("p5m2")
            fetok[k] = vop(lambda e, sl=sl, d=p2dst: e.tensor_copy(
                fwc[sl][:, :], d), [("p5m2", p2tok[k])], sem="p5f")
            if k >= 1:
                emit_down(k - 1)
            if chi == 0 and k >= 64:
                emit_evac(k - 64)
        emit_down(255)
        emit_evac(192)
        for ti in range(4):
            tt = g * 4 + ti
            vop(lambda e, ti=ti: e.reduce_sum(sml[:, 0:1],
                scr2s[:, ti * 16:ti * 16 + 16], AX), [], sem="p5ln")
            vop(lambda e, tt=tt: e.activation(mrow[:, 0:1024], stream[:, tt, 0:1024],
                ACT.Square, accum_out=sml[:, 1:2]), [("p5ln", P.n("p5ln"))],
                sem="p5ln", eng="scalar")
            vop(lambda e, tt=tt: e.activation(mrow[:, 0:1024], stream[:, tt, 1024:2048],
                ACT.Square, accum_out=sml[:, 9:10]), [], sem="p5ln", eng="scalar")
            vop(lambda e: e.tensor_add(sml[:, 1:2], sml[:, 1:2], sml[:, 9:10]),
                [("p5ln", P.n("p5ln"))], sem="p5ln")
            vop(lambda e: e.tensor_scalar(sml[:, 2:3], sml[:, 0:1], 1.0 / D, None,
                ALU.mult))
            vop(lambda e: e.tensor_scalar(sml[:, 3:4], sml[:, 1:2], 1.0 / D, None,
                ALU.mult))
            vop(lambda e: e.tensor_mul(sml[:, 4:5], sml[:, 2:3], sml[:, 2:3]))
            vop(lambda e: e.tensor_sub(sml[:, 5:6], sml[:, 3:4], sml[:, 4:5]),
                [], sem="p5ln")
            vop(lambda e: e.activation(sml[:, 6:7], sml[:, 5:6], ACT.Sqrt,
                bias=sml[:, 49:50]), [("p5ln", P.n("p5ln"))], sem="p5ln",
                eng="scalar")
            vop(lambda e: e.reciprocal(sml[:, 7:8], sml[:, 6:7]),
                [("p5ln", P.n("p5ln"))], sem="p5ln")
            vop(lambda e, tt=tt: e.tensor_scalar(stream[:, tt, :], stream[:, tt, :],
                sml[:, 2:3], sml[:, 7:8], ALU.subtract, ALU.mult),
                [("p5ln", P.n("p5ln"))], sem="p5fin")
            qd = ("scalar", "doutA") if ti % 2 == 0 else ("gpsimd", "doutB")
            P.op(qd[0], lambda e, tt=tt: e.dma_start(out=A["out"][tt],
                 in_=stream[:, tt, :]), [("p5fin", P.n("p5fin"))], (qd[1], 16),
                 dma=True, chain=False)
    P.op("sync", lambda e: e.nop(),
         [("doutA", P.n("doutA")), ("doutB", P.n("doutB"))], ("fin", 1))


def _hadamard(n):
    h = np.array([[1.0]], dtype=np.float32)
    while h.shape[0] < n:
        h = np.block([[h, h], [h, -h]])
    return h.astype(np.float32)


def _colblocks(wt, nci, ndk):
    ind, outd = wt.shape
    assert ind == ndk * 128 and outd == nci * 128
    return np.ascontiguousarray(
        wt.reshape(ndk, 128, nci, 128).transpose(2, 1, 0, 3))


def _rope_tables(cols):
    inv = 1.0 / (10000.0 ** (np.arange(0, HD, 2, dtype=np.float32) / HD))
    ang = cols[:, None].astype(np.float32) * inv[None, :]
    c = np.cos(ang).T
    s = np.sin(ang).T
    ct = np.concatenate([c, c], axis=0)
    st = np.concatenate([-s, s], axis=0)
    return np.ascontiguousarray(ct), np.ascontiguousarray(st)


_SHARED_CACHE = {}


def _prep_shared(inputs):
    key = id(inputs.get("L0_attn_q_kernel"))
    if _SHARED_CACHE.get("key") == key:
        return _SHARED_CACHE["val"]
    f32 = np.float32
    S = {}
    S["l0q"] = _colblocks(np.asarray(inputs["L0_attn_q_kernel"], f32), 16, 16)
    S["l0k"] = _colblocks(np.asarray(inputs["L0_attn_k_kernel"], f32), 4, 16)
    S["l0v"] = np.ascontiguousarray(np.asarray(inputs["L0_attn_v_kernel"], f32).reshape(16, 128, 512))
    S["l0o"] = _colblocks(np.asarray(inputs["L0_attn_out_kernel"], f32), 16, 16)
    S["l0w1"] = np.ascontiguousarray(np.asarray(inputs["L0_ffn_w_gate_up"], f32).reshape(88, 128, D))
    S["l0w2"] = np.ascontiguousarray(np.asarray(inputs["L0_ffn_w_down"], f32).reshape(16, 128, I0))
    S["l1q"] = _colblocks(np.asarray(inputs["L1_attn_Wq"], f32).T, 16, 16)
    S["l1k"] = _colblocks(np.asarray(inputs["L1_attn_Wk"], f32).T, 16, 16)
    S["l1v"] = np.ascontiguousarray(np.asarray(inputs["L1_attn_Wv"], f32).T.reshape(16, 128, D))
    S["l1o"] = _colblocks(np.asarray(inputs["L1_attn_Wo"], f32).T, 16, 16)
    w1 = np.asarray(inputs["L1_ffn_W1"], f32)
    w3 = np.asarray(inputs["L1_ffn_W3"], f32)
    S["l1w1"] = np.ascontiguousarray(w1.reshape(64, 128, 16, 128).transpose(0, 3, 2, 1))
    S["l1w3"] = np.ascontiguousarray(w3.reshape(64, 128, 16, 128).transpose(0, 3, 2, 1))
    w2t = np.ascontiguousarray(np.asarray(inputs["L1_ffn_W2"], f32).T)
    S["l1w2"] = np.ascontiguousarray(
        w2t.reshape(64, 64, 2, D).transpose(1, 2, 0, 3).reshape(64, 128, 4, 512)
        .transpose(2, 0, 1, 3))
    S["ident"] = np.eye(128, dtype=f32)
    perm = np.empty(128, np.int64)
    perm[:64] = 2 * np.arange(64)
    perm[64:] = 2 * np.arange(64) + 1
    S["h128"] = np.ascontiguousarray(_hadamard(128)[:, perm])
    h64 = _hadamard(64)
    S["h64b"] = np.block([[h64, np.zeros((64, 64), f32)],
                          [np.zeros((64, 64), f32), h64]]).astype(f32)
    sel = np.zeros((4, 16), f32)
    for kv in range(4):
        sel[kv, 4 * kv:4 * kv + 4] = 1.0
    S["gqa"] = sel
    swp = np.zeros((128, 128), f32)
    for i in range(128):
        swp[(i + 64) % 128, i] = 1.0
    S["swp"] = swp
    cf, sf = _rope_tables(np.arange(T))
    S["ropec_f"], S["ropes_f"] = cf, sf
    S["epsc"] = np.tile(np.array([[1e-6, 1e-5]], np.float32), (128, 1))
    M_Q2H, M_QTOT, M_KT0, M_KT1 = 0, 16, 17, 18
    M_K2H0, M_K2H1 = 20, 24
    sel = np.zeros((24, 128, 32), np.float32)
    for ci in range(16):
        sel[ci, :, M_Q2H + ci] = 1.0
        sel[ci, :, M_QTOT] = 1.0
    for ci in range(4):
        sel[16 + ci, :, M_K2H0 + ci] = 1.0
        sel[16 + ci, :, M_KT0] = 1.0
        sel[20 + ci, :, M_K2H1 + ci] = 1.0
        sel[20 + ci, :, M_KT1] = 1.0
    S["ssel"] = np.ascontiguousarray(sel.transpose(1, 0, 2))
    _SHARED_CACHE["key"] = key
    _SHARED_CACHE["val"] = S
    return S


def _prep_core(inputs, c, S):
    f32 = np.float32
    b, h = c // 2, c % 2
    own = [h + 2 * i for i in range(8)]
    x = np.asarray(inputs["L0_attn_x"], f32)[b]
    xt = x.reshape(16, 128, D)
    M = dict(S)
    M["xs0"] = np.ascontiguousarray(xt[own])
    xT = np.ascontiguousarray(x.T)
    M["xfm"] = xT.reshape(16, 128, T)
    M["xfmo"] = np.ascontiguousarray(
        xT.reshape(16, 128, 16, 128)[:, :, own, :].reshape(16, 128, 1024))
    owncols = np.concatenate([np.arange(a * 128, (a + 1) * 128) for a in own])
    co, so = _rope_tables(owncols)
    M["ropec_o"], M["ropes_o"] = co, so
    mask = np.zeros((8, 2, 128, 128), f32)
    for qt in range(8):
        for j in range(2):
            abs_s = (2 * qt + j) * 128 + np.arange(128)[:, None]
            abs_q = (2 * qt + h) * 128 + np.arange(128)[None, :]
            mask[qt, j] = np.where(abs_s <= abs_q, 0.0, NEG)
    M["maskd"] = np.ascontiguousarray(mask.transpose(3, 0, 1, 2))
    return M


_NC_CACHE = {}


def _get_nc(tap, upto):
    key = (tap, upto)
    if key not in _NC_CACHE:
        _NC_CACHE[key] = build(tap=tap, upto=upto)
    return _NC_CACHE[key]


def kernel(**inputs):
    tap = DEBUG_TAP
    nc = _get_nc(tap, None)
    S = _prep_shared(inputs)
    in_maps = [_prep_core(inputs, c, S) for c in range(8)]
    res = run_bass_kernel_spmd(nc, in_maps, core_ids=list(range(8)))
    if tap:
        return [np.asarray(r["tap_" + tap]) for r in res.results]
    out = np.empty((B, T, D), np.float32)
    for c in range(8):
        b, h = c // 2, c % 2
        o = np.asarray(res.results[c]["out"])
        for i in range(8):
            a = h + 2 * i
            out[b, a * 128:(a + 1) * 128, :] = o[i]
    return out


# version 15
# speedup vs baseline: 1.0498x; 1.0073x over previous
import math
from contextlib import ExitStack

import numpy as np
import ml_dtypes

import concourse.bass as bass
import concourse.mybir as mybir
from concourse.bass_utils import run_bass_kernel_spmd

F32 = mybir.dt.float32
BF16 = mybir.dt.bfloat16
AX = mybir.AxisListType.X
ALU = mybir.AluOpType
ACT = mybir.ActivationFunctionType

MAGIC = float(1.5 * 2**23)
NEG = -1e30

B, T, D = 4, 2048, 2048
NH, HD = 16, 128
I0 = 5632
H1 = 8192
ISQ = 1.0 / math.sqrt(HD)
INV2C = 0.5 * ISQ
KB = 1024

DEBUG_TAP = None
ENGINES = ("sync", "vector", "scalar", "tensor", "gpsimd")


CHAINED = ("vector", "scalar", "gpsimd")


class Plan:

    def __init__(self):
        self.q = {e: [] for e in ENGINES}
        self.cnt = {}
        self.last = {}
        self.phase = 0
        self.span = {}
        self.nbar = 0
        self.dmasems = set()
        self.swdge = set()

    def _touch(self, sem):
        sp = self.span.setdefault(sem, [self.phase, self.phase])
        sp[1] = self.phase

    def op(self, eng, emit, waits=(), inc=None, dma=False, chain=True):
        post = None
        waits = list(waits)
        if eng in CHAINED and chain:
            if eng in self.last:
                waits.append(self.last[eng])
            if inc is None:
                inc = ("_ch_" + eng, 1)
        if inc is not None:
            s, a = inc
            if dma:
                self.dmasems.add(s)
                prev = self.cnt.get(s, 0)
                if prev:
                    waits.append((s, prev))
            if eng == "gpsimd":
                self.swdge.add(s)
            self.cnt[s] = self.cnt.get(s, 0) + a
            post = self.cnt[s]
            self._touch(s)
            if eng in CHAINED and chain:
                self.last[eng] = (s, post)
        for ss, vv in waits:
            self._touch(ss)
        self.q[eng].append((tuple(waits), emit, inc, self.phase))
        return post

    def n(self, sem):
        return self.cnt.get(sem, 0)

    def barrier(self):
        self.nbar += 1
        tgt = 5 * self.nbar
        dw = [(s, self.n(s)) for s in sorted(self.dmasems) if self.n(s) > 0]
        self.op("sync", lambda en: en.nop(), dw, None)
        for e in ENGINES:
            self.op(e, lambda en: en.nop(), [], ("BAR", 1))
        self.phase += 1
        for e in ENGINES:
            self.op(e, lambda en: en.nop(), [("BAR", tgt)], None)

    def emit_all(self, nc, block, stack):
        persistent = [s for s, sp in self.span.items()
                      if sp[0] != sp[1] or s == "BAR" or s in self.swdge]
        by_phase = {}
        for s, sp in self.span.items():
            if s not in persistent:
                by_phase.setdefault(sp[0], []).append(s)
        nrecycle = max((len(v) for v in by_phase.values()), default=0)
        npool = len(persistent) + nrecycle
        avail = 104
        assert npool <= avail, (len(persistent), nrecycle)
        handles = {}
        base = {}
        phys = [stack.enter_context(nc.semaphore(f"S{i}")) for i in range(npool)]
        physcnt = [0] * npool
        for i, s in enumerate(persistent):
            handles[s] = phys[i]
            base[s] = 0
        for ph, names in by_phase.items():
            for j, s in enumerate(sorted(names)):
                k = len(persistent) + j
                handles[s] = phys[k]
                base[s] = physcnt[k]
                physcnt[k] += self.cnt[s]
        for eng in ENGINES:
            plan = self.q[eng]
            if not plan:
                continue

            def body(e, plan=plan):
                seen = {}
                for waits, emit, inc, ph in plan:
                    for ss, vv in waits:
                        v = vv + base[ss]
                        if vv > 0 and seen.get(ss, -1) < v:
                            e.wait_ge(handles[ss], v)
                            seen[ss] = v
                    inst = emit(e)
                    if inc is not None:
                        ss, a = inc
                        if inst is None:
                            inst = e.nop()
                        inst.then_inc(handles[ss], a)

            getattr(block, eng)(body)


class Arena:
    def __init__(self, ap):
        self.ap = ap

    def f32(self, off, *shape):
        n = int(np.prod(shape))
        return _shape(self.ap[:, off:off + 4 * n].bitcast(F32), shape)

    def bf(self, off, *shape):
        n = int(np.prod(shape))
        return _shape(self.ap[:, off:off + 2 * n].bitcast(BF16), shape)


def _shape(v, shape):
    if len(shape) <= 1:
        return v
    pat_in = "p (" + " ".join(f"d{i}" for i in range(len(shape))) + ")"
    pat_out = "p " + " ".join(f"d{i}" for i in range(len(shape)))
    kw = {f"d{i}": int(s) for i, s in enumerate(shape)}
    return v.rearrange(pat_in + " -> " + pat_out, **kw)


TAP_SHAPES = {
    "x1": [8, 128, D], "x2": [8, 128, D], "x3": [8, 128, D],
    "q0": [128, 16, 1024], "k0": [128, 4, T], "v0": [128, 16, 512],
    "enc0": [128, 16, 1024], "attn0": [8, 128, D],
}


def build(tap=None, upto=None):
    nc = bass.Bass("TRN2", target_bir_lowering=False)
    P = Plan()

    def din(name, shape):
        return nc.dram_tensor(name, shape, F32, kind="ExternalInput").ap()

    A = {}
    A["xs0"] = din("xs0", [8, 128, D])
    A["xfm"] = din("xfm", [16, 128, T])
    A["xfmo"] = din("xfmo", [16, 128, 1024])
    A["l0q"] = din("l0q", [16, 128, 16, 128])
    A["l0k"] = din("l0k", [4, 128, 16, 128])
    A["l0v"] = din("l0v", [16, 128, 512])
    A["l0o"] = din("l0o", [16, 128, 16, 128])
    A["l0w1"] = din("l0w1", [88, 128, D])
    A["l0w2"] = din("l0w2", [16, 128, I0])
    A["l1q"] = din("l1q", [16, 128, 16, 128])
    A["l1k"] = din("l1k", [16, 128, 16, 128])
    A["l1v"] = din("l1v", [16, 128, D])
    A["l1o"] = din("l1o", [16, 128, 16, 128])
    A["l1w1"] = din("l1w1", [64, 128, 16, 128])
    A["l1w3"] = din("l1w3", [64, 128, 16, 128])
    A["l1w2"] = din("l1w2", [4, 64, 128, 512])
    A["ident"] = din("ident", [128, 128])
    A["h128"] = din("h128", [128, 128])
    A["h64b"] = din("h64b", [128, 128])
    A["gqa"] = din("gqa", [4, 16])
    A["ropec_o"] = din("ropec_o", [128, 1024])
    A["ropes_o"] = din("ropes_o", [128, 1024])
    A["ropec_f"] = din("ropec_f", [128, T])
    A["ropes_f"] = din("ropes_f", [128, T])
    A["maskd"] = din("maskd", [128, 8, 2, 128])
    A["swp"] = din("swp", [128, 128])
    A["epsc"] = din("epsc", [128, 2])
    A["ssel"] = din("ssel", [128, 24, 32])

    A["out"] = nc.dram_tensor("out", [8, 128, D], F32, kind="ExternalOutput").ap()

    A["cc_k"] = nc.dram_tensor("cc_k", [16, 128, 1024], BF16)
    A["cc_v"] = nc.dram_tensor("cc_v", [8, 128, D], BF16)
    A["cc_s"] = nc.dram_tensor("cc_s", [16, 2], F32)
    A["gg_k"] = nc.dram_tensor("gg_k", [2, 16, 128, 1024], BF16)
    A["gg_v"] = nc.dram_tensor("gg_v", [2, 8, 128, D], BF16)
    A["gg_s"] = nc.dram_tensor("gg_s", [2, 16, 2], F32)
    A["xsp"] = nc.dram_tensor("xsp", [8, 128, D], F32)
    A["hsp"] = nc.dram_tensor("hsp", [44, 128, 1024], F32)

    A["tap"] = None
    A["tapname"] = tap
    if tap:
        A["tap"] = nc.dram_tensor("tap_" + tap, TAP_SHAPES[tap], F32,
                                  kind="ExternalOutput").ap()

    with ExitStack() as stack:
        AR = Arena(stack.enter_context(
            nc.sbuf_tensor("arena", [128, 207 * KB], mybir.dt.uint8)).ap())
        A["PA"] = stack.enter_context(nc.psum_tensor("PA", [128, 2048], F32)).ap()
        A["PB"] = stack.enter_context(nc.psum_tensor("PB", [128, 1024], F32)).ap()
        A["PC"] = stack.enter_context(nc.psum_tensor("PC", [128, 512], F32)).ap()
        A["PT"] = stack.enter_context(nc.psum_tensor("PT", [128, 1024], BF16)).ap()
        A["PAh"] = [A["PA"][:, 0:1024], A["PA"][:, 1024:2048]]

        block = stack.enter_context(nc.Block())
        emit_program(nc, P, AR, A, upto)
        P.emit_all(nc, block, stack)

    return nc


def emit_program(nc, P, AR, A, upto):
    PA, PB, PC, PT, PAh = A["PA"], A["PB"], A["PC"], A["PT"], A["PAh"]
    tap = A["tapname"]

    C = 184 * KB
    ident_f = AR.f32(C, 128); C += 512
    ident_b = AR.bf(C, 128); C += 256
    h128_b = AR.bf(C, 128); C += 256
    h64b_b = AR.bf(C, 128); C += 256
    onesmat_b = AR.bf(C, 128); C += 256
    onesmat_f = AR.f32(C, 128); C += 512
    gqa_f = AR.f32(C, 16); C += 64
    swp_b = AR.bf(C, 128); C += 256
    ssel_b = AR.bf(C, 24, 32); C += 1536
    sml = AR.f32(C, 128); C += 512
    scr2s = AR.f32(C, 128); C += 512
    mrow = AR.f32(C, 1024); mrowb = AR.bf(C, 2048); C += 4 * KB
    scr = AR.f32(C, 1024); C += 4 * KB
    rowb = AR.f32(C, 2048); C += 8 * KB
    assert C <= 207 * KB, C
    ones_b = onesmat_b[:, 0:1]
    scrb = scr.bitcast(BF16)
    M_Q2H, M_QTOT, M_KT0, M_KT1 = 0, 16, 17, 18
    M_K2H0, M_K2H1 = 20, 24

    def load(dst, src, waits=(), q="sync", sem="ld"):
        return P.op(q, lambda e, d=dst, s=src: e.dma_start(out=d, in_=s), waits,
                    (sem, 16), dma=True)

    def vop(emit, waits=(), sem="vs", eng="vector"):
        return P.op(eng, emit, waits, (sem, 1) if sem else None)

    def mm(out, lhsT, rhs, start, stop, waits=(), sem=None):
        return P.op("tensor",
                    lambda e, o=out, l=lhsT, r=rhs, st=start, sp=stop:
                    e.matmul(o, l, r, start=st, stop=sp),
                    waits, (sem, 1) if sem else None)

    def transpose(out, in_, idn, waits=(), sem=None):
        return P.op("tensor", lambda e, o=out, i=in_, d=idn: e.transpose(o, i, d),
                    waits, (sem, 1) if sem else None)

    t1 = load(ident_f[:, :], A["ident"], sem="dc")
    t2 = load(mrow[:, 0:128], A["h128"], sem="dc")
    t3 = load(mrow[:, 128:256], A["h64b"], sem="dc")
    t4 = load(sml[0:4, 16:32], A["gqa"], sem="dc")
    t5 = load(sml[:, 48:50], A["epsc"], sem="dc")
    vop(lambda e: e.memset(onesmat_b[:, :], 1.0), [], sem=None)
    vop(lambda e: e.memset(onesmat_f[:, :], 1.0), [], sem=None)
    vop(lambda e: e.tensor_copy(ident_b[:, :], ident_f[:, :]), [("dc", t1)], sem=None)
    vop(lambda e: e.tensor_copy(h128_b[:, :], mrow[:, 0:128]), [("dc", t2)], sem=None)
    vop(lambda e: e.tensor_copy(gqa_f[0:4, :], sml[0:4, 16:32]), [("dc", t4)], sem=None)
    t6 = load(scr[:, 0:768], A["ssel"].rearrange("p v c -> p (v c)"), sem="dc")
    vop(lambda e: e.tensor_copy(ssel_b[:, :, :], _shape(scr[:, 0:768], (24, 32))),
        [("dc", t6)], sem=None)
    c_done = vop(lambda e: e.tensor_copy(h64b_b[:, :], mrow[:, 128:256]),
                 [("dc", t3), ("dc", t5)])

    xfm_b = AR.bf(0, 16, T)
    q_b = AR.bf(64 * KB, 16, 1024)
    k_b = AR.bf(96 * KB, 4, T)
    v_b = AR.bf(112 * KB, 16, 512)
    xo_b = AR.bf(128 * KB, 16, 1024)
    stg = [AR.f32(160 * KB + i * 8 * KB, 2048) for i in range(2)]
    wcol = [AR.bf(176 * KB + i * 4 * KB, 16, 128) for i in range(2)]

    for dk in range(16):
        w = [("vs", c_done)] if dk < 2 else [("xfmc", dk - 1)]
        t = load(stg[dk % 2][:, :], A["xfm"][dk], w, sem=f"ds{dk % 2}")
        vop(lambda e, dk=dk, s=stg[dk % 2]: e.tensor_copy(xfm_b[:, dk, :], s[:, :]),
            [(f"ds{dk % 2}", t)], sem="xfmc")
    for dk in range(16):
        w = [("xfmc", 16)] if dk < 2 else [("xoc", dk - 1)]
        t = load(stg[dk % 2][:, 0:1024], A["xfmo"][dk], w, sem=f"ds{dk % 2}")
        vop(lambda e, dk=dk, s=stg[dk % 2]: e.tensor_copy(xo_b[:, dk, :], s[:, 0:1024]),
            [(f"ds{dk % 2}", t)], sem="xoc")

    for ci in range(16):
        w = [("xoc", 16)] if ci < 2 else [("qw", ci - 1), ("qmm", ci - 1)]
        t = load(stg[ci % 2][:, :], A["l0q"][ci].rearrange("p a b -> p (a b)"), w,
                 sem=f"ds{ci % 2}")
        vop(lambda e, ci=ci, s=stg[ci % 2]:
            e.tensor_copy(wcol[ci % 2][:, :, :], _shape(s[:, :], (16, 128))),
            [(f"ds{ci % 2}", t)], sem="qw", eng="gpsimd")
    for ci in range(16):
        wv = wcol[ci % 2]
        half = PAh[ci % 2]
        for dk in range(16):
            for j in range(2):
                waits = []
                if dk == 0 and j == 0:
                    waits = [("qw", ci + 1)] + ([("qev", ci - 1)] if ci >= 2 else [])
                mm(half[:, j * 512:(j + 1) * 512], wv[:, dk, :],
                   xo_b[:, dk, j * 512:(j + 1) * 512], dk == 0, dk == 15,
                   waits, "qmm" if (dk == 15 and j == 1) else None)
        mmn = P.n("qmm")
        sq = vop(lambda e, half=half: e.activation(scrb[:, 0:1024], half, ACT.Square),
                 [("qmm", mmn)], sem="ss", eng="scalar")
        for j in range(2):
            mm(PB[0:30, j * 512:(j + 1) * 512], ssel_b[:, ci, 0:30],
               scrb[:, j * 512:(j + 1) * 512], ci == 0, False, [("ss", sq)],
               "qst" if (ci == 15 and j == 1) else None)
        vop(lambda e, ci=ci, half=half: e.tensor_copy(q_b[:, ci, :], half),
            [("ss", sq)], sem="qev")

    for ci in range(4):
        w = [("qev", 16), ("qmm", 16)] if ci < 2 else [("kw", ci - 1), ("kmm", ci - 1)]
        t = load(stg[ci % 2][:, :], A["l0k"][ci].rearrange("p a b -> p (a b)"), w,
                 sem=f"ds{ci % 2}")
        vop(lambda e, ci=ci, s=stg[ci % 2]:
            e.tensor_copy(wcol[ci % 2][:, :, :], _shape(s[:, :], (16, 128))),
            [(f"ds{ci % 2}", t)], sem="kw", eng="gpsimd")
    for ci in range(4):
        wv = wcol[ci % 2]
        for dk in range(16):
            for j in range(4):
                waits = []
                if dk == 0 and j == 0:
                    waits = [("kw", ci + 1)] + ([("kev", ci)] if ci >= 1 else [])
                mm(PA[:, j * 512:(j + 1) * 512], wv[:, dk, :],
                   xfm_b[:, dk, j * 512:(j + 1) * 512], dk == 0, dk == 15,
                   waits, "kmm" if (dk == 15 and j == 3) else None)
        mmn = P.n("kmm")
        sq = vop(lambda e: e.activation(scrb[:, 0:2048], PA[:, :], ACT.Square),
                 [("kmm", mmn)], sem="ss", eng="scalar")
        for j in range(4):
            v = (16 + ci) if j < 2 else (20 + ci)
            colj = j % 2
            mm(PB[0:30, colj * 512:(colj + 1) * 512], ssel_b[:, v, 0:30],
               scrb[:, j * 512:(j + 1) * 512], False, ci == 3 and j >= 2,
               [("ss", sq)], "kst" if (ci == 3 and j == 3) else None)
        vop(lambda e, ci=ci: e.tensor_copy(k_b[:, ci, :], PA[:, :]),
            [("ss", sq)], sem="kev")

    l0v_b = AR.bf(128 * KB, 16, 512)
    for dk in range(16):
        w = [("kev", 4)] if dk < 2 else [("vw", dk - 1)]
        t = load(stg[dk % 2][:, 0:512], A["l0v"][dk], w, sem=f"ds{dk % 2}")
        vop(lambda e, dk=dk, s=stg[dk % 2]: e.tensor_copy(l0v_b[:, dk, :], s[:, 0:512]),
            [(f"ds{dk % 2}", t)], sem="vw", eng="gpsimd")
    for st in range(16):
        slot = PA[:, (st % 4) * 512:(st % 4 + 1) * 512]
        for dk in range(16):
            waits = []
            if dk == 0:
                waits = [("vw", 16)] + ([("vev", st - 3)] if st >= 4 else [])
            mm(slot, xfm_b[:, dk, st * 128:(st + 1) * 128], l0v_b[:, dk, :],
               dk == 0, dk == 15, waits, "vmm" if dk == 15 else None)
        mmn = P.n("vmm")
        vop(lambda e, st=st, slot=slot: e.tensor_copy(v_b[:, st, :], slot),
            [("vmm", mmn)], sem="vev")

    stev = vop(lambda e: e.tensor_copy(mrow[0:28, 0:1024], PB[0:28, :]),
               [("qst", 1), ("kst", 1)], sem="vs")

    def rowmove(dst, src, waits, sem="dm"):
        return P.op("gpsimd", lambda e, d=dst, s=src: e.dma_start(out=d, in_=s),
                    waits, (sem, 16), dma=True)

    mv1 = rowmove(rowb[0:1, 0:1024], mrow[M_QTOT:M_QTOT + 1, 0:1024], [("vs", stev)])
    s1 = vop(lambda e: e.activation(rowb[0:1, 0:1024], rowb[0:1, 0:1024],
             ACT.Sqrt, bias=sml[0:1, 48:49], scale=1.0 / D), [("dm", mv1)],
             sem="ss", eng="scalar")
    r1 = vop(lambda e: e.reciprocal(rowb[0:1, 0:1024], rowb[0:1, 0:1024]),
             [("ss", s1)], sem="vs")

    tq1 = load(stg[0][:, 0:1024], A["ropec_o"], [("vev", 16)], sem="ds0")
    tq2 = load(stg[0][:, 1024:2048], A["ropes_o"], [("vev", 16)], sem="ds1")
    mm(PB[:, 0:512], onesmat_f[0:1, :], rowb[0:1, 0:512], True, True, [("vs", r1)])
    bq = mm(PB[:, 512:1024], onesmat_f[0:1, :], rowb[0:1, 512:1024], True, True,
            [], "mmx")
    bqn = P.n("mmx")
    rc1 = vop(lambda e: e.tensor_mul(stg[0][:, 0:1024], stg[0][:, 0:1024], PB[:, 0:1024]),
              [("ds0", tq1), ("mmx", bqn)], sem="vs")
    rc2 = vop(lambda e: e.tensor_mul(stg[0][:, 1024:2048], stg[0][:, 1024:2048], PB[:, 0:1024]),
              [("ds1", tq2), ("mmx", bqn)], sem="vs")
    rcq, rsq_t = stg[0][:, 0:1024], stg[0][:, 1024:2048]

    rsq = vop(lambda e: e.tensor_mul(rowb[0:1, 0:1024], rowb[0:1, 0:1024],
              rowb[0:1, 0:1024]), [("mmx", bqn)], sem="vs")
    b1a = mm(PB[0:16, 0:512], onesmat_f[0:1, 0:16], rowb[0:1, 0:512], True, True,
             [("vs", rsq), ("vs", stev), ("vs", rc2)])
    b1b = mm(PB[0:16, 512:1024], onesmat_f[0:1, 0:16], rowb[0:1, 512:1024], True, True,
             [], "mmx")
    mq = vop(lambda e: e.tensor_mul(scr[0:16, 0:1024], mrow[M_Q2H:M_Q2H + 16, 0:1024],
             PB[0:16, 0:1024]), [("mmx", P.n("mmx"))], sem="vs")
    mq2 = vop(lambda e: e.tensor_copy(mrow[M_Q2H:M_Q2H + 16, 0:1024], scr[0:16, 0:1024]),
              [("vs", mq)], sem="vs")
    q2m = vop(lambda e: e.nop(), [("vs", mq2)], sem="vs")

    tsw = load(scr2s[:, 0:64], A["swp"][:, 0:64], [], sem="dc")
    tsw2 = load(scr2s[:, 64:128], A["swp"][:, 64:128], [], sem="dc")
    csw = vop(lambda e: e.tensor_copy(swp_b[:, :], scr2s[:, 0:128]),
              [("dc", tsw2)], sem="vs")

    for ci in range(16):
        half = PAh[ci % 2]
        w = [("vs", rc1), ("vs", rc2), ("vs", csw), ("vs", q2m)] if ci < 2 \
            else [("qrope", ci - 1)]
        for j in range(2):
            mm(half[:, j * 512:(j + 1) * 512], swp_b[:, :],
               q_b[:, ci, j * 512:(j + 1) * 512], True, True,
               w if j == 0 else (), "mmq" if j == 1 else None)
        mmn = P.n("mmq")
        vop(lambda e, ci=ci: e.tensor_mul(scr[:, 0:1024], q_b[:, ci, :], rcq),
            [("mmq", mmn)], sem=None)
        vop(lambda e, ci=ci, half=half: e.tensor_mul(q_b[:, ci, :], half, rsq_t),
            [], sem=None)
        vop(lambda e, ci=ci: e.tensor_add(q_b[:, ci, :], scr[:, 0:1024], q_b[:, ci, :]),
            [], sem="qrope")

    mv2 = rowmove(rowb[0:1, 0:1024], mrow[M_KT0:M_KT0 + 1, 0:1024],
                  [("vs", q2m), ("vs", rc2)])
    mv3 = rowmove(rowb[0:1, 1024:2048], mrow[M_KT1:M_KT1 + 1, 0:1024], [])
    s2 = vop(lambda e: e.activation(rowb[0:1, 0:1024], rowb[0:1, 0:1024],
             ACT.Sqrt, bias=sml[0:1, 48:49], scale=1.0 / 512), [("dm", mv2)],
             sem="ss", eng="scalar")
    s3 = vop(lambda e: e.activation(rowb[0:1, 1024:2048], rowb[0:1, 1024:2048],
             ACT.Sqrt, bias=sml[0:1, 48:49], scale=1.0 / 512), [("dm", mv3)],
             sem="ss", eng="scalar")
    r2 = vop(lambda e: e.reciprocal(rowb[0:1, 0:1024], rowb[0:1, 0:1024]),
             [("ss", s2)], sem="vs")
    r3 = vop(lambda e: e.reciprocal(rowb[0:1, 1024:2048], rowb[0:1, 1024:2048]),
             [("ss", s3)], sem="vs")

    tk1 = load(stg[0][:, :], A["ropec_f"], [("qrope", 16)], sem="ds0")
    tk2 = load(stg[1][:, :], A["ropes_f"], [("qrope", 16)], sem="ds1")
    for j in range(2):
        rkr = rowb[0:1, j * 1024:(j + 1) * 1024]
        mm(PB[:, 0:512], onesmat_f[0:1, :], rkr[:, 0:512], True, True,
           [("vs", r2), ("vs", r3)] if j == 0 else [("krope_t", 2 * j)])
        mm(PB[:, 512:1024], onesmat_f[0:1, :], rkr[:, 512:1024], True, True, [], "mmx")
        bn = P.n("mmx")
        vop(lambda e, j=j: e.tensor_mul(stg[0][:, j * 1024:(j + 1) * 1024],
            stg[0][:, j * 1024:(j + 1) * 1024], PB[:, 0:1024]),
            [("ds0", tk1), ("ds1", tk2), ("mmx", bn)], sem="krope_t")
        vop(lambda e, j=j: e.tensor_mul(stg[1][:, j * 1024:(j + 1) * 1024],
            stg[1][:, j * 1024:(j + 1) * 1024], PB[:, 0:1024]),
            [("mmx", bn)], sem="krope_t")
    rck, rsk = stg[0], stg[1]

    rks = vop(lambda e: e.tensor_mul(rowb[0:1, 0:2048], rowb[0:1, 0:2048],
              rowb[0:1, 0:2048]), [("krope_t", 4)], sem="vs")
    mv4 = rowmove(mrow[64:68, 0:1024], mrow[M_K2H0:M_K2H0 + 4, 0:1024],
                  [("vs", q2m)])
    mv5 = rowmove(mrow[32:36, 0:1024], mrow[M_K2H1:M_K2H1 + 4, 0:1024],
                  [("vs", q2m)])
    mm(PB[64:68, 0:512], onesmat_f[0:1, 0:4], rowb[0:1, 0:512], True, True,
       [("vs", rks), ("krope_t", 4)])
    b2 = mm(PB[64:68, 512:1024], onesmat_f[0:1, 0:4], rowb[0:1, 512:1024], True,
            True, [], "mmx")
    mk0 = vop(lambda e: e.tensor_mul(scr[64:68, 0:1024], mrow[64:68, 0:1024],
              PB[64:68, 0:1024]), [("mmx", P.n("mmx")), ("dm", mv4)], sem="vs")
    k2a = vop(lambda e: e.reduce_max(sml[64:68, 1:2], scr[64:68, 0:1024], AX),
              [("vs", mk0)], sem="vs")
    b3 = mm(PB[32:36, 0:512], onesmat_f[0:1, 0:4], rowb[0:1, 1024:1536], True, True,
            [("vs", k2a)])
    b3b = mm(PB[32:36, 512:1024], onesmat_f[0:1, 0:4], rowb[0:1, 1536:2048], True,
             True, [], "mmx")
    mk1 = vop(lambda e: e.tensor_mul(scr[32:36, 0:1024], mrow[32:36, 0:1024],
              PB[32:36, 0:1024]), [("mmx", P.n("mmx")), ("dm", mv5)], sem="vs")
    k2b = vop(lambda e: e.reduce_max(sml[32:36, 2:3], scr[32:36, 0:1024], AX),
              [("vs", mk1)], sem="vs")
    mv6 = rowmove(sml[64:68, 2:3], sml[32:36, 2:3], [("vs", k2b)])
    k2m0 = vop(lambda e: e.tensor_max(sml[64:68, 1:2], sml[64:68, 1:2],
               sml[64:68, 2:3]), [("dm", mv6)], sem="vs")
    mv7 = rowmove(sml[0:4, 1:2], sml[64:68, 1:2], [("vs", k2m0)])
    k2m = vop(lambda e: e.nop(), [("dm", mv7)], sem="vs")
    g1 = mm(PC[0:16, 0:1], gqa_f[0:4, 0:16], sml[0:4, 1:2], True, True,
            [("vs", k2m)], "mmx")
    e1 = vop(lambda e: e.tensor_copy(sml[0:16, 3:4], PC[0:16, 0:1]),
             [("mmx", P.n("mmx"))], sem="vs")
    e3 = vop(lambda e: e.tensor_scalar(sml[0:16, 3:4], sml[0:16, 3:4], -INV2C, None,
             ALU.mult), [("vs", e1)], sem="vs")
    tr1 = transpose(PC[0:1, 0:16], sml[0:16, 3:4], ident_f[0:16, 0:16],
                    [("vs", e3)], "mmx")
    e4 = vop(lambda e: e.tensor_copy(sml[0:1, 4:20], PC[0:1, 0:16]),
             [("mmx", P.n("mmx"))], sem="vs")
    g2 = mm(PC[0:128, 0:16], onesmat_f[0:1, :], sml[0:1, 4:20], True, True,
            [("vs", e4)], "mmx")
    Mb = sml[:, 32:48]
    e5 = vop(lambda e: e.tensor_copy(sml[:, 32:48], PC[:, 0:16]),
             [("mmx", P.n("mmx"))], sem="vs")

    for ci in range(4):
        w = [("krope_t", 4)] if ci == 0 else [("krope", 2 * ci)]
        for j in range(4):
            mm(PA[:, j * 512:(j + 1) * 512], swp_b[:, :],
               k_b[:, ci, j * 512:(j + 1) * 512], True, True,
               w if j == 0 else (), "mmq" if j == 3 else None)
        mmn = P.n("mmq")
        for j in range(2):
            vop(lambda e, ci=ci, j=j: e.tensor_mul(scr[:, 0:1024],
                k_b[:, ci, j * 1024:(j + 1) * 1024], rck[:, j * 1024:(j + 1) * 1024]),
                [("mmq", mmn)] if j == 0 else [], sem=None)
            vop(lambda e, ci=ci, j=j: e.tensor_mul(k_b[:, ci, j * 1024:(j + 1) * 1024],
                PA[:, j * 1024:(j + 1) * 1024], rsk[:, j * 1024:(j + 1) * 1024]),
                [], sem=None)
            vop(lambda e, ci=ci, j=j: e.tensor_add(k_b[:, ci, j * 1024:(j + 1) * 1024],
                scr[:, 0:1024], k_b[:, ci, j * 1024:(j + 1) * 1024]), [], sem="krope")

    if tap in ("q0", "k0", "v0"):
        src = {"q0": q_b, "k0": k_b, "v0": v_b}[tap]
        w = {"q0": ("qrope", 16), "k0": ("krope", 8), "v0": ("vev", 16)}[tap]
        nch, width = TAP_SHAPES[tap][1], TAP_SHAPES[tap][2]
        for ci in range(nch):
            for j0 in range(0, width, 1024):
                jw = min(1024, width - j0)
                vop(lambda e, ci=ci, j0=j0, jw=jw, src=src:
                    e.tensor_copy(mrow[:, 0:jw], src[:, ci, j0:j0 + jw]), [w], sem="tapv")
                P.op("scalar", lambda e, ci=ci, j0=j0, jw=jw:
                     e.dma_start(out=A["tap"][:, ci, j0:j0 + jw], in_=mrow[:, 0:jw]),
                     [("tapv", P.n("tapv"))], ("st", 16), dma=True)
        P.op("sync", lambda e: e.nop(), [("st", P.n("st"))], ("fin", 1))
        return

    pt_b = AR.bf(128 * KB, 16, 1024)
    p_sb = AR.bf(120 * KB, 2048)
    enc_b = AR.bf(0, 16, 1024)
    negm_t = AR.f32(160 * KB, 8, 16)
    p_sb = [AR.bf(162 * KB, 2048), AR.bf(166 * KB, 2048)]
    tr = transpose
    for qt in range(8):
        tr(PC[:, 0:16], mrow[M_Q2H:M_Q2H + 16, qt * 128:(qt + 1) * 128],
           ident_f[0:16, 0:16], [("vs", e5)] if qt == 0 else [("nmt", qt)], "mmx")
        vop(lambda e, qt=qt: e.scalar_tensor_tensor(negm_t[:, qt, :],
            PC[:, 0:16], -INV2C, Mb[:, 0:16], ALU.mult, ALU.add),
            [("mmx", P.n("mmx"))], sem="nmt")
    nmtn = P.n("nmt")

    lasttr = {}
    for qh in range(16):
        kv = qh // 4
        base = P.n("exp0")
        for qt in range(8):
            for j in range(4):
                w = []
                if j == 0:
                    if qh == 0 and qt == 0:
                        w = [("krope", 8), ("qrope", 16), ("nmt", nmtn)]
                    else:
                        w = [("exp0", P.n("exp0"))]
                    if qt == 0 and qh:
                        w.append(("bcd", qh))
                mm(PA[:, j * 512:(j + 1) * 512], q_b[:, qh, qt * 128:(qt + 1) * 128],
                   k_b[:, kv, j * 512:(j + 1) * 512], True, True, w,
                   "smm" if j == 3 else None)
            smm = P.n("smm")
            ew = [("smm", smm)] + ([("avmm", qh)] if (qh and qt == 0) else [])
            if (qt % 2) in lasttr:
                ew.append(("ptr", lasttr[qt % 2]))
            vop(lambda e, qh=qh, qt=qt: e.activation(p_sb[qt % 2][:, :], PA[:, :],
                ACT.Exp, bias=negm_t[:, qt, qh:qh + 1], scale=ISQ,
                accum_out=sml[:, 50 + qt:51 + qt]),
                ew, sem="exp0", eng="scalar")
            expn = P.n("exp0")
            for w8 in range(2):
                for i in range(8):
                    st = w8 * 8 + i
                    tr(PT[:, i * 128:(i + 1) * 128],
                       p_sb[qt % 2][:, st * 128:(st + 1) * 128],
                       ident_b[:, :],
                       [("exp0", expn), ("ptw", P.n("ptw"))] if i == 0 else [],
                       "ptr" if i == 7 else None)
                trn = P.n("ptr")
                pw = [("ptr", trn)] + ([("avmm", qh)] if (qt == 0 and w8 == 0 and qh)
                                       else [])
                vop(lambda e, qt=qt, w8=w8: e.tensor_copy(
                    pt_b[:, w8 * 8:(w8 + 1) * 8, qt * 128:(qt + 1) * 128],
                    _shape(PT[:, 0:1024], (8, 128))), pw, sem="ptw")
            lasttr[qt % 2] = P.n("ptr")
        for qt in range(8):
            tr(PC[0:1, 0:128], sml[:, 50 + qt:51 + qt], ident_f[:, :],
               [("ptw", P.n("ptw"))] if qt == 0 else [("rcd", P.n("rcd"))], "mmx")
            vop(lambda e, qt=qt: e.tensor_copy(rowb[0:1, qt * 128:(qt + 1) * 128],
                PC[0:1, 0:128]), [("mmx", P.n("mmx"))], sem="rcd")
        rc2_ = vop(lambda e: e.reciprocal(rowb[0:1, 0:1024], rowb[0:1, 0:1024]),
                   [], sem="rcd")
        mm(PAh[0][:, 0:512], onesmat_f[0:1, :], rowb[0:1, 0:512], True, True,
           [("rcd", P.n("rcd"))])
        bc = mm(PAh[0][:, 512:1024], onesmat_f[0:1, :], rowb[0:1, 512:1024],
                True, True, [], "mmx")
        bcn = P.n("mmx")
        ev = vop(lambda e: e.tensor_copy(scr[:, 0:1024], PAh[0]), [("mmx", bcn)],
                 sem="bcd")
        for st in range(16):
            for j in range(2):
                aw = []
                if st == 0 and j == 0:
                    aw = [("encd", qh)] if qh else []
                mm(PB[:, j * 512:(j + 1) * 512], v_b[:, st, kv * 128:(kv + 1) * 128],
                   pt_b[:, st, j * 512:(j + 1) * 512], st == 0, st == 15,
                   aw, "avmm" if (st == 15 and j == 1) else None)
        avn = P.n("avmm")
        vop(lambda e, qh=qh: e.tensor_mul(enc_b[:, qh, :], PB[:, 0:1024], scr[:, 0:1024]),
            [("bcd", P.n("bcd")), ("avmm", avn)], sem="encd")

    if tap == "enc0":
        for ci in range(16):
            vop(lambda e, ci=ci: e.tensor_copy(mrow[:, 0:1024], enc_b[:, ci, :]),
                [("encd", 16)], sem="tapv")
            P.op("scalar", lambda e, ci=ci: e.dma_start(out=A["tap"][:, ci, :],
                 in_=mrow[:, 0:1024]), [("tapv", P.n("tapv"))], ("st", 16), dma=True)
        P.op("sync", lambda e: e.nop(), [("st", P.n("st"))], ("fin", 1))
        return

    stream = AR.f32(32 * KB, 8, 2048)
    for tt in range(8):
        t = load(stream[:, tt, :], A["xs0"][tt], [("encd", 16)] if tt == 0 else [],
                 sem="dx")
    xldn = P.n("dx")

    emit_proj_ln(nc, P, AR, A, dict(
        wsrc="l0o", enc=enc_b, stream=stream, stg=stg, wcol=wcol,
        ident_f=ident_f, mrow=mrow, scr=scr, scr2s=scr2s, sml=sml,
        pre_wait=[("encd", 16)], stt_extra=[("dx", xldn)],
        sem_prefix="o0", nchunks=16), upto_tap=(tap == "x1" or tap == "attn0"),
        tapname=tap)

    if tap in ("x1", "attn0"):
        for tt in range(8):
            P.op("scalar", lambda e, tt=tt: e.dma_start(out=A["tap"][tt],
                 in_=stream[:, tt, :]), [("o0fin", P.n("o0fin"))], ("st", 16), dma=True)
        P.op("sync", lambda e: e.nop(), [("st", P.n("st"))], ("fin", 1))
        return

    P.barrier()
    env = dict(ident_f=ident_f, ident_b=ident_b, onesmat_f=onesmat_f,
               onesmat_b=onesmat_b, ones_b=ones_b, sml=sml, scr=scr, scr2s=scr2s,
               mrow=mrow, mrowb=mrowb, rowb=rowb, stream=stream, stg=stg,
               wcol=wcol, h128_b=h128_b, h64b_b=h64b_b, Mb=Mb, gqa_f=gqa_f,
               ssel_b=ssel_b)
    emit_phase2(nc, P, AR, A, env)
    if tap == "x2":
        for tt in range(8):
            P.op("scalar", lambda e, tt=tt: e.dma_start(out=A["tap"][tt],
                 in_=stream[:, tt, :]), [("p2fin", P.n("p2fin"))], ("st", 16), dma=True)
        P.op("sync", lambda e: e.nop(), [("st", P.n("st"))], ("fin", 1))
        return

    P.barrier()
    emit_phase34(nc, P, AR, A, env)
    if tap == "x3":
        for tt in range(8):
            P.op("scalar", lambda e, tt=tt: e.dma_start(out=A["tap"][tt],
                 in_=stream[:, tt, :]), [("o1fin", P.n("o1fin"))], ("st", 16), dma=True)
        P.op("sync", lambda e: e.nop(), [("st", P.n("st"))], ("fin", 1))
        return

    P.barrier()
    emit_phase5(nc, P, AR, A, env)


def emit_ln(P, A, ctx, pfx, stream, partials, npart):
    mrow, sml = ctx["mrow"], ctx["sml"]
    scr2s = ctx["scr2s"]

    def vop(emit, waits=(), sem="vs", eng="vector"):
        return P.op(eng, emit, waits, (sem, 1) if sem else None)

    for tt in range(8):
        vop(lambda e, tt=tt: e.reduce_sum(sml[:, 0:1],
            scr2s[:, tt * npart:tt * npart + npart], AX), [], sem=pfx + "ln")
        vop(lambda e, tt=tt: e.activation(mrow[:, 0:1024], stream[:, tt, 0:1024],
            ACT.Square, accum_out=sml[:, 1:2]), [(pfx + "ln", P.n(pfx + "ln"))],
            sem=pfx + "ln", eng="scalar")
        vop(lambda e, tt=tt: e.activation(mrow[:, 0:1024], stream[:, tt, 1024:2048],
            ACT.Square, accum_out=sml[:, 9:10]), [], sem=pfx + "ln", eng="scalar")
        vop(lambda e: e.tensor_add(sml[:, 1:2], sml[:, 1:2], sml[:, 9:10]),
            [(pfx + "ln", P.n(pfx + "ln"))], sem=pfx + "ln")
        vop(lambda e: e.tensor_scalar(sml[:, 2:3], sml[:, 0:1], 1.0 / D, None, ALU.mult),
            [], sem=None)
        vop(lambda e: e.tensor_scalar(sml[:, 3:4], sml[:, 1:2], 1.0 / D, None, ALU.mult),
            [], sem=None)
        vop(lambda e: e.tensor_mul(sml[:, 4:5], sml[:, 2:3], sml[:, 2:3]), [], sem=None)
        vop(lambda e: e.tensor_sub(sml[:, 5:6], sml[:, 3:4], sml[:, 4:5]),
            [], sem=pfx + "ln")
        vop(lambda e: e.activation(sml[:, 6:7], sml[:, 5:6], ACT.Sqrt,
            bias=sml[:, 49:50]), [(pfx + "ln", P.n(pfx + "ln"))],
            sem=pfx + "ln", eng="scalar")
        vop(lambda e: e.reciprocal(sml[:, 7:8], sml[:, 6:7]),
            [(pfx + "ln", P.n(pfx + "ln"))], sem=pfx + "ln")
        vop(lambda e, tt=tt: e.tensor_scalar(stream[:, tt, :], stream[:, tt, :],
            sml[:, 2:3], sml[:, 7:8], ALU.subtract, ALU.mult),
            [(pfx + "ln", P.n(pfx + "ln"))], sem=pfx + "fin")


def emit_proj_ln(nc, P, AR, A, ctx, upto_tap=False, tapname=None):
    PA, PB, PC, PAh = A["PA"], A["PB"], A["PC"], A["PAh"]
    pfx = ctx["sem_prefix"]
    enc, stream, stg, wcol = ctx["enc"], ctx["stream"], ctx["stg"], ctx["wcol"]
    ident_f, mrow, scr, scr2, sml = (ctx["ident_f"], ctx["mrow"], ctx["scr"],
                                     ctx["scr2s"], ctx["sml"])
    wsrc = A[ctx["wsrc"]]

    def load(dst, src, waits=(), sem="ld"):
        return P.op("sync", lambda e, d=dst, s=src: e.dma_start(out=d, in_=s),
                    waits, (sem, 16), dma=True)

    def vop(emit, waits=(), sem="vs", eng="vector"):
        return P.op(eng, emit, waits, (sem, 1) if sem else None)

    def mm(out, lhsT, rhs, start, stop, waits=(), sem=None):
        return P.op("tensor", lambda e, o=out, l=lhsT, r=rhs, s1=start, s2=stop:
                    e.matmul(o, l, r, start=s1, stop=s2), waits,
                    (sem, 1) if sem else None)

    for oc in range(16):
        w = list(ctx["pre_wait"]) if oc < 2 else [(pfx + "w", oc - 1),
                                                   (pfx + "mm", oc - 1)]
        t = load(stg[oc % 2][:, :], wsrc[oc].rearrange("p a b -> p (a b)"), w,
                 sem=f"ds{oc % 2}")
        vop(lambda e, oc=oc, s=stg[oc % 2]:
            e.tensor_copy(wcol[oc % 2][:, :, :], _shape(s[:, :], (16, 128))),
            [(f"ds{oc % 2}", t)], sem=pfx + "w", eng="gpsimd")
    for oc in range(16):
        wv = wcol[oc % 2]
        half = PAh[oc % 2]
        for eh in range(16):
            waits = []
            if eh == 0:
                waits = [(pfx + "w", oc + 1)] + ([(pfx + "ev", oc - 1)] if oc >= 2 else [])
            for j in range(2):
                mm(half[:, j * 512:(j + 1) * 512], wv[:, eh, :],
                   enc[:, eh, j * 512:(j + 1) * 512], eh == 0, eh == 15,
                   waits if (eh == 0 and j == 0) else (),
                   pfx + "mm" if (eh == 15 and j == 1) else None)
        mmn = P.n(pfx + "mm")
        ev = vop(lambda e, half=half: e.tensor_copy(scr[:, 0:1024], half),
                 [(pfx + "mm", mmn), (pfx + "tr", oc)], sem=pfx + "ev")
        for tt in range(8):
            P.op("tensor", lambda e, tt=tt: e.transpose(
                PB[:, tt * 128:(tt + 1) * 128], scr[:, tt * 128:(tt + 1) * 128],
                ident_f[:, :]),
                [(pfx + "ev", P.n(pfx + "ev")), (pfx + "st", 8 * oc)] if tt == 0 else [],
                (pfx + "tr", 1) if tt == 7 else None)
        trn = P.n(pfx + "tr")
        for tt in range(8):
            w = [(pfx + "tr", trn)] + (list(ctx.get("stt_extra", ())) if oc == 0 else [])
            vop(lambda e, oc=oc, tt=tt: e.scalar_tensor_tensor(
                stream[:, tt, oc * 128:(oc + 1) * 128], PB[:, tt * 128:(tt + 1) * 128],
                1.0, stream[:, tt, oc * 128:(oc + 1) * 128], ALU.mult, ALU.add,
                accum_out=scr2[:, tt * 16 + oc:tt * 16 + oc + 1]),
                w if tt == 0 else [], sem=pfx + "st")
    stn = P.n(pfx + "st")
    P.op("vector", lambda e: e.nop(), [(pfx + "st", stn)], (pfx + "lngate", 1))
    emit_ln(P, A, ctx, pfx, stream, None, 16)


def emit_phase2(nc, P, AR, A, env):
    PA, PB, PC, PT, PAh = A["PA"], A["PB"], A["PC"], A["PT"], A["PAh"]
    ident_f, ident_b = env["ident_f"], env["ident_b"]
    onesmat_f, sml, scr = env["onesmat_f"], env["sml"], env["scr"]
    scr2s, mrow, rowb, stream = env["scr2s"], env["mrow"], env["rowb"], env["stream"]
    MQ = MAGIC

    xq_fm = AR.bf(0, 16, 1024)
    xqt = AR.bf(96 * KB, 2048)
    h_fm = AR.bf(96 * KB, 44, 1024)
    stg2 = [AR.f32(32 * KB + i * 8 * KB, 2048) for i in range(2)]
    w1col = [AR.bf(48 * KB + i * 4 * KB, 16, 128) for i in range(2)] + \
            [AR.bf(86 * KB + i * 4 * KB, 16, 128) for i in range(2)]
    sws = AR.f32(56 * KB, 176)
    sxb = AR.f32(57 * KB, 1024)
    rshb = AR.f32(61 * KB, 1024)
    w1int = [AR.bf(65 * KB + i * 4 * KB, 2048) for i in range(2)]
    qscr = AR.f32(74 * KB, 2048)
    habs = AR.f32(82 * KB, 1024)
    qstg = [AR.f32(i * 5632, 1408) for i in range(2)]
    w2col = AR.bf(11 * KB, 44, 128)
    x2fm = AR.bf(96 * KB, 16, 1024)
    x2sc = AR.bf(128 * KB, 2048)

    def vop(emit, waits=(), sem="p2v", eng="vector"):
        return P.op(eng, emit, waits, (sem, 1) if sem else None)

    def mm(out, lhsT, rhs, start, stop, waits=(), sem=None):
        return P.op("tensor", lambda e, o=out, l=lhsT, r=rhs, s1=start, s2=stop:
                    e.matmul(o, l, r, start=s1, stop=s2), waits,
                    (sem, 1) if sem else None)

    def tr(out, in_, idn, waits=(), sem=None):
        return P.op("tensor", lambda e, o=out, i=in_, d=idn: e.transpose(o, i, d),
                    waits, (sem, 1) if sem else None)

    o0f = P.n("o0fin")
    for tt in range(8):
        w = [("o0fin", o0f)] if tt == 0 else []
        vop(lambda e, tt=tt: e.reduce_max(sml[:, 50:51], stream[:, tt, :], AX,
            apply_absolute_value=True), w)
        vop(lambda e, tt=tt: e.tensor_scalar(sml[:, 64 + tt:65 + tt], sml[:, 50:51],
            1.0 / 127.0, 1e-8, ALU.mult, ALU.max))
        vop(lambda e, tt=tt: e.reciprocal(sml[:, 52:53], sml[:, 64 + tt:65 + tt]))
        vop(lambda e, tt=tt: e.tensor_mul(sml[:, 53:54], sml[:, 64 + tt:65 + tt],
            sml[:, 52:53]))
        vop(lambda e: e.tensor_scalar(sml[:, 53:54], sml[:, 53:54], 2.0, -1.0,
            ALU.subtract, ALU.mult))
        vop(lambda e: e.tensor_mul(sml[:, 52:53], sml[:, 52:53], sml[:, 53:54]))
        for j2 in range(2):
            vop(lambda e, tt=tt, j2=j2: e.tensor_scalar(scr[:, 0:1024],
                stream[:, tt, j2 * 1024:(j2 + 1) * 1024], sml[:, 52:53], MQ,
                ALU.mult, ALU.add))
            vop(lambda e, j2=j2: e.tensor_scalar(xqt[:, j2 * 1024:(j2 + 1) * 1024],
                scr[:, 0:1024], MQ, None, ALU.subtract))
        qn = P.n("p2v")
        for w8 in range(2):
            for i in range(8):
                dk = w8 * 8 + i
                tr(PT[:, i * 128:(i + 1) * 128], xqt[:, dk * 128:(dk + 1) * 128],
                   ident_b[:, :],
                   [("p2v", qn), ("p2tw", P.n("p2tw"))] if i == 0 else [],
                   "p2tr" if i == 7 else None)
            trn = P.n("p2tr")
            vop(lambda e, tt=tt, w8=w8: e.tensor_copy(
                xq_fm[:, w8 * 8:(w8 + 1) * 8, tt * 128:(tt + 1) * 128],
                _shape(PT[:, 0:1024], (8, 128))), [("p2tr", trn)], sem="p2tw")
        P.op("scalar", lambda e, tt=tt: e.dma_start(out=A["xsp"][tt],
             in_=stream[:, tt, :]), [("p2tw", P.n("p2tw"))], ("dsp", 16),
             dma=True, chain=False)

    for tt in range(8):
        tr(PC[0:1, 0:128], sml[:, 64 + tt:65 + tt], ident_f[:, :],
           [("p2v", P.n("p2v"))] if tt == 0 else [("p2sx", P.n("p2sx"))], "p2c")
        vop(lambda e, tt=tt: e.tensor_copy(rowb[0:1, tt * 128:(tt + 1) * 128],
            PC[0:1, 0:128]), [("p2c", P.n("p2c"))], sem="p2sx")
    mm(PB[:, 0:512], onesmat_f[0:1, :], rowb[0:1, 0:512], True, True,
       [("p2sx", P.n("p2sx"))])
    mm(PB[:, 512:1024], onesmat_f[0:1, :], rowb[0:1, 512:1024], True, True, [], "p2c")
    vop(lambda e: e.tensor_copy(sxb[:, :], PB[:, 0:1024]),
        [("p2c", P.n("p2c")), ("dsp", 128)], sem="p2sx")
    sxdone = P.n("p2sx")

    wqt, cpt, mmt, evt = {}, {}, {}, {}
    dlo, dhi = {}, {}
    prevab = [0]

    trtok = {}

    def w1_load(c):
        col = (c // 2) + 44 * (c % 2)
        w = [("p2wq", wqt[c - 2])] if c >= 2 else [("p2sx", sxdone)]
        dlo[c] = P.op("sync", lambda e, col=col: e.dma_start(
            out=stg2[c % 2][:, 0:1024], in_=A["l0w1"][col][:, 0:1024]), w,
            ("dwlo", 16), dma=True)
        dhi[c] = P.op("gpsimd", lambda e, col=col: e.dma_start(
            out=stg2[c % 2][:, 1024:2048], in_=A["l0w1"][col][:, 1024:2048]),
            list(w), ("dwhi", 16), dma=True, chain=False)

    def w1_quant(c):
        vop(lambda e, c=c: e.reduce_max(sml[:, 50:51], stg2[c % 2][:, :], AX,
            apply_absolute_value=True), [("dwlo", dlo[c]), ("dwhi", dhi[c])])
        vop(lambda e, c=c: e.tensor_scalar(sws[:, c:c + 1], sml[:, 50:51],
            1.0 / 127.0, 1e-8, ALU.mult, ALU.max))
        vop(lambda e, c=c: e.reciprocal(sml[:, 52:53], sws[:, c:c + 1]))
        vop(lambda e, c=c: e.tensor_mul(sml[:, 53:54], sws[:, c:c + 1],
            sml[:, 52:53]))
        vop(lambda e: e.tensor_scalar(sml[:, 53:54], sml[:, 53:54], 2.0, -1.0,
            ALU.subtract, ALU.mult))
        vop(lambda e: e.tensor_mul(sml[:, 52:53], sml[:, 52:53], sml[:, 53:54]))
        vop(lambda e, c=c: e.tensor_scalar(qscr[:, :], stg2[c % 2][:, :],
            sml[:, 52:53], MQ, ALU.mult, ALU.add))
        ww = [("p2tr", trtok[c - 2])] if c >= 2 else []
        wqt[c] = vop(lambda e, c=c: e.tensor_scalar(w1int[c % 2][:, :],
            qscr[:, :], MQ, None, ALU.subtract), ww, sem="p2wq")

    def w1_trans(c):
        for w8 in range(2):
            for i in range(8):
                dk = w8 * 8 + i
                trw = [("p2wq", wqt[c]), ("p2tw", P.n("p2tw"))] if i == 0 else []
                tr(PT[:, i * 128:(i + 1) * 128],
                   w1int[c % 2][:, dk * 128:(dk + 1) * 128], ident_b[:, :],
                   trw, "p2tr" if i == 7 else None)
            trn = P.n("p2tr")
            cw = [("p2tr", trn)]
            if c >= 4:
                cw.append(("p2mm", mmt[c - 4]))
            vop(lambda e, c=c, w8=w8: e.tensor_copy(
                w1col[c % 4][:, w8 * 8:(w8 + 1) * 8, :],
                _shape(PT[:, 0:1024], (8, 128))), cw, sem="p2tw")
        cpt[c] = P.n("p2tw")
        trtok[c] = P.n("p2tr")

    def w1_mm(c):
        gi = c % 2
        p = c // 2
        for j in range(2):
            for dk in range(16):
                w = []
                if dk == 0:
                    w = [("p2tw", cpt[c])]
                    if p >= 1:
                        w.append(("p2ev", evt[(p - 1, j, gi)]))
                mm(PAh[gi][:, j * 512:(j + 1) * 512], w1col[c % 4][:, dk, :],
                   xq_fm[:, dk, j * 512:(j + 1) * 512], dk == 0, dk == 15,
                   w, "p2mm" if dk == 15 else None)
        mmt[c] = P.n("p2mm")

    def evac_p(p):
        for j in range(2):
            jo = slice(j * 512, (j + 1) * 512)
            w = [("p2mm", mmt[2 * p + 1] - (1 - j))]
            if prevab[0]:
                w.append(("p2sg", prevab[0]))
            t1 = vop(lambda e, p=p, jo=jo: e.scalar_tensor_tensor(
                scr[:, 0:512], PAh[0][:, jo], sws[:, 2 * p:2 * p + 1],
                sxb[:, jo], ALU.mult, ALU.mult), w, sem="p2ev")
            evt[(p, j, 0)] = t1
            sg = P.op("scalar", lambda e: e.activation(mrow[:, 0:512],
                scr[:, 0:512], ACT.Sigmoid), [("p2ev", t1)], ("p2sg", 1))
            vop(lambda e: e.tensor_mul(scr[:, 0:512], scr[:, 0:512],
                mrow[:, 0:512]), [("p2sg", sg)], sem=None)
            t2 = vop(lambda e, p=p, jo=jo: e.scalar_tensor_tensor(
                scr[:, 512:1024], PAh[1][:, jo], sws[:, 2 * p + 1:2 * p + 2],
                sxb[:, jo], ALU.mult, ALU.mult), [], sem="p2ev")
            evt[(p, j, 1)] = t2
            hv = vop(lambda e: e.tensor_mul(scr[:, 0:512], scr[:, 0:512],
                scr[:, 512:1024]), [], sem="p2ev")
            vop(lambda e, p=p, jo=jo: e.tensor_copy(h_fm[:, p, jo],
                scr[:, 0:512]), [], sem=None)
            ab = P.op("scalar", lambda e: e.activation(mrow[:, 512:1024],
                scr[:, 0:512], ACT.Abs), [("p2ev", hv)], ("p2sg", 1))
            prevab[0] = ab
            if p == 0:
                vop(lambda e, jo=jo: e.tensor_copy(habs[:, jo],
                    mrow[:, 512:1024]), [("p2sg", ab)], sem="p2ev")
            else:
                vop(lambda e, jo=jo: e.tensor_max(habs[:, jo], habs[:, jo],
                    mrow[:, 512:1024]), [("p2sg", ab)], sem="p2ev")

    w1_load(0)
    w1_load(1)
    w1_quant(0)
    for c in range(88):
        if c + 2 < 88:
            w1_load(c + 2)
        if c + 1 < 88:
            w1_quant(c + 1)
        w1_trans(c)
        w1_mm(c)
        if c % 2 == 1:
            evac_p(c // 2)
    evA = P.n("p2ev")

    for tt in range(8):
        tr(PB[:, tt * 128:(tt + 1) * 128], habs[:, tt * 128:(tt + 1) * 128],
           ident_f[:, :], [("p2ev", evA)] if tt == 0 else [],
           "p2c" if tt == 7 else None)
    cn = P.n("p2c")
    for tt in range(8):
        vop(lambda e, tt=tt: e.reduce_max(sml[:, 50:51],
            PB[:, tt * 128:(tt + 1) * 128], AX), [("p2c", cn)] if tt == 0 else [])
        vop(lambda e, tt=tt: e.tensor_scalar(sml[:, 80 + tt:81 + tt], sml[:, 50:51],
            1.0 / 127.0, 1e-8, ALU.mult, ALU.max))
        vop(lambda e, tt=tt: e.reciprocal(sml[:, 88 + tt:89 + tt],
            sml[:, 80 + tt:81 + tt]))
        vop(lambda e, tt=tt: e.tensor_mul(sml[:, 53:54], sml[:, 80 + tt:81 + tt],
            sml[:, 88 + tt:89 + tt]))
        vop(lambda e: e.tensor_scalar(sml[:, 53:54], sml[:, 53:54], 2.0, -1.0,
            ALU.subtract, ALU.mult))
        vop(lambda e, tt=tt: e.tensor_mul(sml[:, 88 + tt:89 + tt],
            sml[:, 88 + tt:89 + tt], sml[:, 53:54]), [], sem="p2hs")
    hsn = P.n("p2hs")
    for tt in range(8):
        tr(PC[0:1, 0:128], sml[:, 88 + tt:89 + tt], ident_f[:, :],
           [("p2hs", hsn)] if tt == 0 else [("p2sx", P.n("p2sx"))], "p2c")
        vop(lambda e, tt=tt: e.tensor_copy(rowb[0:1, tt * 128:(tt + 1) * 128],
            PC[0:1, 0:128]), [("p2c", P.n("p2c"))], sem="p2sx")
    mm(PB[:, 0:512], onesmat_f[0:1, :], rowb[0:1, 0:512], True, True,
       [("p2sx", P.n("p2sx"))])
    mm(PB[:, 512:1024], onesmat_f[0:1, :], rowb[0:1, 512:1024], True, True, [], "p2c")
    vop(lambda e: e.tensor_copy(rshb[:, :], PB[:, 0:1024]),
        [("p2c", P.n("p2c"))], sem="p2hs")

    for p in range(44):
        w = [("p2hs", P.n("p2hs"))] if p == 0 else []
        vop(lambda e, p=p: e.tensor_mul(scr[:, 0:1024], h_fm[:, p, :],
            rshb[:, :]), w)
        vop(lambda e, p=p: e.tensor_scalar(h_fm[:, p, :], scr[:, 0:1024], MQ, MQ,
            ALU.add, ALU.subtract), [], sem="p2hq")
    hqn = P.n("p2hq")

    for i, tt in enumerate((0, 1, 2, 5, 6, 7, 3, 4)):
        w = []
        if i == 0:
            w = [("p2wq", wqt[87]), ("p2mm", mmt[87]),
                 ("p2hs", P.n("p2hs")), ("dsp", P.n("dsp"))]
        elif tt == 3:
            w = [("p2hq", hqn)]
        P.op("sync", lambda e, tt=tt: e.dma_start(out=stream[:, tt, :],
             in_=A["xsp"][tt]), w, ("dx2", 16), dma=True)
    x2ld = P.n("dx2")

    contok = {}
    dql = {}
    mmt2, evt2, cpt2 = {}, {}, {}

    def w2_load(kk):
        if kk >= 128:
            return
        c2, q4 = kk // 8, kk % 4
        sl = kk % 2
        if kk >= 2:
            w = [("p2q2" if (kk - 2) % 8 < 4 else "p2t2", contok[kk - 2])]
        else:
            w = [("p2mm", mmt[87])]
        qd = ("sync", "dq0") if sl == 0 else ("scalar", "dq1")
        dql[kk] = P.op(qd[0], lambda e, c2=c2, q4=q4, sl=sl: e.dma_start(
            out=qstg[sl][:, :],
            in_=A["l0w2"][c2][:, q4 * 1408:(q4 + 1) * 1408]), w,
            (qd[1], 16), dma=True, chain=(qd[0] == "sync"))

    def w2_maxq(c2, q4):
        kk = c2 * 8 + q4
        sl = kk % 2
        dw = [(("dq0", "dq1")[sl], dql[kk])]
        if q4 == 0:
            contok[kk] = vop(lambda e, sl=sl: e.reduce_max(sml[:, 50:51],
                qstg[sl][:, :], AX, apply_absolute_value=True), dw, sem="p2q2")
        else:
            vop(lambda e, sl=sl: e.reduce_max(sml[:, 51:52], qstg[sl][:, :], AX,
                apply_absolute_value=True), dw, sem=None)
            contok[kk] = vop(lambda e: e.tensor_max(sml[:, 50:51], sml[:, 50:51],
                sml[:, 51:52]), [], sem="p2q2")

    def w2_scale(c2):
        vop(lambda e, c2=c2: e.tensor_scalar(sml[:, 96 + c2:97 + c2],
            sml[:, 50:51], 1.0 / 127.0, 1e-8, ALU.mult, ALU.max))
        vop(lambda e, c2=c2: e.reciprocal(sml[:, 52:53], sml[:, 96 + c2:97 + c2]))
        vop(lambda e, c2=c2: e.tensor_mul(sml[:, 53:54], sml[:, 96 + c2:97 + c2],
            sml[:, 52:53]))
        vop(lambda e: e.tensor_scalar(sml[:, 53:54], sml[:, 53:54], 2.0, -1.0,
            ALU.subtract, ALU.mult))
        vop(lambda e: e.tensor_mul(sml[:, 52:53], sml[:, 52:53], sml[:, 53:54]))

    def w2_quant(c2, q4):
        kk = c2 * 8 + 4 + q4
        sl = kk % 2
        dw = [(("dq0", "dq1")[sl], dql[kk])]
        vop(lambda e, sl=sl: e.tensor_scalar(qstg[sl][:, :], qstg[sl][:, :],
            sml[:, 52:53], MQ, ALU.mult, ALU.add), dw)
        qt = vop(lambda e, sl=sl: e.tensor_scalar(qstg[sl][:, :], qstg[sl][:, :],
            MQ, None, ALU.subtract), [], sem="p2q2")
        for wv, lo, hi in ((0, 0, 8), (1, 8, 11)):
            trw = []
            if wv == 0:
                trw = [("p2q2", qt), ("p2w2c", P.n("p2w2c"))]
                if c2 >= 1 and q4 == 0:
                    trw.append(("p2st", P.n("p2st")))
            else:
                trw = [("p2w2c", P.n("p2w2c"))]
            for i in range(lo, hi):
                tr(PB[:, (i - lo) * 128:(i - lo + 1) * 128],
                   qstg[sl][:, i * 128:(i + 1) * 128], ident_f[:, :],
                   trw if i == lo else [], "p2t2" if i == hi - 1 else None)
            trn = P.n("p2t2")
            cw = [("p2t2", trn)]
            if c2 >= 1 and q4 == 0 and wv == 0:
                cw.append(("p2m2", mmt2[c2 - 1]))
            vop(lambda e, q4=q4, lo=lo, hi=hi: e.tensor_copy(
                w2col[:, q4 * 11 + lo:q4 * 11 + hi, :],
                _shape(PB[:, 0:(hi - lo) * 128], (hi - lo, 128))), cw,
                sem="p2w2c")
        contok[kk] = P.n("p2t2")

    def w2_mm(c2):
        cpt2[c2] = P.n("p2w2c")
        for ci in range(44):
            for j in range(2):
                w = []
                if ci == 0 and j == 0:
                    w = [("p2w2c", cpt2[c2]), ("p2hq", hqn)]
                    if c2 >= 2:
                        w.append(("p2ev2", evt2[c2 - 2]))
                    else:
                        w.append(("p2ev", evA))
                mm(PAh[c2 % 2][:, j * 512:(j + 1) * 512], w2col[:, ci, :],
                   h_fm[:, ci, j * 512:(j + 1) * 512], ci == 0, ci == 43,
                   w, "p2m2" if (ci == 43 and j == 1) else None)
        mmt2[c2] = P.n("p2m2")

    def evac2(c2):
        vop(lambda e, c2=c2: e.tensor_scalar(scr[:, 0:1024], PAh[c2 % 2],
            sml[:, 96 + c2:97 + c2], None, ALU.mult),
            [("p2m2", mmt2[c2]), ("p2dt", P.n("p2dt"))], sem="p2ev2")
        evt2[c2] = P.n("p2ev2")
        for i in range(8):
            tr(PB[:, i * 128:(i + 1) * 128], scr[:, i * 128:(i + 1) * 128],
               ident_f[:, :],
               [("p2ev2", evt2[c2]), ("p2st", P.n("p2st")),
                ("p2w2c", P.n("p2w2c"))] if i == 0 else [],
               "p2dt" if i == 7 else None)
        dtn = P.n("p2dt")
        for tt in range(8):
            w = [("p2dt", dtn)] + ([("dx2", x2ld)] if c2 == 0 else [])
            vop(lambda e, c2=c2, tt=tt: e.scalar_tensor_tensor(
                stream[:, tt, c2 * 128:(c2 + 1) * 128],
                PB[:, tt * 128:(tt + 1) * 128], sml[:, 80 + tt:81 + tt],
                stream[:, tt, c2 * 128:(c2 + 1) * 128], ALU.mult, ALU.add,
                accum_out=scr2s[:, tt * 16 + c2:tt * 16 + c2 + 1]),
                w if tt == 0 else [], sem="p2st")

    w2_load(0)
    w2_load(1)
    for c2 in range(16):
        for q4 in range(4):
            w2_maxq(c2, q4)
            w2_load(c2 * 8 + q4 + 2)
        w2_scale(c2)
        for q4 in range(4):
            w2_quant(c2, q4)
            w2_load(c2 * 8 + 4 + q4 + 2)
        w2_mm(c2)
        evac2(c2)

    ctx2 = dict(mrow=mrow, sml=sml, scr2s=scr2s)
    emit_ln(P, A, ctx2, "p2", stream, None, 16)
    for tt in range(8):
        P.op("scalar", lambda e, tt=tt: e.dma_start(out=A["xsp"][tt],
             in_=stream[:, tt, :]), [("p2fin", P.n("p2fin"))] if tt == 0 else [],
             ("dsp2", 16), dma=True)
        vop(lambda e, tt=tt: e.tensor_copy(x2sc[:, :], stream[:, tt, :]),
            [("p2fin", P.n("p2fin"))])
        qn = P.n("p2v")
        for w8 in range(2):
            for i in range(8):
                dk = w8 * 8 + i
                tr(PT[:, i * 128:(i + 1) * 128], x2sc[:, dk * 128:(dk + 1) * 128],
                   ident_b[:, :],
                   [("p2v", qn), ("p2xw", P.n("p2xw"))] if i == 0 else [],
                   "p2tr" if i == 7 else None)
            trn = P.n("p2tr")
            vop(lambda e, tt=tt, w8=w8: e.tensor_copy(
                x2fm[:, w8 * 8:(w8 + 1) * 8, tt * 128:(tt + 1) * 128],
                _shape(PT[:, 0:1024], (8, 128))), [("p2tr", trn)], sem="p2xw")
    P.op("vector", lambda e: e.nop(), [("p2xw", P.n("p2xw"))], ("p2done", 1))


def emit_phase34(nc, P, AR, A, env):
    PA, PB, PC, PT, PAh = A["PA"], A["PB"], A["PC"], A["PT"], A["PAh"]
    ident_f, ident_b = env["ident_f"], env["ident_b"]
    onesmat_f, ones_b = env["onesmat_f"], env["ones_b"]
    sml, scr, scr2s = env["sml"], env["scr"], env["scr2s"]
    mrow, rowb, stream = env["mrow"], env["rowb"], env["stream"]
    ssel_b = env["ssel_b"]
    stg, wcol = env["stg"], env["wcol"]

    q_fm = AR.bf(0, 16, 1024)
    x2fm = AR.bf(96 * KB, 16, 1024)
    kstg = AR.bf(128 * KB, 1024)
    vstg = AR.bf(130 * KB, 512)
    vcol = AR.bf(132 * KB, 16, 512)
    enc_b = AR.bf(96 * KB, 16, 1024)
    pt_b = AR.bf(128 * KB, 16, 1024)
    Kh = [AR.bf(160 * KB + i * 4 * KB, 16, 128) for i in range(2)]
    Vh = [AR.bf(168 * KB + i * 4 * KB, 16, 128) for i in range(2)]
    mask_sb = AR.bf(176 * KB, 8, 2, 128)
    p_sb2 = [AR.bf(180 * KB, 2048), env["mrowb"]]

    def vop(emit, waits=(), sem="p3v", eng="vector"):
        return P.op(eng, emit, waits, (sem, 1) if sem else None)

    def mm(out, lhsT, rhs, start, stop, waits=(), sem=None):
        return P.op("tensor", lambda e, o=out, l=lhsT, r=rhs, s1=start, s2=stop:
                    e.matmul(o, l, r, start=s1, stop=s2), waits,
                    (sem, 1) if sem else None)

    def tr(out, in_, idn, waits=(), sem=None):
        return P.op("tensor", lambda e, o=out, i=in_, d=idn: e.transpose(o, i, d),
                    waits, (sem, 1) if sem else None)

    def rowmove(dst, src, waits, sem="dmv"):
        return P.op("gpsimd", lambda e, d=dst, s=src: e.dma_start(out=d, in_=s),
                    waits, (sem, 16), dma=True)

    p2d = P.n("p2done")

    for ci in range(16):
        w = [("p2done", p2d)] if ci < 2 else [("qw1", ci - 1), ("qm1", ci - 1)]
        t = P.op("sync", lambda e, ci=ci: e.dma_start(out=stg[ci % 2][:, :],
             in_=A["l1q"][ci].rearrange("p a b -> p (a b)")), w,
             (f"ds{ci % 2}", 16), dma=True)
        vop(lambda e, ci=ci: e.tensor_copy(wcol[ci % 2][:, :, :],
            _shape(stg[ci % 2][:, :], (16, 128))), [(f"ds{ci % 2}", t)],
            sem="qw1", eng="gpsimd")
    for ci in range(16):
        half = PAh[ci % 2]
        for dk in range(16):
            for j in range(2):
                w = []
                if dk == 0 and j == 0:
                    w = [("qw1", ci + 1)] + ([("qe1", ci - 1)] if ci >= 2 else [])
                mm(half[:, j * 512:(j + 1) * 512], wcol[ci % 2][:, dk, :],
                   x2fm[:, dk, j * 512:(j + 1) * 512], dk == 0, dk == 15,
                   w, "qm1" if (dk == 15 and j == 1) else None)
        mmn = P.n("qm1")
        sq = vop(lambda e, half=half: e.activation(scr.bitcast(BF16)[:, 0:1024], half,
                 ACT.Square), [("qm1", mmn)], sem="ss1", eng="scalar")
        for j in range(2):
            mm(PB[0:30, j * 512:(j + 1) * 512], ssel_b[:, ci, 0:30],
               scr.bitcast(BF16)[:, j * 512:(j + 1) * 512], ci == 0, ci == 15,
               [("ss1", sq)], "qst1" if (ci == 15 and j == 1) else None)
        vop(lambda e, ci=ci, half=half: e.tensor_copy(q_fm[:, ci, :], half),
            [("ss1", sq)], sem="qe1")
    ste = vop(lambda e: e.tensor_copy(mrow[0:16, 0:1024], PB[0:16, :]),
              [("qst1", 1)], sem="p3v")
    q2mn = P.n("p3v")

    for ci in range(16):
        w = [("qe1", 16)] if ci < 2 else [("kw1", ci - 1), ("km1", ci - 1)]
        t = P.op("sync", lambda e, ci=ci: e.dma_start(out=stg[ci % 2][:, :],
             in_=A["l1k"][ci].rearrange("p a b -> p (a b)")), w,
             (f"ds{ci % 2}", 16), dma=True)
        vop(lambda e, ci=ci: e.tensor_copy(wcol[ci % 2][:, :, :],
            _shape(stg[ci % 2][:, :], (16, 128))), [(f"ds{ci % 2}", t)],
            sem="kw1", eng="gpsimd")
    for ci in range(16):
        half = PAh[ci % 2]
        for dk in range(16):
            for j in range(2):
                w = []
                if dk == 0 and j == 0:
                    w = [("kw1", ci + 1)] + ([("dkc", 16 * (ci - 1))] if ci >= 2 else [])
                mm(half[:, j * 512:(j + 1) * 512], wcol[ci % 2][:, dk, :],
                   x2fm[:, dk, j * 512:(j + 1) * 512], dk == 0, dk == 15,
                   w, "km1" if (dk == 15 and j == 1) else None)
        mmn = P.n("km1")
        sq = vop(lambda e, half=half: e.activation(scr.bitcast(BF16)[:, 0:1024], half,
                 ACT.Square), [("km1", mmn)], sem="ss1", eng="scalar")
        for j in range(2):
            mm(PB[0:30, j * 512:(j + 1) * 512], ssel_b[:, ci, 0:30],
               scr.bitcast(BF16)[:, j * 512:(j + 1) * 512], ci == 0, ci == 15,
               [("ss1", sq), ("p3v", q2mn)], "kst1" if (ci == 15 and j == 1) else None)
        ke = vop(lambda e, ci=ci, half=half: e.tensor_copy(kstg[:, :], half),
                 [("ss1", sq), ("dkc", P.n("dkc"))], sem="p3v")
        P.op("scalar", lambda e, ci=ci: e.dma_start(out=A["cc_k"][ci],
             in_=kstg[:, :]), [("p3v", ke)], ("dkc", 16), dma=True)
    ken = P.n("p3v")
    ste2 = vop(lambda e: e.tensor_copy(mrow[32:48, 0:1024], PB[0:16, :]),
               [("kst1", 1)], sem="p3v")
    vop(lambda e: e.reduce_max(sml[32:48, 25:26], mrow[32:48, 0:1024], AX),
        [("p3v", ste2)], sem="p3v")
    mvs = rowmove(sml[0:16, 25:26], sml[32:48, 25:26], [("p3v", P.n("p3v"))])
    st3 = vop(lambda e: e.tensor_copy(sml[0:16, 26:27], sml[0:16, 25:26]),
              [("dmv", P.n("dmv"))], sem="p3v")
    P.op("scalar", lambda e: e.dma_start(out=A["cc_s"].ap(), in_=sml[0:16, 25:27]),
         [("p3v", st3)], ("dks", 16), dma=True)

    for og in range(4):
        for dk in range(16):
            w = [("km1", 16)] if (og == 0 and dk < 2) else \
                [("vw1", 16 * og + dk - 1)]
            if og > 0 and dk == 0:
                w = w + [("vm1", 8 * og)]
            t = P.op("sync", lambda e, og=og, dk=dk: e.dma_start(
                 out=stg[dk % 2][:, 0:512], in_=A["l1v"][dk][:, og * 512:(og + 1) * 512]),
                 w, (f"ds{dk % 2}", 16), dma=True)
            vop(lambda e, dk=dk: e.tensor_copy(vcol[:, dk, :], stg[dk % 2][:, 0:512]),
                [(f"ds{dk % 2}", t)], sem="vw1", eng="gpsimd")
        for lt in range(8):
            slot = PA[:, (lt % 4) * 512:(lt % 4 + 1) * 512]
            for dk in range(16):
                w = []
                if dk == 0:
                    w = [("vw1", 16 * (og + 1))] + \
                        ([("dvc", 16 * (8 * og + lt - 3))] if (8 * og + lt) >= 4 else [])
                    if og == 0 and lt == 0:
                        w.append(("p3v", ken))
                mm(slot, x2fm[:, dk, lt * 128:(lt + 1) * 128], vcol[:, dk, :],
                   dk == 0, dk == 15, w, "vm1" if dk == 15 else None)
            mmn = P.n("vm1")
            ve = vop(lambda e, slot=slot: e.tensor_copy(vstg[:, :], slot),
                     [("vm1", mmn), ("dvc", P.n("dvc"))], sem="p3v")
            P.op("scalar", lambda e, lt=lt, og=og: e.dma_start(
                 out=A["cc_v"][lt][:, og * 512:(og + 1) * 512], in_=vstg[:, :]),
                 [("p3v", ve)], ("dvc", 16), dma=True)

    RG = [[0, 1], [2, 3], [4, 5], [6, 7]]
    P.op("gpsimd", lambda e: e.collective_compute(
        "AllGather", ALU.bypass, replica_groups=RG,
        ins=[A["cc_k"].ap().opt()], outs=[A["gg_k"].ap().opt()]),
        [("dkc", P.n("dkc")), ("dvc", P.n("dvc")), ("dks", P.n("dks"))], ("ccs", 1))
    P.op("gpsimd", lambda e: e.collective_compute(
        "AllGather", ALU.bypass, replica_groups=RG,
        ins=[A["cc_v"].ap().opt()], outs=[A["gg_v"].ap().opt()]), [], ("ccs", 1))
    P.op("gpsimd", lambda e: e.collective_compute(
        "AllGather", ALU.bypass, replica_groups=RG,
        ins=[A["cc_s"].ap().opt()], outs=[A["gg_s"].ap().opt()]), [], ("ccs", 1))

    t = P.op("sync", lambda e: e.dma_start(out=sml[0:16, 20:22], in_=A["gg_s"][0]),
             [("ccs", 3)], ("dgs", 16), dma=True)
    t2 = P.op("sync", lambda e: e.dma_start(out=sml[0:16, 22:24], in_=A["gg_s"][1]),
              [], ("dgs2", 16), dma=True)
    vop(lambda e: e.tensor_max(sml[0:16, 28:29], sml[0:16, 20:21], sml[0:16, 22:23]),
        [("dgs", t), ("dgs2", t2)], sem="p3v")
    e3 = vop(lambda e: e.tensor_scalar(sml[0:16, 28:29], sml[0:16, 28:29], -INV2C,
             None, ALU.mult), [], sem="p3v")
    tr(PC[0:1, 0:16], sml[0:16, 28:29], ident_f[0:16, 0:16], [("p3v", e3)], "p3m")
    e4 = vop(lambda e: e.tensor_copy(sml[0:1, 4:20], PC[0:1, 0:16]),
             [("p3m", P.n("p3m"))], sem="p3v")
    mm(PC[0:128, 0:16], onesmat_f[0:1, :], sml[0:1, 4:20], True, True,
       [("p3v", e4)], "p3m")
    Mb2 = sml[:, 108:124]
    e5 = vop(lambda e: e.tensor_copy(sml[:, 108:124], PC[:, 0:16]),
             [("p3m", P.n("p3m"))], sem="p3v")
    mbn = P.n("p3v")

    tm0 = P.op("sync", lambda e: e.dma_start(out=stg[0][:, :],
          in_=A["maskd"].rearrange("p a b f -> p (a b f)")), [("ccs", 3)],
          ("ds0", 16), dma=True)
    tm_ = vop(lambda e: e.tensor_copy(mask_sb[:, :, :, :],
              _shape(stg[0][:, :], (8, 2, 128))), [("ds0", tm0)], sem="dmk")

    negm2 = _shape(scr2s[:, 0:128], (8, 16))
    for qt in range(8):
        P.op("tensor", lambda e, qt=qt: e.transpose(PC[:, 0:16],
             mrow[0:16, qt * 128:(qt + 1) * 128], ident_f[0:16, 0:16]),
             [("p3v", P.n("p3v")), ("ccs", 3)] if qt == 0 else [("nmt2", qt)], ("p3m", 1))
        vop(lambda e, qt=qt: e.scalar_tensor_tensor(negm2[:, qt, :],
            PC[:, 0:16], -INV2C, Mb2[:, 0:16], ALU.mult, ALU.add),
            [("p3m", P.n("p3m"))], sem="nmt2")
    nmtn2 = P.n("nmt2")

    lasttr1 = {}
    for qh in range(16):
        for r in range(2):
            tk = P.op("sync", lambda e, qh=qh, r=r: e.dma_start(
                 out=Kh[qh % 2][:, r:16:2, :],
                 in_=A["gg_k"][r, qh].rearrange("p (l f) -> p l f", f=128)),
                 ([("ccs", 3), ("dmk", 1)] if qh < 2 else [("p4kf", qh - 1)])
                 if r == 0 else [],
                 (f"dkh{qh % 2}", 16), dma=True)
            tv = P.op("sync", lambda e, qh=qh, r=r: e.dma_start(
                 out=Vh[qh % 2][:, r:16:2, :],
                 in_=A["gg_v"][r][:, :, qh * 128:(qh + 1) * 128].rearrange(
                     "l p f -> p l f")),
                 [], (f"dvh{qh % 2}", 16), dma=True)
        base = P.n("exp1")
        for qt in range(8):
            ns = (2 * qt + 2) * 128
            npc = (ns + 511) // 512
            for j in range(npc):
                c0, c1 = j * 512, min((j + 1) * 512, ns)
                w = []
                if j == 0:
                    if qh == 0 and qt == 0:
                        w = [("nmt2", nmtn2), ("dmk", 1), (f"dkh{qh % 2}", tk)]
                    else:
                        w = [("exp1", P.n("exp1"))]
                    if qt == 0:
                        w.append((f"dkh{qh % 2}", tk))
                        if qh:
                            w.append(("bcd1", qh))
                mm(PA[:, c0:c1], q_fm[:, qh, qt * 128:(qt + 1) * 128],
                   Kh[qh % 2].rearrange("p a f -> p (a f)")[:, c0:c1],
                   True, True, w, "smm1" if j == npc - 1 else None)
            smm = P.n("smm1")
            mk = vop(lambda e, qt=qt, ns=ns: e.tensor_add(PA[:, 2 * qt * 128:ns],
                     PA[:, 2 * qt * 128:ns], mask_sb[:, qt, :, :].rearrange(
                     "p a f -> p (a f)")), [("smm1", smm)], sem="p3v")
            ew = [("p3v", mk)]
            if (qt % 2) in lasttr1:
                ew.append(("ptr1", lasttr1[qt % 2]))
            vop(lambda e, qh=qh, qt=qt, ns=ns: e.activation(
                p_sb2[qt % 2][:, 0:ns],
                PA[:, 0:ns], ACT.Exp, bias=negm2[:, qt, qh:qh + 1], scale=ISQ,
                accum_out=sml[:, 50 + qt:51 + qt]),
                ew, sem="exp1", eng="scalar")
            expn = P.n("exp1")
            nst = 2 * qt + 2
            for w8 in range((nst + 7) // 8):
                lo = w8 * 8
                hi = min(lo + 8, nst)
                for i in range(lo, hi):
                    tr(PT[:, (i - lo) * 128:(i - lo + 1) * 128],
                       p_sb2[qt % 2][:, i * 128:(i + 1) * 128], ident_b[:, :],
                       [("exp1", expn), ("ptw1", P.n("ptw1"))] if i == lo else [],
                       "ptr1" if i == hi - 1 else None)
                trn = P.n("ptr1")
                nw = hi - lo
                pw = [("ptr1", trn)] + ([("avm1", qh)] if (qt == 0 and w8 == 0 and qh)
                                        else [])
                vop(lambda e, qt=qt, lo=lo, nw=nw: e.tensor_copy(
                    pt_b[:, lo:lo + nw, qt * 128:(qt + 1) * 128],
                    _shape(PT[:, 0:nw * 128], (nw, 128))), pw, sem="ptw1")
            lasttr1[qt % 2] = P.n("ptr1")
        for qt in range(8):
            tr(PC[0:1, 0:128], sml[:, 50 + qt:51 + qt], ident_f[:, :],
               [("ptw1", P.n("ptw1"))] if qt == 0 else [("rcd1", P.n("rcd1"))], "p3m")
            vop(lambda e, qt=qt: e.tensor_copy(rowb[0:1, qt * 128:(qt + 1) * 128],
                PC[0:1, 0:128]), [("p3m", P.n("p3m"))], sem="rcd1")
        vop(lambda e: e.reciprocal(rowb[0:1, 0:1024], rowb[0:1, 0:1024]),
            [], sem="rcd1")
        mm(PAh[0][:, 0:512], onesmat_f[0:1, :], rowb[0:1, 0:512], True, True,
           [("rcd1", P.n("rcd1"))])
        mm(PAh[0][:, 512:1024], onesmat_f[0:1, :], rowb[0:1, 512:1024],
           True, True, [], "p3m")
        bcn = P.n("p3m")
        ev = vop(lambda e: e.tensor_copy(scr[:, 0:1024], PAh[0]), [("p3m", bcn)],
                 sem="bcd1")
        for st in range(16):
            qs = (st // 2) * 128
            for j0 in range(2):
                lo, hi = j0 * 512, (j0 + 1) * 512
                if qs >= hi:
                    continue
                last = (st == 7) if j0 == 0 else (st == 15)
                aw = []
                if st == 0 and j0 == 0:
                    aw = ([("encd1", qh)] if qh else []) + [(f"dvh{qh % 2}", tv)]
                mm(PB[:, max(qs, lo):hi], Vh[qh % 2][:, st, :],
                   pt_b[:, st, max(qs, lo):hi], st == 0, last,
                   aw, "avm1" if (st == 15 and j0 == 1) else None)
        avn = P.n("avm1")
        P.op("tensor", lambda e: e.nop(), [("avm1", avn)], ("p4kf", 1))
        vop(lambda e, qh=qh: e.tensor_mul(enc_b[:, qh, :], PB[:, 0:1024],
            scr[:, 0:1024]), [("bcd1", P.n("bcd1")), ("avm1", avn)], sem="encd1")

    env2 = dict(env)
    emit_proj_ln(nc, P, AR, A, dict(
        wsrc="l1o", enc=enc_b, stream=stream, stg=stg, wcol=wcol,
        ident_f=ident_f, mrow=mrow, scr=scr, scr2s=scr2s, sml=sml,
        pre_wait=[("encd1", 16)], stt_extra=[],
        sem_prefix="o1", nchunks=16))




def emit_phase5(nc, P, AR, A, env):
    PA, PB, PC, PT, PAh = A["PA"], A["PB"], A["PC"], A["PT"], A["PAh"]
    ident_b = env["ident_b"]
    ident_f = env["ident_f"]
    sml, scr = env["sml"], env["scr"]
    scr2s, mrow, stream = env["scr2s"], env["mrow"], env["stream"]
    h128_b, h64b_b = env["h128_b"], env["h64b_b"]
    FSC = 1.0 / math.sqrt(H1)

    x3fm = AR.bf(0, 16, 512)
    w3c5 = [AR.bf(16 * KB + i * 4 * KB, 16, 128) for i in range(2)]
    w3c5f = [AR.bf(16 * KB + i * 4 * KB, 2048) for i in range(2)]
    ha = [AR.bf(24 * KB + i * KB, 512) for i in range(2)]
    sil = AR.f32(26 * KB, 512)
    fwc = [AR.bf(28 * KB + i * KB, 512) for i in range(2)]
    pstag = [AR.bf(30 * KB + i * KB, 512) for i in range(2)]
    g1T = AR.bf(96 * KB, 64, 512)
    stgw1 = AR.f32(160 * KB, 2048)
    stgw3 = AR.f32(168 * KB, 2048)
    wc5 = [AR.bf(176 * KB + i * 4 * KB, 16, 128) for i in range(2)]
    wc5f = [AR.bf(176 * KB + i * 4 * KB, 2048) for i in range(2)]
    w2s = [AR.f32(160 * KB + i * 2 * KB, 512) for i in range(2)]
    w2q = [AR.bf(164 * KB + i * KB, 512) for i in range(2)]
    x3c = AR.bf(166 * KB, 2048)

    def vop(emit, waits=(), sem="p5v", eng="vector"):
        return P.op(eng, emit, waits, (sem, 1) if sem else None)

    def mm(out, lhsT, rhs, start, stop, waits=(), sem=None):
        return P.op("tensor", lambda e, o=out, l=lhsT, r=rhs, s1=start, s2=stop:
                    e.matmul(o, l, r, start=s1, stop=s2), waits,
                    (sem, 1) if sem else None)

    def tr(out, in_, idn, waits=(), sem=None):
        return P.op("tensor", lambda e, o=out, i=in_, d=idn: e.transpose(o, i, d),
                    waits, (sem, 1) if sem else None)

    o1f = P.n("o1fin")
    for g in range(2):
        for ti in range(4):
            tt = g * 4 + ti
            w = []
            if ti == 0:
                w = [("o1fin", o1f)] if g == 0 else \
                    [("p5dm", P.n("p5dm")), ("p5wq", P.n("p5wq"))]
            vop(lambda e, tt=tt: e.tensor_copy(x3c[:, :], stream[:, tt, :]), w)
            cn = P.n("p5v")
            for w8 in range(2):
                for i in range(8):
                    dk = w8 * 8 + i
                    trw = []
                    if i == 0:
                        trw = [("p5v", cn), ("p5w", P.n("p5w"))]
                        if ti == 0 and w8 == 0:
                            trw += [("p5f", P.n("p5f")), ("p5g", P.n("p5g"))]
                    tr(PT[:, i * 128:(i + 1) * 128], x3c[:, dk * 128:(dk + 1) * 128],
                       ident_b[:, :], trw, "p5t" if i == 7 else None)
                trn = P.n("p5t")
                vop(lambda e, ti=ti, w8=w8: e.tensor_copy(
                    x3fm[:, w8 * 8:(w8 + 1) * 8, ti * 128:(ti + 1) * 128],
                    _shape(PT[:, 0:1024], (8, 128))), [("p5t", trn)], sem="p5w")
        x3n = P.n("p5w")

        qtok, q3tok, mmtok, hatok, p1tok, g1tok = {}, {}, {}, {}, {}, {}
        dA, dB, dC, dD, satok = {}, {}, {}, {}, {}

        def emit_scatter(a, g=g):
            pp = a % 2
            w = [("p5p1", p1tok[a])]
            if a >= 2:
                w += [("dsa", satok[a - 2][0]), ("dsb", satok[a - 2][1])]
            g1tok[a] = vop(lambda e, pp=pp: e.tensor_copy(
                pstag[pp][:, :], PB[:, pp * 512:pp * 512 + 512]), w, sem="p5g")
            sw = [("p5g", g1tok[a])]
            if g == 1 and a == 0:
                sw.append(("p5m2", P.n("p5m2")))
            ta = P.op("gpsimd", lambda e, a=a, pp=pp: e.dma_start(
                out=g1T[a:a + 1, :, :], in_=pstag[pp][0:64, :]), sw,
                ("dsa", 16), dma=True, chain=False)
            tb = P.op("gpsimd", lambda e, a=a, pp=pp: e.dma_start(
                out=g1T[64 + a:65 + a, :, :], in_=pstag[pp][64:128, :]),
                list(sw), ("dsb", 16), dma=True, chain=False)
            satok[a] = (ta, tb)

        def emit_loads(a):
            w1w = [("p5w", x3n)] if a == 0 else [("p5q", qtok[a - 1])]
            w3w = [("p5w", x3n)] if a == 0 else [("p5q3", q3tok[a - 1])]
            src1 = A["l1w1"][a].rearrange("p c f -> p (c f)")
            src3 = A["l1w3"][a].rearrange("p c f -> p (c f)")
            dA[a] = P.op("sync", lambda e, s=src1: e.dma_start(
                out=stgw1[:, 0:1024], in_=s[:, 0:1024]), w1w, ("dwA", 16), dma=True)
            dB[a] = P.op("gpsimd", lambda e, s=src1: e.dma_start(
                out=stgw1[:, 1024:2048], in_=s[:, 1024:2048]), list(w1w),
                ("dwB", 16), dma=True, chain=False)
            dC[a] = P.op("scalar", lambda e, s=src3: e.dma_start(
                out=stgw3[:, 0:1024], in_=s[:, 0:1024]), w3w, ("dwC", 16),
                dma=True, chain=False)
            dD[a] = P.op("sync", lambda e, s=src3: e.dma_start(
                out=stgw3[:, 1024:2048], in_=s[:, 1024:2048]), list(w3w),
                ("dwD", 16), dma=True)

        def emit_casts(a):
            cw = [("dwA", dA[a]), ("dwB", dB[a])]
            c3 = [("dwC", dC[a]), ("dwD", dD[a])]
            if a >= 2:
                cw.append(("p5m", mmtok[a - 2]))
                c3.append(("p5m", mmtok[a - 2]))
            qtok[a] = vop(lambda e, a=a: e.tensor_copy(
                wc5f[a % 2][:, :], stgw1[:, :]), cw, sem="p5q")
            q3tok[a] = P.op("scalar", lambda e, a=a: e.activation(
                w3c5f[a % 2][:, :], stgw3[:, :], ACT.Copy), c3, ("p5q3", 1))

        emit_loads(0)
        emit_casts(0)
        for a in range(64):
            if a + 1 < 64:
                emit_loads(a + 1)
                emit_casts(a + 1)
            ga = (a % 2) * 512
            ua = 1024 + (a % 2) * 512
            for dk in range(16):
                w = []
                if dk == 0:
                    w = [("p5q", qtok[a]), ("p5q3", q3tok[a])]
                    if a >= 2:
                        w.append(("p5e", hatok[a - 2]))
                mm(PA[:, ga:ga + 512], wc5[a % 2][:, dk, :], x3fm[:, dk, :],
                   dk == 0, dk == 15, w, None)
                mm(PA[:, ua:ua + 512], w3c5[a % 2][:, dk, :], x3fm[:, dk, :],
                   dk == 0, dk == 15, [], "p5m" if dk == 15 else None)
            mmtok[a] = P.n("p5m")
            if a >= 1:
                pp = (a - 1) % 2
                p1tok[a - 1] = mm(PB[:, pp * 512:pp * 512 + 512], h128_b[:, :],
                                  ha[pp][:, :], True, True,
                                  [("p5e", hatok[a - 1])] +
                                  ([("p5g", g1tok[a - 3])] if a >= 3 else []),
                                  "p5p1")
            sgw = [("p5m", mmtok[a])]
            if a >= 1:
                sgw.append(("p5e", hatok[a - 1]))
            sg = P.op("scalar", lambda e, ga=ga: e.activation(
                sil[:, :], PA[:, ga:ga + 512], ACT.Sigmoid), sgw, ("sl5", 1))
            vop(lambda e, ga=ga: e.tensor_mul(sil[:, :], PA[:, ga:ga + 512],
                sil[:, :]), [("sl5", sg)], sem=None)
            hw = [("p5p1", p1tok[a - 2])] if a >= 2 else []
            hatok[a] = vop(lambda e, a=a, ua=ua: e.tensor_mul(
                ha[a % 2][:, :], sil[:, :], PA[:, ua:ua + 512]), hw, sem="p5e")
            if a >= 1:
                emit_scatter(a - 1)
        p1tok[63] = mm(PB[:, 512:1024], h128_b[:, :], ha[1][:, :], True, True,
                       [("p5e", hatok[63]), ("p5g", g1tok[61])], "p5p1")
        emit_scatter(63)
        g1n = P.n("p5g")
        qnA, q3nA = P.n("p5q"), P.n("p5q3")
        dsn = (P.n("dsa"), P.n("dsb"))

        w2tok, wqtok, p2tok, fetok, dmtok = {}, {}, {}, {}, {}

        def emit_bloads(k):
            oq, chi = k // 64, k % 64
            sl = k % 2
            w2w = [("p5wq", wqtok[k - 2])] if k >= 2 else \
                  [("p5q", qnA), ("p5q3", q3nA)]
            ta = P.op("sync", lambda e, oq=oq, chi=chi, sl=sl: e.dma_start(
                out=w2s[sl][0:64, :],
                in_=A["l1w2"][oq, chi, 0:64, :]), w2w,
                ("dw2a", 16), dma=True)
            tb = P.op("scalar", lambda e, oq=oq, chi=chi, sl=sl: e.dma_start(
                out=w2s[sl][64:128, :],
                in_=A["l1w2"][oq, chi, 64:128, :]), list(w2w),
                ("dw2b", 16), dma=True, chain=False)
            w2tok[k] = (ta, tb)

        def emit_down(k):
            oq, chi = k // 64, k % 64
            sl = k % 2
            for ot in range(4):
                w = []
                if ot == 0:
                    w = [("p5f", fetok[k]), ("p5wq", wqtok[k])]
                    if chi == 0 and (oq or g):
                        w.append(("p5ev", P.n("p5ev")))
                mm(PA[:, ot * 512:(ot + 1) * 512],
                   w2q[sl][:, ot * 128:(ot + 1) * 128],
                   fwc[sl][:, :], chi == 0, chi == 63, w,
                   "p5dm" if ot == 3 else None)
            dmtok[k] = P.n("p5dm")

        def emit_evac(kbase):
            oq = kbase // 64
            dmn = dmtok[kbase + 63]
            for ot in range(4):
                vop(lambda e, ot=ot: e.tensor_scalar(scr[:, 0:512],
                    PA[:, ot * 512:(ot + 1) * 512], FSC, None, ALU.mult),
                    [("p5dm", dmn)] if ot == 0 else [("p5dt", P.n("p5dt"))],
                    sem="p5ev")
                evn = P.n("p5ev")
                for ti in range(4):
                    tr(PB[:, ti * 128:(ti + 1) * 128],
                       scr[:, ti * 128:(ti + 1) * 128], ident_f[:, :],
                       [("p5ev", evn), ("p5st", P.n("p5st"))] if ti == 0 else [],
                       "p5dt" if ti == 3 else None)
                dtn = P.n("p5dt")
                for ti in range(4):
                    tt = g * 4 + ti
                    oc = oq * 4 + ot
                    vop(lambda e, tt=tt, oc=oc, ti=ti: e.scalar_tensor_tensor(
                        stream[:, tt, oc * 128:(oc + 1) * 128],
                        PB[:, ti * 128:(ti + 1) * 128], 1.0,
                        stream[:, tt, oc * 128:(oc + 1) * 128], ALU.mult, ALU.add,
                        accum_out=scr2s[:, ti * 16 + oc:ti * 16 + oc + 1]),
                        [("p5dt", dtn)] if ti == 0 else [], sem="p5st")

        emit_bloads(0)
        for k in range(256):
            oq, chi = k // 64, k % 64
            sl = k % 2
            pth = k % 2
            if k + 1 < 256:
                emit_bloads(k + 1)
            cw = [("dw2a", w2tok[k][0]), ("dw2b", w2tok[k][1])]
            if k >= 2:
                cw.append(("p5dm", dmtok[k - 2]))
            wqtok[k] = vop(lambda e, sl=sl: e.tensor_copy(w2q[sl][:, :],
                           w2s[sl][:, :]), cw, sem="p5wq")
            p2dst = PC[:, 0:512] if pth == 0 else PB[:, 512:1024]
            pw = [("dsa", dsn[0]), ("dsb", dsn[1])] if k < 2 else \
                 [("p5f", fetok[k - 2])]
            mm(p2dst, h64b_b[:, :], g1T[:, chi, :], True, True, pw, "p5m2")
            p2tok[k] = P.n("p5m2")
            fetok[k] = vop(lambda e, sl=sl, d=p2dst: e.tensor_copy(
                fwc[sl][:, :], d), [("p5m2", p2tok[k])], sem="p5f")
            if k >= 1:
                emit_down(k - 1)
            if chi == 0 and k >= 64:
                emit_evac(k - 64)
        emit_down(255)
        emit_evac(192)
        for ti in range(4):
            tt = g * 4 + ti
            vop(lambda e, ti=ti: e.reduce_sum(sml[:, 0:1],
                scr2s[:, ti * 16:ti * 16 + 16], AX), [], sem="p5ln")
            vop(lambda e, tt=tt: e.activation(mrow[:, 0:1024], stream[:, tt, 0:1024],
                ACT.Square, accum_out=sml[:, 1:2]), [("p5ln", P.n("p5ln"))],
                sem="p5ln", eng="scalar")
            vop(lambda e, tt=tt: e.activation(mrow[:, 0:1024], stream[:, tt, 1024:2048],
                ACT.Square, accum_out=sml[:, 9:10]), [], sem="p5ln", eng="scalar")
            vop(lambda e: e.tensor_add(sml[:, 1:2], sml[:, 1:2], sml[:, 9:10]),
                [("p5ln", P.n("p5ln"))], sem="p5ln")
            vop(lambda e: e.tensor_scalar(sml[:, 2:3], sml[:, 0:1], 1.0 / D, None,
                ALU.mult))
            vop(lambda e: e.tensor_scalar(sml[:, 3:4], sml[:, 1:2], 1.0 / D, None,
                ALU.mult))
            vop(lambda e: e.tensor_mul(sml[:, 4:5], sml[:, 2:3], sml[:, 2:3]))
            vop(lambda e: e.tensor_sub(sml[:, 5:6], sml[:, 3:4], sml[:, 4:5]),
                [], sem="p5ln")
            vop(lambda e: e.activation(sml[:, 6:7], sml[:, 5:6], ACT.Sqrt,
                bias=sml[:, 49:50]), [("p5ln", P.n("p5ln"))], sem="p5ln",
                eng="scalar")
            vop(lambda e: e.reciprocal(sml[:, 7:8], sml[:, 6:7]),
                [("p5ln", P.n("p5ln"))], sem="p5ln")
            vop(lambda e, tt=tt: e.tensor_scalar(stream[:, tt, :], stream[:, tt, :],
                sml[:, 2:3], sml[:, 7:8], ALU.subtract, ALU.mult),
                [("p5ln", P.n("p5ln"))], sem="p5fin")
            qd = ("scalar", "doutA") if ti % 2 == 0 else ("gpsimd", "doutB")
            P.op(qd[0], lambda e, tt=tt: e.dma_start(out=A["out"][tt],
                 in_=stream[:, tt, :]), [("p5fin", P.n("p5fin"))], (qd[1], 16),
                 dma=True, chain=False)
    P.op("sync", lambda e: e.nop(),
         [("doutA", P.n("doutA")), ("doutB", P.n("doutB"))], ("fin", 1))


def _hadamard(n):
    h = np.array([[1.0]], dtype=np.float32)
    while h.shape[0] < n:
        h = np.block([[h, h], [h, -h]])
    return h.astype(np.float32)


def _colblocks(wt, nci, ndk):
    ind, outd = wt.shape
    assert ind == ndk * 128 and outd == nci * 128
    return np.ascontiguousarray(
        wt.reshape(ndk, 128, nci, 128).transpose(2, 1, 0, 3))


def _rope_tables(cols):
    inv = 1.0 / (10000.0 ** (np.arange(0, HD, 2, dtype=np.float32) / HD))
    ang = cols[:, None].astype(np.float32) * inv[None, :]
    c = np.cos(ang).T
    s = np.sin(ang).T
    ct = np.concatenate([c, c], axis=0)
    st = np.concatenate([-s, s], axis=0)
    return np.ascontiguousarray(ct), np.ascontiguousarray(st)


_SHARED_CACHE = {}


def _prep_shared(inputs):
    key = id(inputs.get("L0_attn_q_kernel"))
    if _SHARED_CACHE.get("key") == key:
        return _SHARED_CACHE["val"]
    f32 = np.float32
    S = {}
    S["l0q"] = _colblocks(np.asarray(inputs["L0_attn_q_kernel"], f32), 16, 16)
    S["l0k"] = _colblocks(np.asarray(inputs["L0_attn_k_kernel"], f32), 4, 16)
    S["l0v"] = np.ascontiguousarray(np.asarray(inputs["L0_attn_v_kernel"], f32).reshape(16, 128, 512))
    S["l0o"] = _colblocks(np.asarray(inputs["L0_attn_out_kernel"], f32), 16, 16)
    S["l0w1"] = np.ascontiguousarray(np.asarray(inputs["L0_ffn_w_gate_up"], f32).reshape(88, 128, D))
    S["l0w2"] = np.ascontiguousarray(np.asarray(inputs["L0_ffn_w_down"], f32).reshape(16, 128, I0))
    S["l1q"] = _colblocks(np.asarray(inputs["L1_attn_Wq"], f32).T, 16, 16)
    S["l1k"] = _colblocks(np.asarray(inputs["L1_attn_Wk"], f32).T, 16, 16)
    S["l1v"] = np.ascontiguousarray(np.asarray(inputs["L1_attn_Wv"], f32).T.reshape(16, 128, D))
    S["l1o"] = _colblocks(np.asarray(inputs["L1_attn_Wo"], f32).T, 16, 16)
    w1 = np.asarray(inputs["L1_ffn_W1"], f32)
    w3 = np.asarray(inputs["L1_ffn_W3"], f32)
    S["l1w1"] = np.ascontiguousarray(w1.reshape(64, 128, 16, 128).transpose(0, 3, 2, 1))
    S["l1w3"] = np.ascontiguousarray(w3.reshape(64, 128, 16, 128).transpose(0, 3, 2, 1))
    w2t = np.ascontiguousarray(np.asarray(inputs["L1_ffn_W2"], f32).T)
    S["l1w2"] = np.ascontiguousarray(
        w2t.reshape(64, 64, 2, D).transpose(1, 2, 0, 3).reshape(64, 128, 4, 512)
        .transpose(2, 0, 1, 3))
    S["ident"] = np.eye(128, dtype=f32)
    perm = np.empty(128, np.int64)
    perm[:64] = 2 * np.arange(64)
    perm[64:] = 2 * np.arange(64) + 1
    S["h128"] = np.ascontiguousarray(_hadamard(128)[:, perm])
    h64 = _hadamard(64)
    S["h64b"] = np.block([[h64, np.zeros((64, 64), f32)],
                          [np.zeros((64, 64), f32), h64]]).astype(f32)
    sel = np.zeros((4, 16), f32)
    for kv in range(4):
        sel[kv, 4 * kv:4 * kv + 4] = 1.0
    S["gqa"] = sel
    swp = np.zeros((128, 128), f32)
    for i in range(128):
        swp[(i + 64) % 128, i] = 1.0
    S["swp"] = swp
    cf, sf = _rope_tables(np.arange(T))
    S["ropec_f"], S["ropes_f"] = cf, sf
    S["epsc"] = np.tile(np.array([[1e-6, 1e-5]], np.float32), (128, 1))
    M_Q2H, M_QTOT, M_KT0, M_KT1 = 0, 16, 17, 18
    M_K2H0, M_K2H1 = 20, 24
    sel = np.zeros((24, 128, 32), np.float32)
    for ci in range(16):
        sel[ci, :, M_Q2H + ci] = 1.0
        sel[ci, :, M_QTOT] = 1.0
    for ci in range(4):
        sel[16 + ci, :, M_K2H0 + ci] = 1.0
        sel[16 + ci, :, M_KT0] = 1.0
        sel[20 + ci, :, M_K2H1 + ci] = 1.0
        sel[20 + ci, :, M_KT1] = 1.0
    S["ssel"] = np.ascontiguousarray(sel.transpose(1, 0, 2))
    _SHARED_CACHE["key"] = key
    _SHARED_CACHE["val"] = S
    return S


def _prep_core(inputs, c, S):
    f32 = np.float32
    b, h = c // 2, c % 2
    own = [h + 2 * i for i in range(8)]
    x = np.asarray(inputs["L0_attn_x"], f32)[b]
    xt = x.reshape(16, 128, D)
    M = dict(S)
    M["xs0"] = np.ascontiguousarray(xt[own])
    xT = np.ascontiguousarray(x.T)
    M["xfm"] = xT.reshape(16, 128, T)
    M["xfmo"] = np.ascontiguousarray(
        xT.reshape(16, 128, 16, 128)[:, :, own, :].reshape(16, 128, 1024))
    owncols = np.concatenate([np.arange(a * 128, (a + 1) * 128) for a in own])
    co, so = _rope_tables(owncols)
    M["ropec_o"], M["ropes_o"] = co, so
    mask = np.zeros((8, 2, 128, 128), f32)
    for qt in range(8):
        for j in range(2):
            abs_s = (2 * qt + j) * 128 + np.arange(128)[:, None]
            abs_q = (2 * qt + h) * 128 + np.arange(128)[None, :]
            mask[qt, j] = np.where(abs_s <= abs_q, 0.0, NEG)
    M["maskd"] = np.ascontiguousarray(mask.transpose(3, 0, 1, 2))
    return M


_NC_CACHE = {}


def _get_nc(tap, upto):
    key = (tap, upto)
    if key not in _NC_CACHE:
        _NC_CACHE[key] = build(tap=tap, upto=upto)
    return _NC_CACHE[key]


def kernel(**inputs):
    tap = DEBUG_TAP
    nc = _get_nc(tap, None)
    S = _prep_shared(inputs)
    in_maps = [_prep_core(inputs, c, S) for c in range(8)]
    res = run_bass_kernel_spmd(nc, in_maps, core_ids=list(range(8)))
    if tap:
        return [np.asarray(r["tap_" + tap]) for r in res.results]
    out = np.empty((B, T, D), np.float32)
    for c in range(8):
        b, h = c // 2, c % 2
        o = np.asarray(res.results[c]["out"])
        for i in range(8):
            a = h + 2 * i
            out[b, a * 128:(a + 1) * 128, :] = o[i]
    return out
